# version 05
# baseline (speedup 1.0000x reference)
import contextlib
import numpy as np
import concourse.bass as bass
import concourse.mybir as mybir
from concourse.bass_utils import run_bass_kernel_spmd

F32 = mybir.dt.float32
BF16 = mybir.dt.bfloat16
AF = mybir.ActivationFunctionType
OP = mybir.AluOpType
AX = mybir.AxisListType

T, C, I, NH = 2048, 1024, 2816, 64
TM, CC, IC = T // 128, C // 128, I // 128
NCORES = 8


class Sched:
    ENGS = ("v", "a", "t", "g", "s")

    def __init__(self, nc, stack):
        self.nc = nc
        self.eng = {"v": nc.vector, "a": nc.scalar, "t": nc.tensor,
                    "g": nc.gpsimd, "s": nc.sync}
        self.sem = {}
        self.cnt = {}
        self.wm = {k: {} for k in self.ENGS}
        for k in self.ENGS:
            self.sem[k] = stack.enter_context(nc.semaphore(f"S_{k}"))
            self.cnt[k] = 0
        for k in ("ds", "dg"):
            self.sem[k] = stack.enter_context(nc.semaphore(f"S_{k}"))
            self.cnt[k] = 0
        self.bar_sem = stack.enter_context(nc.semaphore("S_bar"))
        self.bar_n = 0
        self.bufs = {}

    def _wait(self, ek, deps, war=()):
        e = self.eng[ek]
        need = {}
        for d in deps:
            if d is None:
                continue
            sk, val = d
            if val > need.get(sk, 0):
                need[sk] = val
        for d in war:
            if d is None:
                continue
            sk, val = d
            if sk == ek:
                continue
            if val > need.get(sk, 0):
                need[sk] = val
        for sk in sorted(need):
            val = need[sk]
            if self.wm[ek].get(sk, -1) >= val:
                continue
            e.wait_ge(self.sem[sk], val)
            self.wm[ek][sk] = val

    def _slot(self, key):
        return self.bufs.setdefault(key, {"w": None, "r": []})

    def op(self, ek, fn, reads=(), writes=(), deps=()):
        dl = list(deps)
        war = []
        for k in reads:
            dl.append(self._slot(k)["w"])
        for k in writes:
            s = self._slot(k)
            war.extend(s["r"])
            war.append(s["w"])
        self._wait(ek, dl, war)
        inst = fn(self.eng[ek])
        self.cnt[ek] += 1
        inst.then_inc(self.sem[ek], 1)
        tok = (ek, self.cnt[ek])
        for k in reads:
            self._slot(k)["r"].append(tok)
        for k in writes:
            s = self._slot(k)
            s["w"] = tok
            s["r"] = []
        return tok

    def dop(self, qk, out_ap, in_ap, reads=(), writes=(), deps=(), **kw):
        dk = {"s": "ds", "g": "dg"}[qk]
        dl = list(deps)
        for k in reads:
            dl.append(self._slot(k)["w"])
        for k in writes:
            s = self._slot(k)
            dl.extend(s["r"])
            dl.append(s["w"])
        self._wait(qk, dl, ())
        self.eng[qk].dma_start(out_ap, in_ap, **kw).then_inc(self.sem[dk], 16)
        self.cnt[dk] += 16
        tok = (dk, self.cnt[dk])
        for k in reads:
            self._slot(k)["r"].append(tok)
        for k in writes:
            s = self._slot(k)
            s["w"] = tok
            s["r"] = []
        return tok

    def barrier(self):
        for qk, dk in (("s", "ds"), ("g", "dg")):
            if self.cnt[dk] > 0 and self.wm[qk].get(dk, -1) < self.cnt[dk]:
                self.eng[qk].wait_ge(self.sem[dk], self.cnt[dk])
                self.wm[qk][dk] = self.cnt[dk]
        self.bar_n += 1
        for ek in self.ENGS:
            self.eng[ek].nop().then_inc(self.bar_sem, 1)
        for ek in self.ENGS:
            self.eng[ek].wait_ge(self.bar_sem, 5 * self.bar_n)
        self.bufs = {}
        for ek in self.ENGS:
            self.wm[ek] = {}


def build_nc(dump_stage=None):
    nc = bass.Bass("TRN2", target_bir_lowering=False)
    ap = lambda t: t.ap()
    x_in = ap(nc.dram_tensor("x", [T, C], F32, kind="ExternalInput"))
    wi_in = ap(nc.dram_tensor("wi", [C, C], F32, kind="ExternalInput"))
    bi_in = ap(nc.dram_tensor("bi", [1, C], F32, kind="ExternalInput"))
    wr_in = ap(nc.dram_tensor("wr", [C, C], F32, kind="ExternalInput"))
    br_in = ap(nc.dram_tensor("br", [1, C], F32, kind="ExternalInput"))
    lam_in = ap(nc.dram_tensor("lam", [1, 1], F32, kind="ExternalInput"))
    wqkv_in = ap(nc.dram_tensor("wqkv", [192, C], F32, kind="ExternalInput"))
    wout_in = ap(nc.dram_tensor("wout", [C, 64], F32, kind="ExternalInput"))
    wg_in = ap(nc.dram_tensor("wg", [I, C], F32, kind="ExternalInput"))
    wu_in = ap(nc.dram_tensor("wu", [I, C], F32, kind="ExternalInput"))
    wd_in = ap(nc.dram_tensor("wd", [C, I], F32, kind="ExternalInput"))
    out_hbm = ap(nc.dram_tensor("out", [T, C], F32, kind="ExternalOutput"))
    wqt_g = ap(nc.dram_tensor("wqt_g", [CC, 128, I], BF16, kind="Internal"))
    wqt_u = ap(nc.dram_tensor("wqt_u", [CC, 128, I], BF16, kind="Internal"))
    wqt_d = ap(nc.dram_tensor("wqt_d", [IC, 128, C], BF16, kind="Internal"))
    qT_dram = ap(nc.dram_tensor("qT_d", [64, T], BF16, kind="Internal"))

    gstack = contextlib.ExitStack()
    with gstack:
        sb = lambda n, shp, dt: gstack.enter_context(nc.sbuf_tensor(n, shp, dt))
        S = Sched(nc, gstack)

        x_s = sb("x_s", [128, TM, C], F32)
        xnT = sb("xnT", [128, CC, T], BF16)
        ident_b = sb("ident_b", [128, 128], BF16)
        tri_b = sb("tri_b", [128, 128], BF16)
        ones_f = sb("ones_f", [128, 128], F32)
        onesrow = sb("onesrow", [1, 128], F32)
        half_t = sb("half_t", [1, 1], F32)
        cvec = sb("cvec", [128, 2], F32)
        ccol = sb("ccol", [128, 4], F32)
        bih = sb("bih", [128, CC], F32)
        brh = sb("brh", [128, CC], F32)
        lamt = sb("lamt", [1, 8], F32)
        stat = sb("stat", [128, 4 * TM], F32)
        qS = sb("qS_", [128, 2, 512], F32)
        qA = sb("qA_", [128, 2, 512], F32)
        qB = sb("qB_", [128, 2, 512], F32)
        qC = sb("qC_", [128, 2, 512], F32)
        qM = sb("qM_", [128, 2, 32], F32)
        qR = sb("qR_", [128, 2, 32], F32)
        qL = sb("qL_", [128, 2, 32], F32)
        xsq = qS
        ps_tr = gstack.enter_context(nc.psum_tensor("ps_tr", [128, 2, 1024], BF16))

        sx = lambda m: stat[:, m:m + 1]
        ssq = lambda m: stat[:, TM + m:TM + m + 1]
        mu = lambda m: stat[:, 2 * TM + m:2 * TM + m + 1]
        rstd = lambda m: stat[:, 3 * TM + m:3 * TM + m + 1]

        def pe_t_bf(src_ap, dst_ap, slot, reads, writes, ev="a"):
            f = src_ap.shape[-1]
            p = src_ap.shape[0]
            pk = ("ps_tr", slot)
            S.op("t", lambda e: e.transpose(ps_tr[0:f, slot, 0:p], src_ap,
                                            ident_b[0:p, 0:p]),
                 reads=list(reads) + ["identb"], writes=[pk])
            if ev == "a":
                S.op("a", lambda e: e.activation(dst_ap, ps_tr[0:f, slot, 0:p],
                                                 AF.Copy),
                     reads=[pk], writes=list(writes))
            else:
                S.op("v", lambda e: e.tensor_copy(dst_ap, ps_tr[0:f, slot, 0:p]),
                     reads=[pk], writes=list(writes))

        def ln_stats(m):
            S.op("a", lambda e: e.activation(
                    xsq.rearrange("p a b -> p (a b)"), x_s[:, m, :], AF.Square,
                    accum_out=ssq(m)),
                 reads=[("x_s", m)],
                 writes=[("QS", 0, "S"), ("QS", 1, "S"), ("ssq", m)])
            S.op("v", lambda e: e.reduce_sum(sx(m), x_s[:, m, :], AX.X),
                 reads=[("x_s", m)], writes=[("sx", m)])

        def ln_finish():
            sxa = stat[:, 0:TM]
            ssqa = stat[:, TM:2 * TM]
            mua = stat[:, 2 * TM:3 * TM]
            rsa = stat[:, 3 * TM:4 * TM]
            rd = [("sx", m) for m in range(TM)] + [("ssq", m) for m in range(TM)]
            S.op("v", lambda e: e.tensor_scalar(mua, sxa, 1.0 / C, None, OP.mult),
                 reads=rd, writes=["mua"])
            S.op("v", lambda e: e.tensor_mul(rsa, mua, mua),
                 reads=["mua"], writes=["rsa"])
            S.op("v", lambda e: e.scalar_tensor_tensor(rsa, ssqa, 1.0 / C, rsa,
                                                       OP.mult, OP.subtract),
                 reads=rd + ["rsa"], writes=["rsa"])
            S.op("v", lambda e: e.tensor_scalar(rsa, rsa, 1e-5, None, OP.add),
                 reads=["rsa"], writes=["rsa"])
            S.op("a", lambda e: e.activation(rsa, rsa, AF.Sqrt),
                 reads=["rsa"], writes=["rsa"])
            S.op("v", lambda e: e.reciprocal(rsa, rsa),
                 reads=["rsa"], writes=["rsa"])

        def ln_norm(m, out_ap, out_key):
            S.op("v", lambda e: e.tensor_scalar(out_ap, x_s[:, m, :], mu(m),
                                                rstd(m), OP.subtract, OP.mult),
                 reads=[("x_s", m), "mua", "rsa"], writes=[out_key])

        def quant_steps(src_ap, out_ap, W, src_key, out_key, qs):
            B = W // 16
            src3 = src_ap.rearrange("p (b k) -> p b k", k=16)
            M = qM[:, qs, 0:B]
            R = qR[:, qs, 0:B]
            L = qL[:, qs, 0:B]
            Sf = qS[:, qs, 0:W]
            A = qA[:, qs, 0:W]
            Bm = qB[:, qs, 0:W]
            Cm = qC[:, qs, 0:W]
            K = lambda n: ("QS", qs, n)
            R_b = R.unsqueeze(2).broadcast_to([128, B, 16])
            L_b = L.unsqueeze(2).broadcast_to([128, B, 16])
            C3 = Cm.rearrange("p (b k) -> p b k", k=16)
            A3 = A.rearrange("p (b k) -> p b k", k=16)
            S3 = Sf.rearrange("p (b k) -> p b k", k=16)
            st = []
            st.append(("v", lambda e: e.tensor_reduce(M, src3, AX.X, OP.max,
                                                      apply_absolute_value=True),
                       [src_key], [K("M")]))
            st.append(("v", lambda e: e.tensor_scalar(M, M, 6e-12, None, OP.max),
                       [K("M")], [K("M")]))
            st.append(("v", lambda e: e.reciprocal(R, M), [K("M")], [K("R")]))
            st.append(("v", lambda e: e.tensor_scalar(R, R, 24.0, None, OP.mult),
                       [K("R")], [K("R")]))
            st.append(("v", lambda e: e.tensor_scalar(L, M, 1.0 / 12.0, None,
                                                      OP.mult),
                       [K("M")], [K("L")]))
            st.append(("v", lambda e: e.tensor_tensor(C3, src3, R_b, OP.mult),
                       [src_key, K("R")], [K("C")]))
            st.append(("a", lambda e: e.activation(Sf, Cm, AF.Square),
                       [K("C")], [K("S")]))
            st.append(("v", lambda e: e.tensor_scalar(A, Sf, 1.0, None, OP.is_gt),
                       [K("S")], [K("A")]))
            for thr in (9.0, 25.0, 49.0):
                st.append(("v", lambda e, th=thr: e.scalar_tensor_tensor(
                        A, Sf, th, A, OP.is_gt, OP.add), [K("S"), K("A")],
                        [K("A")]))
            st.append(("v", lambda e: e.tensor_scalar(Bm, Sf, 100.0, None,
                                                      OP.is_gt),
                       [K("S")], [K("B")]))
            st.append(("v", lambda e: e.scalar_tensor_tensor(Bm, Sf, 196.0, Bm,
                                                             OP.is_gt, OP.add),
                       [K("S"), K("B")], [K("B")]))
            st.append(("v", lambda e: e.tensor_scalar(Cm, Sf, 400.0, None,
                                                      OP.is_gt),
                       [K("S"), K("C")], [K("C")]))
            st.append(("v", lambda e: e.scalar_tensor_tensor(Bm, Cm, 2.0, Bm,
                                                             OP.mult, OP.add),
                       [K("B"), K("C")], [K("B")]))
            st.append(("v", lambda e: e.scalar_tensor_tensor(A, Bm, 2.0, A,
                                                             OP.mult, OP.add),
                       [K("A"), K("B")], [K("A")]))
            st.append(("v", lambda e: e.tensor_scalar(Cm, src_ap, 0.0, 2.0,
                                                      OP.is_ge, OP.mult),
                       [src_key, K("C")], [K("C")]))
            st.append(("v", lambda e: e.tensor_tensor(S3, A3, L_b, OP.mult),
                       [K("A"), K("L")], [K("S")]))
            st.append(("v", lambda e: e.scalar_tensor_tensor(out_ap, Cm, -1.0,
                                                             Sf, OP.add, OP.mult),
                       [K("C"), K("S")], [out_key]))
            return st

        def run_quant(jobs):
            steps = [quant_steps(*j) for j in jobs]
            n = max(len(s) for s in steps)
            for i in range(n):
                for s in steps:
                    if i < len(s):
                        ek, fn, rd, wr = s[i]
                        S.op(ek, fn, reads=rd, writes=wr)

        st1 = contextlib.ExitStack()
        with st1:
            sb1 = lambda n, shp, dt: st1.enter_context(nc.sbuf_tensor(n, shp, dt))
            ps1 = lambda n, shp, dt: st1.enter_context(nc.psum_tensor(n, shp, dt))
            wiT = sb1("wiT", [128, CC, C], BF16)
            wrT = sb1("wrT", [128, CC, C], BF16)
            ident_f = sb1("ident_f", [128, 128], F32)
            tb_b = sb1("tb_b", [128, 4, 128], BF16)
            ps_trf = ps1("ps_trf", [128, 2, 512], F32)
            ps_a = ps1("ps_a", [128, 4, 512], F32)
            ps_c = ps_a[:, 0, :]
            PSA0 = ("ps_a", 0)

            st1t = contextlib.ExitStack()
            with st1t:
                sb1t = lambda n, shp, dt: st1t.enter_context(
                    nc.sbuf_tensor(n, shp, dt))
                w_raw = sb1t("w_raw", [128, CC // 2, C], F32)
                wbf = sb1t("wbf", [128, CC, C], BF16)
                birow = sb1t("birow", [1, C], F32)
                brrow = sb1t("brrow", [1, C], F32)
                trif = sb1t("trif", [128, 128], F32)
                xn_t = sb1t("xn_t", [128, 2, C], BF16)

                xr = x_in.rearrange("(m p) c -> p m c", p=128)
                for j in range(4):
                    S.dop("g", x_s[:, 4 * j:4 * (j + 1), :],
                          xr[:, 4 * j:4 * (j + 1), :],
                          writes=[("x_s", m) for m in range(4 * j, 4 * j + 4)])
                S.dop("s", birow[:, :], bi_in, writes=["birow"])
                S.dop("s", brrow[:, :], br_in, writes=["brrow"])
                S.dop("s", lamt[:, 0:1], lam_in, writes=["lam0"])
                S.op("g", lambda e: e.memset(ones_f[:, :], 1.0), writes=["ones_f"])
                S.op("g", lambda e: e.memset(onesrow[:, :], 1.0), writes=["onesrow"])
                S.op("g", lambda e: e.memset(half_t[:, :], 0.5), writes=["half"])
                S.op("g", lambda e: e.memset(ccol[:, 0:1], 0.5), writes=["ccol"])
                S.op("g", lambda e: e.memset(ccol[:, 1:2], -1.0), writes=["ccol"])
                S.op("g", lambda e: e.memset(ccol[:, 2:3], 1.0), writes=["ccol"])
                S.op("g", lambda e: e.memset(ccol[:, 3:4], 0.125), writes=["ccol"])
                S.op("g", lambda e: e.affine_select(ident_f[:, :], ones_f[:, :],
                                                    [[1, 128]], OP.is_equal, 0.0,
                                                    base=0, channel_multiplier=-1),
                     reads=["ones_f"], writes=["identf"])
                S.op("v", lambda e: e.tensor_copy(ident_b[:, :], ident_f[:, :]),
                     reads=["identf"], writes=["identb"])
                S.op("g", lambda e: e.affine_select(trif[:, :], ones_f[:, :],
                                                    [[1, 128]], OP.is_ge, 0.0,
                                                    base=0, channel_multiplier=-1),
                     reads=["ones_f"], writes=["trif"])
                S.op("v", lambda e: e.tensor_copy(tri_b[:, :], trif[:, :]),
                     reads=["trif"], writes=["tri"])
                S.op("a", lambda e: e.activation(lamt[:, 1:2], lamt[:, 0:1], AF.Exp,
                                                 scale=ccol[0:1, 1:2]),
                     reads=["lam0", "ccol"], writes=["lam1"])
                S.op("a", lambda e: e.activation(lamt[:, 2:3], lamt[:, 1:2], AF.Ln,
                                                 bias=ccol[0:1, 2:3]),
                     reads=["lam1", "ccol"], writes=["lam2"])
                S.op("v", lambda e: e.tensor_add(lamt[:, 3:4], lamt[:, 0:1],
                                                 lamt[:, 2:3]),
                     reads=["lam0", "lam2"], writes=["lam3"])
                S.op("v", lambda e: e.tensor_scalar(lamt[:, 4:5], lamt[:, 3:4],
                                                    -4.0, None, OP.mult),
                     reads=["lam3"], writes=["lam4"])
                S.op("v", lambda e: e.tensor_scalar(lamt[:, 5:6], lamt[:, 3:4],
                                                    -8.0, None, OP.mult),
                     reads=["lam3"], writes=["lam5"])
                S.op("t", lambda e: e.matmul(ps_c[:, 0:2], onesrow[:, :],
                                             lamt[:, 4:6], start=True, stop=True),
                     reads=["onesrow", "lam4", "lam5"], writes=[PSA0])
                S.op("v", lambda e: e.tensor_copy(cvec[:, :], ps_c[:, 0:2]),
                     reads=[PSA0], writes=["cvec"])
                for k in range(CC):
                    S.op("t", lambda e, k=k: e.matmul(
                            ps_c[:, 8 + k:9 + k],
                            birow[0:1, 128 * k:128 * (k + 1)], half_t[:, :],
                            start=True, stop=True),
                         reads=["birow", "half"], writes=[PSA0])
                S.op("v", lambda e: e.tensor_copy(bih[:, :], ps_c[:, 8:8 + CC]),
                     reads=[PSA0], writes=["bih"])
                for k in range(CC):
                    S.op("t", lambda e, k=k: e.matmul(
                            ps_c[:, 8 + k:9 + k],
                            brrow[0:1, 128 * k:128 * (k + 1)], half_t[:, :],
                            start=True, stop=True),
                         reads=["brrow", "half"], writes=[PSA0])
                S.op("v", lambda e: e.tensor_copy(brh[:, :], ps_c[:, 8:8 + CC]),
                     reads=[PSA0], writes=["brh"])

                for m in range(TM):
                    ln_stats(m)
                ln_finish()
                for m in range(TM):
                    sl = m % 2
                    ln_norm(m, xn_t[:, sl, :], ("xn_t", sl))
                    if dump_stage == 10:
                        S.op("v", lambda e, sl=sl: e.tensor_copy(
                                xsq[:, :], xn_t[:, sl, :]),
                             reads=[("xn_t", sl)], writes=["xsq"])
                        S.dop("s", out_hbm.rearrange("(m p) c -> p m c", p=128)
                              [:, m, :], xsq[:, :], reads=["xsq"])
                    for k in range(CC):
                        pe_t_bf(xn_t[:, sl, 128 * k:128 * (k + 1)],
                                xnT[:, k, 128 * m:128 * (m + 1)], k % 2,
                                reads=[("xn_t", sl)], writes=[("xnT", k, m)])
                for (w_src, dstT, dk) in ((wi_in, wiT, "wiT"),
                                          (wr_in, wrT, "wrT")):
                    wsr = w_src.rearrange("(d p) c -> p d c", p=128)
                    for j in range(2):
                        S.dop("s", w_raw[:, :, :], wsr[:, 4 * j:4 * (j + 1), :],
                              writes=["w_raw"])
                        S.op("g", lambda e, j=j: e.tensor_copy(
                                wbf[:, 4 * j:4 * (j + 1), :], w_raw[:, :, :]),
                             reads=["w_raw"], writes=[("wbf", j)])
                    for dcj in range(CC):
                        for k in range(CC):
                            pe_t_bf(wbf[:, dcj, 128 * k:128 * (k + 1)],
                                    dstT[:, k, 128 * dcj:128 * (dcj + 1)], k % 2,
                                    reads=[("wbf", dcj // 4)],
                                    writes=[(dk, k, dcj)])
                S.barrier()

            st1b = contextlib.ExitStack()
            with st1b:
                at_bufs = st1b.enter_context(
                    nc.sbuf_tensor("atb", [128, 5, T], F32))
                wraw = st1b.enter_context(nc.sbuf_tensor("wraw", [128, 2, C], F32))
                wqb = st1b.enter_context(nc.sbuf_tensor("wqb", [128, 2, C], BF16))
                AB = lambda i: at_bufs[:, i, :]

                def wq_chunk_gate(wsrc, dst, i):
                    sl = i % 2
                    S.dop("s", wraw[:, sl, :], wsrc[128 * i:128 * (i + 1), :],
                          writes=[("wraw", sl)])
                    run_quant([
                        (wraw[:, sl, 0:512], wqb[:, sl, 0:512], 512,
                         ("wraw", sl), ("wqb", sl, 0), 0),
                        (wraw[:, sl, 512:1024], wqb[:, sl, 512:1024], 512,
                         ("wraw", sl), ("wqb", sl, 1), 1)])
                    for k in range(CC):
                        tslot = k % 2
                        dst_ap = tb_b[:, 2 + tslot, :]
                        pe_t_bf(wqb[:, sl, 128 * k:128 * (k + 1)], dst_ap,
                                tslot, reads=[("wqb", sl, k // 4)],
                                writes=[("wtt", tslot)])
                        S.dop("s", dst[k, :, 128 * i:128 * (i + 1)], dst_ap,
                              reads=[("wtt", tslot)])

                def wq_chunk_down(j, p0, pw):
                    sl = (p0 // 1024) % 2
                    S.dop("s", wraw[:, sl, 0:pw],
                          wd_in[128 * j:128 * (j + 1), p0:p0 + pw],
                          writes=[("wraw", sl)])
                    w0 = 512
                    w1 = pw - 512
                    run_quant([
                        (wraw[:, sl, 0:w0], wqb[:, sl, 0:w0], w0,
                         ("wraw", sl), ("wqb", sl, 0), 0),
                        (wraw[:, sl, w0:pw], wqb[:, sl, w0:pw], w1,
                         ("wraw", sl), ("wqb", sl, 1), 1)])
                    for kk in range(pw // 128):
                        k = p0 // 128 + kk
                        tslot = k % 2
                        dst_ap = tb_b[:, 2 + tslot, :]
                        pe_t_bf(wqb[:, sl, 128 * kk:128 * (kk + 1)], dst_ap,
                                tslot,
                                reads=[("wqb", sl, 0 if 128 * kk < w0 else 1)],
                                writes=[("wtt", tslot)])
                        S.dop("s", wqt_d[k, :, 128 * j:128 * (j + 1)], dst_ap,
                              reads=[("wtt", tslot)])

                wq_jobs = []
                for (wsrc, dst) in ((wg_in, wqt_g), (wu_in, wqt_u)):
                    for i in range(IC):
                        wq_jobs.append(lambda wsrc=wsrc, dst=dst, i=i:
                                       wq_chunk_gate(wsrc, dst, i))
                pieces = [(0, 1024), (1024, 1024), (2048, 768)]
                for j in range(CC):
                    for (p0, pw) in pieces:
                        wq_jobs.append(lambda j=j, p0=p0, pw=pw:
                                       wq_chunk_down(j, p0, pw))
                njobs = len(wq_jobs)
                ji = [0]

                def drain_wq(n):
                    while ji[0] < njobs and n > 0:
                        wq_jobs[ji[0]]()
                        ji[0] += 1
                        n -= 1

                for d in range(CC):
                    for (obuf, wT, wkeys, bcol, bkey) in (
                            (0, wiT, "wiT", bih, "bih"),
                            (1, wrT, "wrT", brh, "brh")):
                        for t4 in range(4):
                            slot = 2 * (t4 % 2) + obuf
                            pk = ("ps_a", slot)
                            for k in range(CC):
                                S.op("t", lambda e, k=k, t4=t4, wT=wT, d=d,
                                     slot=slot: e.matmul(
                                        ps_a[:, slot, :],
                                        wT[:, k, 128 * d:128 * (d + 1)],
                                        xnT[:, k, 512 * t4:512 * (t4 + 1)],
                                        start=(k == 0), stop=(k == CC - 1)),
                                     writes=[pk])
                            S.op("a", lambda e, t4=t4, d=d, bcol=bcol, obuf=obuf,
                                 slot=slot: e.activation(
                                    AB(obuf)[:, 512 * t4:512 * (t4 + 1)],
                                    ps_a[:, slot, :], AF.Tanh,
                                    scale=ccol[:, 0:1], bias=bcol[:, d:d + 1]),
                                 reads=[pk, bkey, "ccol"], writes=[("ab", obuf)])
                    S.op("a", lambda e: e.activation(AB(2)[:, :], AB(1)[:, :],
                                                     AF.Exp, scale=cvec[:, 0:1],
                                                     bias=cvec[:, 0:1]),
                         reads=[("ab", 1), "cvec"], writes=[("ab", 2)])
                    S.op("a", lambda e: e.activation(AB(3)[:, :], AB(1)[:, :],
                                                     AF.Exp, scale=cvec[:, 1:2],
                                                     bias=cvec[:, 1:2]),
                         reads=[("ab", 1), "cvec"], writes=[("ab", 3)])
                    S.op("a", lambda e: e.activation(AB(1)[:, :], AB(3)[:, :],
                                                     AF.Sqrt, scale=ccol[:, 1:2],
                                                     bias=ccol[:, 2:3]),
                         reads=[("ab", 3), "ccol"], writes=[("ab", 1)])
                    S.op("v", lambda e, d=d: e.scalar_tensor_tensor(
                            AB(4)[:, :], AB(0)[:, :], 1.0, xnT[:, d, :], OP.add,
                            OP.mult),
                         reads=[("ab", 0)], writes=[("ab", 4)])
                    S.op("v", lambda e: e.scalar_tensor_tensor(
                            AB(4)[:, :], AB(4)[:, :], 0.5, AB(1)[:, :], OP.mult,
                            OP.mult),
                         reads=[("ab", 4), ("ab", 1)], writes=[("ab", 4)])
                    S.op("v", lambda e: e.tensor_tensor_scan(
                            AB(3)[:, :], AB(2)[:, :], AB(4)[:, :], 0.0, OP.mult,
                            OP.add),
                         reads=[("ab", 2), ("ab", 4)], writes=[("ab", 3)])
                    if dump_stage in (11, 12, 13, 14, 15):
                        buf = {11: 0, 12: 1, 13: 2, 14: 4, 15: 3}[dump_stage]
                        dv = out_hbm.rearrange("(d p) c -> d p c", p=128)
                        S.dop("s", dv[2 * d, :, :], AB(buf)[:, 0:1024],
                              reads=[("ab", buf)])
                        S.dop("s", dv[2 * d + 1, :, :], AB(buf)[:, 1024:2048],
                              reads=[("ab", buf)])
                    for m in range(TM):
                        slot = m % 2
                        pk = ("ps_trf", slot)
                        S.op("t", lambda e, m=m, slot=slot: e.transpose(
                                ps_trf[:, slot, 0:128],
                                AB(3)[:, 128 * m:128 * (m + 1)], ident_f[:, :]),
                             reads=[("ab", 3), "identf"], writes=[pk])
                        S.op("v", lambda e, m=m, d=d, slot=slot: e.tensor_add(
                                x_s[:, m, 128 * d:128 * (d + 1)],
                                x_s[:, m, 128 * d:128 * (d + 1)],
                                ps_trf[:, slot, 0:128]),
                             reads=[pk, ("x_s", m)], writes=[("x_s", m)])
                    drain_wq(9)
                drain_wq(njobs)
        S.barrier()

        def ffn_phase(tag):
            st2 = contextlib.ExitStack()
            with st2:
                sb2 = lambda n, shp, dt: st2.enter_context(
                    nc.sbuf_tensor(n + tag, shp, dt))
                ps2 = lambda n, shp, dt: st2.enter_context(
                    nc.psum_tensor(n + tag, shp, dt))
                xn2 = sb2("xn2", [128, 2, C], F32)
                xq_t = sb2("xq_t", [128, 2, C], BF16)
                wqdT = sb2("wqdT", [128, IC, C], BF16)
                wcol = sb2("wcol", [128, 2, CC, 512], BF16)
                sg = sb2("sg", [128, 2, 512], F32)
                hsrc = sb2("hsrc", [128, 2, 512], F32)
                hqq = sb2("hqq", [128, 2, 512], BF16)
                hqT = sb2("hqT", [128, IC, 256], BF16)
                ps_g = ps2("ps_g", [128, 2, 512], F32)
                ps_d = ps2("ps_d", [128, 2, 512], F32)
                wqdr = wqt_d.rearrange("k p c -> p k c")
                for j in range(4):
                    k0, k1 = 6 * j, min(6 * (j + 1), IC)
                    S.dop("s", wqdT[:, k0:k1, :], wqdr[:, k0:k1, :],
                          writes=["wqdT"])
                for m in range(TM):
                    ln_stats(m)
                ln_finish()
                for m in range(TM):
                    sl = m % 2
                    ln_norm(m, xn2[:, sl, :], ("xn2", sl))
                    run_quant([
                        (xn2[:, sl, 0:512], xq_t[:, sl, 0:512], 512,
                         ("xn2", sl), ("xq", sl, 0), 0),
                        (xn2[:, sl, 512:1024], xq_t[:, sl, 512:1024], 512,
                         ("xn2", sl), ("xq", sl, 1), 1)])
                    for k in range(CC):
                        pe_t_bf(xq_t[:, sl, 128 * k:128 * (k + 1)],
                                xnT[:, k, 128 * m:128 * (m + 1)], k % 2,
                                reads=[("xq", sl, k // 4)],
                                writes=[("xqT", k, m)])
                icols = [(512 * n, min(512, I - 512 * n))
                         for n in range((I + 511) // 512)]
                wgr = wqt_g.rearrange("k p i -> p k i")
                wur = wqt_u.rearrange("k p i -> p k i")
                def down_mg(mg):
                    for mm in range(2):
                        m = 2 * mg + mm
                        for n2 in range(2):
                            pk = ("ps_d", n2)
                            hts = [("hqT", k2, mm) for k2 in range(IC)]
                            for k in range(IC):
                                S.op("t", lambda e, k=k, mm=mm, n2=n2:
                                     e.matmul(ps_d[:, n2, :],
                                              hqT[:, k, 128 * mm:128 * (mm + 1)],
                                              wqdT[:, k, 512 * n2:512 * (n2 + 1)],
                                              start=(k == 0), stop=(k == IC - 1)),
                                     reads=hts + ["wqdT"], writes=[pk])
                            S.op("v", lambda e, m=m, n2=n2: e.tensor_add(
                                    x_s[:, m, 512 * n2:512 * (n2 + 1)],
                                    x_s[:, m, 512 * n2:512 * (n2 + 1)],
                                    ps_d[:, n2, :]),
                                 reads=[pk, ("x_s", m)], writes=[("x_s", m)])

                sched = []
                for mg in range(8):
                    for (c0, cw) in icols:
                        sched.append(("g", mg, c0, cw))
                        sched.append(("u", mg, c0, cw))

                def load_col(idx, slot):
                    mat, _, c0, cw = sched[idx]
                    srcr = wgr if mat == "g" else wur
                    S.dop("s", wcol[:, slot, :, 0:cw], srcr[:, :, c0:c0 + cw],
                          writes=[("wcol", slot)])

                load_col(0, 0)
                for idx, (mat, mg, c0, cw) in enumerate(sched):
                    slot = idx % 2
                    if idx + 1 < len(sched):
                        load_col(idx + 1, 1 - slot)
                    if mat == "g":
                        for mm in range(2):
                            m = 2 * mg + mm
                            for k in range(CC):
                                S.op("t", lambda e, k=k, m=m, mm=mm, cw=cw,
                                     ws=slot: e.matmul(
                                        ps_g[:, mm, 0:cw],
                                        xnT[:, k, 128 * m:128 * (m + 1)],
                                        wcol[:, ws, k, 0:cw],
                                        start=(k == 0), stop=(k == CC - 1)),
                                     reads=[("wcol", slot), ("xqT", k, m)],
                                     writes=[("ps_g", mm)])
                            S.op("a", lambda e, mm=mm, cw=cw: e.activation(
                                    sg[:, mm, 0:cw], ps_g[:, mm, 0:cw], AF.Silu),
                                 reads=[("ps_g", mm)], writes=[("sg", mm)])
                    else:
                        for mm in range(2):
                            m = 2 * mg + mm
                            for k in range(CC):
                                S.op("t", lambda e, k=k, m=m, mm=mm, cw=cw,
                                     ws=slot: e.matmul(
                                        ps_g[:, mm, 0:cw],
                                        xnT[:, k, 128 * m:128 * (m + 1)],
                                        wcol[:, ws, k, 0:cw],
                                        start=(k == 0), stop=(k == CC - 1)),
                                     reads=[("wcol", slot), ("xqT", k, m),
                                            ("sg", mm)],
                                     writes=[("ps_g", mm)])
                            sl = mm % 2
                            S.op("v", lambda e, mm=mm, cw=cw, sl=sl:
                                 e.tensor_mul(hsrc[:, sl, 0:cw], sg[:, mm, 0:cw],
                                              ps_g[:, mm, 0:cw]),
                                 reads=[("sg", mm), ("ps_g", mm)],
                                 writes=[("hs", sl)])
                            if sl == 1:
                                run_quant([
                                    (hsrc[:, 0, 0:cw], hqq[:, 0, 0:cw], cw,
                                     ("hs", 0), ("hqq", 0), 0),
                                    (hsrc[:, 1, 0:cw], hqq[:, 1, 0:cw], cw,
                                     ("hs", 1), ("hqq", 1), 1)])
                                for q2 in range(2):
                                    mmq = mm - 1 + q2
                                    for kk in range(cw // 128):
                                        ic = (c0 // 128) + kk
                                        pe_t_bf(hqq[:, q2,
                                                    128 * kk:128 * (kk + 1)],
                                                hqT[:, ic,
                                                    128 * mmq:128 * (mmq + 1)],
                                                kk % 2, reads=[("hqq", q2)],
                                                writes=[("hqT", ic, mmq)])
                    if mat == "u" and (c0, cw) == icols[-1]:
                        down_mg(mg)
            S.barrier()

        ffn_phase("F1")
        if dump_stage == 2:
            for j in range(4):
                S.dop("s", out_hbm.rearrange("(m p) c -> p m c", p=128)
                      [:, 4 * j:4 * (j + 1), :], x_s[:, 4 * j:4 * (j + 1), :])

        st3 = contextlib.ExitStack()
        with st3:
            sb3 = lambda n, shp, dt: st3.enter_context(nc.sbuf_tensor(n, shp, dt))
            ps3 = lambda n, shp, dt: st3.enter_context(nc.psum_tensor(n, shp, dt))
            xn3 = sb3("xn3", [128, 2, C], BF16)
            wqkv_raw = sb3("wqkv_raw", [128, 2, C], F32)
            qkv_bf = sb3("qkv_bf", [128, 2, C], BF16)
            wqkvT = sb3("wqkvT", [128, CC, 192], BF16)
            wout_raw = sb3("wout_raw", [128, CC, 64], F32)
            wout_bf = sb3("wout_bf", [128, CC, 64], BF16)
            woutT = sb3("woutT", [64, C], BF16)
            qkv_sb = sb3("qkv_sb", [128, TM, 192], F32)
            q_bf = sb3("q_bf", [128, 64], BF16)
            qT_sb = sb3("qT_sb", [64, T], BF16)
            vo16 = sb3("vo16", [128, TM, 128], BF16)
            qb = sb3("qb", [128, 3, T], BF16)
            Pb = sb3("Pb", [128, 4, T], BF16)
            nd = sb3("nd", [128, T], F32)
            rec = sb3("rec", [64, T], F32)
            ohm = sb3("ohm", [64, T], BF16)
            ps_n = ps3("ps_n", [128, T], F32)

            for m in range(TM):
                ln_stats(m)
            ln_finish()
            for m in range(TM):
                sl = m % 2
                ln_norm(m, xn3[:, sl, :], ("xn3", sl))
                for k in range(CC):
                    pe_t_bf(xn3[:, sl, 128 * k:128 * (k + 1)],
                            xnT[:, k, 128 * m:128 * (m + 1)], k % 2,
                            reads=[("xn3", sl)], writes=[("xnT", k, m)])
            S.dop("s", wqkv_raw[:, 0, :], wqkv_in[0:128, :], writes=["qkvr"])
            S.dop("s", wqkv_raw[0:64, 1, :], wqkv_in[128:192, :], writes=["qkvr"])
            S.op("g", lambda e: e.tensor_copy(qkv_bf[:, :, :], wqkv_raw[:, :, :]),
                 reads=["qkvr"], writes=["qkv_bf"])
            for k in range(CC):
                pe_t_bf(qkv_bf[:, 0, 128 * k:128 * (k + 1)], wqkvT[:, k, 0:128],
                        k % 2, reads=["qkv_bf"], writes=[("wqkvT", k)])
                pe_t_bf(qkv_bf[0:64, 1, 128 * k:128 * (k + 1)],
                        wqkvT[:, k, 128:192], k % 2, reads=["qkv_bf"],
                        writes=[("wqkvT", k)])
            S.dop("s", wout_raw[:, :, :],
                  wout_in.rearrange("(cc p) h -> p cc h", p=128),
                  writes=["wout_raw"])
            S.op("g", lambda e: e.tensor_copy(wout_bf[:, :, :], wout_raw[:, :, :]),
                 reads=["wout_raw"], writes=["wout_bf"])
            for k in range(CC):
                pe_t_bf(wout_bf[:, k, 0:64], woutT[0:64, 128 * k:128 * (k + 1)],
                        k % 2, reads=["wout_bf"], writes=["woutT"])
            for m in range(TM):
                pkk = "ps_n"
                for k in range(CC):
                    S.op("t", lambda e, k=k, m=m: e.matmul(
                            ps_n[:, 0:192], xnT[:, k, 128 * m:128 * (m + 1)],
                            wqkvT[:, k, :], start=(k == 0), stop=(k == CC - 1)),
                         reads=[("xnT", k, m)] + [("wqkvT", kk) for kk in
                                                  range(CC)],
                         writes=[pkk])
                S.op("a", lambda e, m=m: e.activation(qkv_sb[:, m, 0:64],
                                                      ps_n[:, 0:64], AF.Copy),
                     reads=[pkk], writes=[("qkv", m)])
                S.op("a", lambda e, m=m: e.activation(qkv_sb[:, m, 64:128],
                                                      ps_n[:, 64:128], AF.Copy,
                                                      scale=ccol[:, 3:4]),
                     reads=[pkk, "ccol"], writes=[("qkv", m)])
                S.op("a", lambda e, m=m: e.activation(qkv_sb[:, m, 128:192],
                                                      ps_n[:, 128:192], AF.Copy),
                     reads=[pkk], writes=[("qkv", m)])
                S.op("v", lambda e, m=m: e.tensor_copy(q_bf[:, :],
                                                       qkv_sb[:, m, 0:64]),
                     reads=[("qkv", m)], writes=["q_bf"])
                pe_t_bf(q_bf[:, :], qT_sb[0:64, 128 * m:128 * (m + 1)], m % 2,
                        reads=["q_bf"], writes=["qT"])
            S.dop("s", qT_dram, qT_sb[0:64, :], reads=["qT"], writes=["qTd"])
            S.op("g", lambda e: e.memset(vo16[:, :, :], 0.0), writes=["vo"])
            for h in range(NH):
                if h > 0:
                    S.op("g", lambda e, h=h: e.memset(vo16[:, :, h - 1:h], 0.0),
                         writes=["vo"])
                    S.op("g", lambda e, h=h: e.memset(
                            vo16[:, :, 63 + h:64 + h], 0.0), writes=["vo"])
                S.op("v", lambda e, h=h: e.tensor_copy(
                        vo16[:, :, h:h + 1], qkv_sb[:, :, 128 + h:129 + h]),
                     reads=[("qkv", m2) for m2 in range(TM)], writes=["vo"])
                S.op("g", lambda e, h=h: e.memset(vo16[:, :, 64 + h:65 + h], 1.0),
                     writes=["vo"])
                qsl = h % 3
                S.dop("g", qb[:, qsl, :],
                      qT_dram[h:h + 1, :].partition_broadcast(128),
                      reads=["qTd"], writes=[("qb", qsl)])
                for b in range(TM):
                    ext = T - 128 * b
                    psl = (h * TM + b) % 4
                    S.op("a", lambda e, h=h, b=b, psl=psl, qsl=qsl, ext=ext:
                         e.activation(Pb[:, psl, 0:ext], qb[:, qsl, 128 * b:T],
                                      AF.Exp,
                                      scale=qkv_sb[:, b, 64 + h:65 + h]),
                         reads=[("qb", qsl), ("qkv", b)], writes=[("Pb", psl)])
                    S.op("v", lambda e, psl=psl: e.tensor_mul(
                            Pb[:, psl, 0:128], Pb[:, psl, 0:128], tri_b[:, :]),
                         reads=[("Pb", psl), "tri"], writes=[("Pb", psl)])
                    i0 = 128 * b
                    while i0 < T:
                        i1 = min((i0 // 512 + 1) * 512, T)
                        S.op("t", lambda e, b=b, psl=psl, i0=i0, i1=i1, h=h:
                             e.matmul(ps_n[:, i0:i1], vo16[:, b, :],
                                      Pb[:, psl, i0 - 128 * b:i1 - 128 * b],
                                      start=(h == 0 and b == 0),
                                      stop=(h == NH - 1)),
                             reads=["vo", ("Pb", psl)], writes=["ps_n"])
                        i0 = i1
            S.op("v", lambda e: e.tensor_copy(nd[:, :], ps_n[:, :]),
                 reads=["ps_n"], writes=["nd"])
            S.op("v", lambda e: e.reciprocal(rec[0:64, :], nd[64:128, :]),
                 reads=["nd"], writes=["rec"])
            S.op("v", lambda e: e.tensor_mul(ohm[0:64, :], nd[0:64, :],
                                             rec[0:64, :]),
                 reads=["nd", "rec"], writes=["ohm"])
            for m in range(TM):
                for n2 in range(2):
                    S.op("t", lambda e, m=m, n2=n2: e.matmul(
                            ps_n[:, 512 * n2:512 * (n2 + 1)],
                            ohm[0:64, 128 * m:128 * (m + 1)],
                            woutT[0:64, 512 * n2:512 * (n2 + 1)],
                            start=True, stop=True),
                         reads=["ohm", "woutT"], writes=["ps_n"])
                    S.op("v", lambda e, m=m, n2=n2: e.tensor_add(
                            x_s[:, m, 512 * n2:512 * (n2 + 1)],
                            x_s[:, m, 512 * n2:512 * (n2 + 1)],
                            ps_n[:, 512 * n2:512 * (n2 + 1)]),
                         reads=["ps_n", ("x_s", m)], writes=[("x_s", m)])
        S.barrier()
        if dump_stage == 3:
            for j in range(4):
                S.dop("s", out_hbm.rearrange("(m p) c -> p m c", p=128)
                      [:, 4 * j:4 * (j + 1), :], x_s[:, 4 * j:4 * (j + 1), :])

        ffn_phase("F2")

        if dump_stage is None:
            outr = out_hbm.rearrange("(m p) c -> p m c", p=128)
            for j in range(4):
                S.dop("s", outr[:, 4 * j:4 * (j + 1), :],
                      x_s[:, 4 * j:4 * (j + 1), :],
                      reads=[("x_s", m) for m in range(4 * j, 4 * j + 4)])
        S.eng["s"].wait_ge(S.sem["ds"], S.cnt["ds"])
        S.eng["g"].wait_ge(S.sem["dg"], S.cnt["dg"])
    return nc


_NC_CACHE = {}


def kernel(L0_attn_a_x, L0_attn_a_Wi, L0_attn_a_bi, L0_attn_a_Wr, L0_attn_a_br,
           L0_attn_a_cap_lambda, L0_attn_b_w_qkv, L0_attn_b_w_out,
           L0_ffn_w_gate, L0_ffn_w_up, L0_ffn_w_down):
    cc = np.ascontiguousarray
    x = cc(np.asarray(L0_attn_a_x, dtype=np.float32))
    wi = cc(np.asarray(L0_attn_a_Wi, dtype=np.float32))
    bi = cc(np.asarray(L0_attn_a_bi, dtype=np.float32)).reshape(1, C)
    wr = cc(np.asarray(L0_attn_a_Wr, dtype=np.float32))
    br = cc(np.asarray(L0_attn_a_br, dtype=np.float32)).reshape(1, C)
    lam = cc(np.asarray(L0_attn_a_cap_lambda, dtype=np.float32)).reshape(1, 1)
    wqkv = cc(np.asarray(L0_attn_b_w_qkv, dtype=np.float32))
    wout = cc(np.asarray(L0_attn_b_w_out, dtype=np.float32))
    wg = cc(np.asarray(L0_ffn_w_gate, dtype=np.float32))
    wu = cc(np.asarray(L0_ffn_w_up, dtype=np.float32))
    wd = cc(np.asarray(L0_ffn_w_down, dtype=np.float32))

    if "nc" not in _NC_CACHE:
        _NC_CACHE["nc"] = build_nc()
    nc = _NC_CACHE["nc"]
    in_maps = []
    for b in range(NCORES):
        in_maps.append({"x": cc(x[b]), "wi": wi, "bi": bi, "wr": wr, "br": br,
                        "lam": lam, "wqkv": wqkv, "wout": wout, "wg": wg,
                        "wu": wu, "wd": wd})
    res = run_bass_kernel_spmd(nc, in_maps, core_ids=list(range(NCORES)))
    return np.stack([r["out"] for r in res.results], axis=0)


# version 06
# speedup vs baseline: 1.2739x; 1.2739x over previous
import contextlib
import numpy as np
import concourse.bass as bass
import concourse.mybir as mybir
from concourse.bass_utils import run_bass_kernel_spmd

F32 = mybir.dt.float32
BF16 = mybir.dt.bfloat16
AF = mybir.ActivationFunctionType
OP = mybir.AluOpType
AX = mybir.AxisListType

T, C, I, NH = 2048, 1024, 2816, 64
TM, CC, IC = T // 128, C // 128, I // 128
NCORES = 8


class Sched:
    ENGS = ("v", "a", "t", "g", "s")

    def __init__(self, nc, stack):
        self.nc = nc
        self.eng = {"v": nc.vector, "a": nc.scalar, "t": nc.tensor,
                    "g": nc.gpsimd, "s": nc.sync}
        self.sem = {}
        self.cnt = {}
        self.wm = {k: {} for k in self.ENGS}
        for k in self.ENGS:
            self.sem[k] = stack.enter_context(nc.semaphore(f"S_{k}"))
            self.cnt[k] = 0
        for k in ("ds", "dg"):
            self.sem[k] = stack.enter_context(nc.semaphore(f"S_{k}"))
            self.cnt[k] = 0
        self.bar_sem = stack.enter_context(nc.semaphore("S_bar"))
        self.bar_n = 0
        self.bufs = {}

    def _wait(self, ek, deps, war=()):
        e = self.eng[ek]
        need = {}
        for d in deps:
            if d is None:
                continue
            sk, val = d
            if val > need.get(sk, 0):
                need[sk] = val
        for d in war:
            if d is None:
                continue
            sk, val = d
            if sk == ek:
                continue
            if val > need.get(sk, 0):
                need[sk] = val
        for sk in sorted(need):
            val = need[sk]
            if self.wm[ek].get(sk, -1) >= val:
                continue
            e.wait_ge(self.sem[sk], val)
            self.wm[ek][sk] = val

    def _slot(self, key):
        return self.bufs.setdefault(key, {"w": None, "r": []})

    def op(self, ek, fn, reads=(), writes=(), deps=()):
        dl = list(deps)
        war = []
        for k in reads:
            dl.append(self._slot(k)["w"])
        for k in writes:
            s = self._slot(k)
            war.extend(s["r"])
            war.append(s["w"])
        self._wait(ek, dl, war)
        inst = fn(self.eng[ek])
        self.cnt[ek] += 1
        inst.then_inc(self.sem[ek], 1)
        tok = (ek, self.cnt[ek])
        for k in reads:
            self._slot(k)["r"].append(tok)
        for k in writes:
            s = self._slot(k)
            s["w"] = tok
            s["r"] = []
        return tok

    def dop(self, qk, out_ap, in_ap, reads=(), writes=(), deps=(), **kw):
        dk = {"s": "ds", "g": "dg"}[qk]
        dl = list(deps)
        for k in reads:
            dl.append(self._slot(k)["w"])
        for k in writes:
            s = self._slot(k)
            dl.extend(s["r"])
            dl.append(s["w"])
        self._wait(qk, dl, ())
        self.eng[qk].dma_start(out_ap, in_ap, **kw).then_inc(self.sem[dk], 16)
        self.cnt[dk] += 16
        tok = (dk, self.cnt[dk])
        for k in reads:
            self._slot(k)["r"].append(tok)
        for k in writes:
            s = self._slot(k)
            s["w"] = tok
            s["r"] = []
        return tok

    def barrier(self):
        for qk, dk in (("s", "ds"), ("g", "dg")):
            if self.cnt[dk] > 0 and self.wm[qk].get(dk, -1) < self.cnt[dk]:
                self.eng[qk].wait_ge(self.sem[dk], self.cnt[dk])
                self.wm[qk][dk] = self.cnt[dk]
        self.bar_n += 1
        for ek in self.ENGS:
            self.eng[ek].nop().then_inc(self.bar_sem, 1)
        for ek in self.ENGS:
            self.eng[ek].wait_ge(self.bar_sem, 5 * self.bar_n)
        self.bufs = {}
        for ek in self.ENGS:
            self.wm[ek] = {}


def build_nc(dump_stage=None):
    nc = bass.Bass("TRN2", target_bir_lowering=False)
    ap = lambda t: t.ap()
    x_in = ap(nc.dram_tensor("x", [T, C], F32, kind="ExternalInput"))
    wi_in = ap(nc.dram_tensor("wi", [C, C], F32, kind="ExternalInput"))
    bi_in = ap(nc.dram_tensor("bi", [1, C], F32, kind="ExternalInput"))
    wr_in = ap(nc.dram_tensor("wr", [C, C], F32, kind="ExternalInput"))
    br_in = ap(nc.dram_tensor("br", [1, C], F32, kind="ExternalInput"))
    lam_in = ap(nc.dram_tensor("lam", [1, 1], F32, kind="ExternalInput"))
    wqkv_in = ap(nc.dram_tensor("wqkv", [192, C], F32, kind="ExternalInput"))
    wout_in = ap(nc.dram_tensor("wout", [C, 64], F32, kind="ExternalInput"))
    wg_in = ap(nc.dram_tensor("wg", [I, C], F32, kind="ExternalInput"))
    wu_in = ap(nc.dram_tensor("wu", [I, C], F32, kind="ExternalInput"))
    wd_in = ap(nc.dram_tensor("wd", [C, I], F32, kind="ExternalInput"))
    out_hbm = ap(nc.dram_tensor("out", [T, C], F32, kind="ExternalOutput"))
    wqt_g = ap(nc.dram_tensor("wqt_g", [CC, 128, I], BF16, kind="Internal"))
    wqt_u = ap(nc.dram_tensor("wqt_u", [CC, 128, I], BF16, kind="Internal"))
    wqt_d = ap(nc.dram_tensor("wqt_d", [IC, 128, C], BF16, kind="Internal"))
    qT_dram = ap(nc.dram_tensor("qT_d", [64, T], BF16, kind="Internal"))

    gstack = contextlib.ExitStack()
    with gstack:
        sb = lambda n, shp, dt: gstack.enter_context(nc.sbuf_tensor(n, shp, dt))
        S = Sched(nc, gstack)

        x_s = sb("x_s", [128, TM, C], F32)
        xnT = sb("xnT", [128, CC, T], BF16)
        ident_b = sb("ident_b", [128, 128], BF16)
        tri_b = sb("tri_b", [128, 128], BF16)
        cvec = sb("cvec", [128, 2], F32)
        ccol = sb("ccol", [128, 5], F32)
        bih = sb("bih", [128, CC], F32)
        brh = sb("brh", [128, CC], F32)
        lamt = sb("lamt", [1, 8], F32)
        stat = sb("stat", [128, 4 * TM], F32)
        qS = sb("qS_", [128, 4, 512], BF16)
        qA = sb("qA_", [128, 4, 512], BF16)
        qB = sb("qB_", [128, 4, 512], BF16)
        qC = sb("qC_", [128, 4, 512], BF16)
        qM = sb("qM_", [128, 4, 32], F32)
        qR = sb("qR_", [128, 4, 32], F32)
        qL = sb("qL_", [128, 4, 32], BF16)
        ps_tr = gstack.enter_context(nc.psum_tensor("ps_tr", [128, 2, 1024], BF16))

        sx = lambda m: stat[:, m:m + 1]
        ssq = lambda m: stat[:, TM + m:TM + m + 1]
        mu = lambda m: stat[:, 2 * TM + m:2 * TM + m + 1]
        rstd = lambda m: stat[:, 3 * TM + m:3 * TM + m + 1]

        def pe_t_bf(src_ap, dst_ap, slot, reads, writes, ev="a"):
            f = src_ap.shape[-1]
            p = src_ap.shape[0]
            pk = ("ps_tr", slot)
            S.op("t", lambda e: e.transpose(ps_tr[0:f, slot, 0:p], src_ap,
                                            ident_b[0:p, 0:p]),
                 reads=list(reads) + ["identb"], writes=[pk])
            if ev == "a":
                S.op("a", lambda e: e.activation(dst_ap, ps_tr[0:f, slot, 0:p],
                                                 AF.Copy),
                     reads=[pk], writes=list(writes))
            else:
                S.op("v", lambda e: e.tensor_copy(dst_ap, ps_tr[0:f, slot, 0:p]),
                     reads=[pk], writes=list(writes))

        def ln_stats(m):
            S.op("a", lambda e: e.activation(
                    qS.rearrange("p a b -> p (a b)")[:, 0:C], x_s[:, m, :],
                    AF.Square, accum_out=ssq(m)),
                 reads=[("x_s", m)],
                 writes=[("QS", 0, "S"), ("QS", 1, "S"), ("ssq", m)])
            S.op("v", lambda e: e.reduce_sum(sx(m), x_s[:, m, :], AX.X),
                 reads=[("x_s", m)], writes=[("sx", m)])

        def ln_finish():
            sxa = stat[:, 0:TM]
            ssqa = stat[:, TM:2 * TM]
            mua = stat[:, 2 * TM:3 * TM]
            rsa = stat[:, 3 * TM:4 * TM]
            rd = [("sx", m) for m in range(TM)] + [("ssq", m) for m in range(TM)]
            S.op("v", lambda e: e.tensor_scalar(mua, sxa, 1.0 / C, None, OP.mult),
                 reads=rd, writes=["mua"])
            S.op("v", lambda e: e.tensor_mul(rsa, mua, mua),
                 reads=["mua"], writes=["rsa"])
            S.op("v", lambda e: e.scalar_tensor_tensor(rsa, ssqa, 1.0 / C, rsa,
                                                       OP.mult, OP.subtract),
                 reads=rd + ["rsa"], writes=["rsa"])
            S.op("v", lambda e: e.tensor_scalar(rsa, rsa, 1e-5, None, OP.add),
                 reads=["rsa"], writes=["rsa"])
            S.op("a", lambda e: e.activation(rsa, rsa, AF.Sqrt),
                 reads=["rsa"], writes=["rsa"])
            S.op("v", lambda e: e.reciprocal(rsa, rsa),
                 reads=["rsa"], writes=["rsa"])

        def ln_norm(m, out_ap, out_key):
            S.op("v", lambda e: e.tensor_scalar(out_ap, x_s[:, m, :], mu(m),
                                                rstd(m), OP.subtract, OP.mult),
                 reads=[("x_s", m), "mua", "rsa"], writes=[out_key])

        def quant_steps(src_ap, out_ap, W, src_key, out_key, qs):
            B = W // 16
            src3 = src_ap.rearrange("p (b k) -> p b k", k=16)
            M = qM[:, qs, 0:B]
            R = qR[:, qs, 0:B]
            L = qL[:, qs, 0:B]
            Sf = qS[:, qs, 0:W]
            A = qA[:, qs, 0:W]
            Bm = qB[:, qs, 0:W]
            Cm = qC[:, qs, 0:W]
            K = lambda n: ("QS", qs, n)
            R_b = R.unsqueeze(2).broadcast_to([128, B, 16])
            L_b = L.unsqueeze(2).broadcast_to([128, B, 16])
            B3 = Bm.rearrange("p (b k) -> p b k", k=16)
            A3 = A.rearrange("p (b k) -> p b k", k=16)
            S3 = Sf.rearrange("p (b k) -> p b k", k=16)
            st = []
            st.append(("v", lambda e: e.tensor_reduce(M, src3, AX.X, OP.max,
                                                      apply_absolute_value=True),
                       [src_key], [K("M")]))
            st.append(("v", lambda e: e.tensor_scalar(M, M, 6e-12, None, OP.max),
                       [K("M")], [K("M")]))
            st.append(("v", lambda e: e.reciprocal(R, M), [K("M")], [K("R")]))
            st.append(("v", lambda e: e.tensor_scalar(R, R, 24.0, None, OP.mult),
                       [K("R")], [K("R")]))
            st.append(("v", lambda e: e.tensor_scalar(L, M, 1.0 / 12.0, None,
                                                      OP.mult),
                       [K("M")], [K("L")]))
            st.append(("v", lambda e: e.tensor_tensor(B3, src3, R_b, OP.mult),
                       [src_key, K("R")], [K("B")]))
            st.append(("a", lambda e: e.activation(Sf, Bm, AF.Square),
                       [K("B")], [K("S")]))
            st.append(("v", lambda e: e.tensor_scalar(A, Sf, 1.0, None, OP.is_gt),
                       [K("S")], [K("A")]))
            for thr in (9.0, 25.0, 49.0):
                st.append(("v", lambda e, th=thr: e.scalar_tensor_tensor(
                        A, Sf, th, A, OP.is_gt, OP.add), [K("S"), K("A")],
                        [K("A")]))
            st.append(("v", lambda e: e.tensor_scalar(Bm, Sf, 100.0, None,
                                                      OP.is_gt),
                       [K("S"), K("B")], [K("B")]))
            st.append(("v", lambda e: e.scalar_tensor_tensor(Bm, Sf, 196.0, Bm,
                                                             OP.is_gt, OP.add),
                       [K("S"), K("B")], [K("B")]))
            st.append(("a", lambda e: e.activation(Cm, Sf, AF.Sign,
                                                   bias=ccol[:, 4:5]),
                       [K("S"), "ccol"], [K("C")]))
            st.append(("v", lambda e: e.scalar_tensor_tensor(Bm, Cm, 1.0, Bm,
                                                             OP.add, OP.add),
                       [K("B"), K("C")], [K("B")]))
            st.append(("v", lambda e: e.scalar_tensor_tensor(A, Bm, 2.0, A,
                                                             OP.mult, OP.add),
                       [K("A"), K("B")], [K("A")]))
            st.append(("v", lambda e: e.tensor_tensor(S3, A3, L_b, OP.mult),
                       [K("A"), K("L")], [K("S")]))
            st.append(("a", lambda e: e.activation(Cm, src_ap, AF.Sign),
                       [src_key, K("C")], [K("C")]))
            st.append(("v", lambda e: e.tensor_mul(out_ap, Sf, Cm),
                       [K("S"), K("C")], [out_key]))
            return st

        def run_quant(jobs):
            steps = [quant_steps(*j) for j in jobs]
            n = max(len(s) for s in steps)
            for i in range(n):
                for s in steps:
                    if i < len(s):
                        ek, fn, rd, wr = s[i]
                        S.op(ek, fn, reads=rd, writes=wr)

        st1 = contextlib.ExitStack()
        with st1:
            sb1 = lambda n, shp, dt: st1.enter_context(nc.sbuf_tensor(n, shp, dt))
            ps1 = lambda n, shp, dt: st1.enter_context(nc.psum_tensor(n, shp, dt))
            wiT = sb1("wiT", [128, CC, C], BF16)
            wrT = sb1("wrT", [128, CC, C], BF16)
            ident_f = sb1("ident_f", [128, 128], F32)
            tb_b = sb1("tb_b", [128, 4, 128], BF16)
            ones_f = sb1("ones_f", [128, 128], F32)
            onesrow = sb1("onesrow", [1, 128], F32)
            half_t = sb1("half_t", [1, 1], F32)
            ps_trf = ps1("ps_trf", [128, 2, 512], F32)
            ps_a = ps1("ps_a", [128, 4, 512], F32)
            ps_c = ps_a[:, 0, :]
            PSA0 = ("ps_a", 0)

            st1t = contextlib.ExitStack()
            with st1t:
                sb1t = lambda n, shp, dt: st1t.enter_context(
                    nc.sbuf_tensor(n, shp, dt))
                w_raw = sb1t("w_raw", [128, CC // 2, C], F32)
                wbf = sb1t("wbf", [128, CC, C], BF16)
                birow = sb1t("birow", [1, C], F32)
                brrow = sb1t("brrow", [1, C], F32)
                trif = sb1t("trif", [128, 128], F32)
                xn_t = sb1t("xn_t", [128, 2, C], BF16)

                xr = x_in.rearrange("(m p) c -> p m c", p=128)
                for j in range(4):
                    S.dop("g", x_s[:, 4 * j:4 * (j + 1), :],
                          xr[:, 4 * j:4 * (j + 1), :],
                          writes=[("x_s", m) for m in range(4 * j, 4 * j + 4)])
                S.dop("s", birow[:, :], bi_in, writes=["birow"])
                S.dop("s", brrow[:, :], br_in, writes=["brrow"])
                S.dop("s", lamt[:, 0:1], lam_in, writes=["lam0"])
                S.op("g", lambda e: e.memset(ones_f[:, :], 1.0), writes=["ones_f"])
                S.op("g", lambda e: e.memset(onesrow[:, :], 1.0), writes=["onesrow"])
                S.op("g", lambda e: e.memset(half_t[:, :], 0.5), writes=["half"])
                S.op("g", lambda e: e.memset(ccol[:, 0:1], 0.5), writes=["ccol"])
                S.op("g", lambda e: e.memset(ccol[:, 1:2], -1.0), writes=["ccol"])
                S.op("g", lambda e: e.memset(ccol[:, 2:3], 1.0), writes=["ccol"])
                S.op("g", lambda e: e.memset(ccol[:, 3:4], 0.125), writes=["ccol"])
                S.op("g", lambda e: e.memset(ccol[:, 4:5], -400.0), writes=["ccol"])
                S.op("g", lambda e: e.affine_select(ident_f[:, :], ones_f[:, :],
                                                    [[1, 128]], OP.is_equal, 0.0,
                                                    base=0, channel_multiplier=-1),
                     reads=["ones_f"], writes=["identf"])
                S.op("v", lambda e: e.tensor_copy(ident_b[:, :], ident_f[:, :]),
                     reads=["identf"], writes=["identb"])
                S.op("g", lambda e: e.affine_select(trif[:, :], ones_f[:, :],
                                                    [[1, 128]], OP.is_ge, 0.0,
                                                    base=0, channel_multiplier=-1),
                     reads=["ones_f"], writes=["trif"])
                S.op("v", lambda e: e.tensor_copy(tri_b[:, :], trif[:, :]),
                     reads=["trif"], writes=["tri"])
                S.op("a", lambda e: e.activation(lamt[:, 1:2], lamt[:, 0:1], AF.Exp,
                                                 scale=ccol[0:1, 1:2]),
                     reads=["lam0", "ccol"], writes=["lam1"])
                S.op("a", lambda e: e.activation(lamt[:, 2:3], lamt[:, 1:2], AF.Ln,
                                                 bias=ccol[0:1, 2:3]),
                     reads=["lam1", "ccol"], writes=["lam2"])
                S.op("v", lambda e: e.tensor_add(lamt[:, 3:4], lamt[:, 0:1],
                                                 lamt[:, 2:3]),
                     reads=["lam0", "lam2"], writes=["lam3"])
                S.op("v", lambda e: e.tensor_scalar(lamt[:, 4:5], lamt[:, 3:4],
                                                    -4.0, None, OP.mult),
                     reads=["lam3"], writes=["lam4"])
                S.op("v", lambda e: e.tensor_scalar(lamt[:, 5:6], lamt[:, 3:4],
                                                    -8.0, None, OP.mult),
                     reads=["lam3"], writes=["lam5"])
                S.op("t", lambda e: e.matmul(ps_c[:, 0:2], onesrow[:, :],
                                             lamt[:, 4:6], start=True, stop=True),
                     reads=["onesrow", "lam4", "lam5"], writes=[PSA0])
                S.op("v", lambda e: e.tensor_copy(cvec[:, :], ps_c[:, 0:2]),
                     reads=[PSA0], writes=["cvec"])
                for k in range(CC):
                    S.op("t", lambda e, k=k: e.matmul(
                            ps_c[:, 8 + k:9 + k],
                            birow[0:1, 128 * k:128 * (k + 1)], half_t[:, :],
                            start=True, stop=True),
                         reads=["birow", "half"], writes=[PSA0])
                S.op("v", lambda e: e.tensor_copy(bih[:, :], ps_c[:, 8:8 + CC]),
                     reads=[PSA0], writes=["bih"])
                for k in range(CC):
                    S.op("t", lambda e, k=k: e.matmul(
                            ps_c[:, 8 + k:9 + k],
                            brrow[0:1, 128 * k:128 * (k + 1)], half_t[:, :],
                            start=True, stop=True),
                         reads=["brrow", "half"], writes=[PSA0])
                S.op("v", lambda e: e.tensor_copy(brh[:, :], ps_c[:, 8:8 + CC]),
                     reads=[PSA0], writes=["brh"])

                for m in range(TM):
                    ln_stats(m)
                ln_finish()
                for m in range(TM):
                    sl = m % 2
                    ln_norm(m, xn_t[:, sl, :], ("xn_t", sl))
                    if dump_stage == 10:
                        S.op("v", lambda e, sl=sl: e.tensor_copy(
                                xsq[:, :], xn_t[:, sl, :]),
                             reads=[("xn_t", sl)], writes=["xsq"])
                        S.dop("s", out_hbm.rearrange("(m p) c -> p m c", p=128)
                              [:, m, :], xsq[:, :], reads=["xsq"])
                    for k in range(CC):
                        pe_t_bf(xn_t[:, sl, 128 * k:128 * (k + 1)],
                                xnT[:, k, 128 * m:128 * (m + 1)], k % 2,
                                reads=[("xn_t", sl)], writes=[("xnT", k, m)])
                for (w_src, dstT, dk) in ((wi_in, wiT, "wiT"),
                                          (wr_in, wrT, "wrT")):
                    wsr = w_src.rearrange("(d p) c -> p d c", p=128)
                    for j in range(2):
                        S.dop("s", w_raw[:, :, :], wsr[:, 4 * j:4 * (j + 1), :],
                              writes=["w_raw"])
                        S.op("g", lambda e, j=j: e.tensor_copy(
                                wbf[:, 4 * j:4 * (j + 1), :], w_raw[:, :, :]),
                             reads=["w_raw"], writes=[("wbf", j)])
                    for dcj in range(CC):
                        for k in range(CC):
                            pe_t_bf(wbf[:, dcj, 128 * k:128 * (k + 1)],
                                    dstT[:, k, 128 * dcj:128 * (dcj + 1)], k % 2,
                                    reads=[("wbf", dcj // 4)],
                                    writes=[(dk, k, dcj)])
                S.barrier()

            st1b = contextlib.ExitStack()
            with st1b:
                at_bufs = st1b.enter_context(
                    nc.sbuf_tensor("atb", [128, 5, T], F32))
                wraw = st1b.enter_context(nc.sbuf_tensor("wraw", [128, 2, C], F32))
                wqb = st1b.enter_context(nc.sbuf_tensor("wqb", [128, 2, C], BF16))
                AB = lambda i: at_bufs[:, i, :]

                def wq_chunk_gate(wsrc, dst, i):
                    jobs = []
                    for ii in (i, i + 1):
                        if ii >= IC:
                            break
                        sl = ii % 2
                        S.dop("s", wraw[:, sl, :],
                              wsrc[128 * ii:128 * (ii + 1), :],
                              writes=[("wraw", sl)])
                        jobs.append((wraw[:, sl, 0:512], wqb[:, sl, 0:512], 512,
                                     ("wraw", sl), ("wqb", sl, 0), 2 * sl))
                        jobs.append((wraw[:, sl, 512:1024], wqb[:, sl, 512:1024],
                                     512, ("wraw", sl), ("wqb", sl, 1),
                                     2 * sl + 1))
                    run_quant(jobs)
                    for ii in (i, i + 1):
                        if ii >= IC:
                            break
                        sl = ii % 2
                        for k in range(CC):
                            tslot = k % 2
                            dst_ap = tb_b[:, 2 + tslot, :]
                            pe_t_bf(wqb[:, sl, 128 * k:128 * (k + 1)], dst_ap,
                                    tslot, reads=[("wqb", sl, k // 4)],
                                    writes=[("wtt", tslot)])
                            S.dop("s", dst[k, :, 128 * ii:128 * (ii + 1)],
                                  dst_ap, reads=[("wtt", tslot)])

                def wq_chunk_down(j, pieces_grp):
                    jobs = []
                    for (p0, pw) in pieces_grp:
                        sl = (p0 // 1024) % 2
                        S.dop("s", wraw[:, sl, 0:pw],
                              wd_in[128 * j:128 * (j + 1), p0:p0 + pw],
                              writes=[("wraw", sl)])
                        w0 = 512
                        jobs.append((wraw[:, sl, 0:w0], wqb[:, sl, 0:w0], w0,
                                     ("wraw", sl), ("wqb", sl, 0), 2 * sl))
                        jobs.append((wraw[:, sl, w0:pw], wqb[:, sl, w0:pw],
                                     pw - w0, ("wraw", sl), ("wqb", sl, 1),
                                     2 * sl + 1))
                    run_quant(jobs)
                    for (p0, pw) in pieces_grp:
                        sl = (p0 // 1024) % 2
                        for kk in range(pw // 128):
                            k = p0 // 128 + kk
                            tslot = k % 2
                            dst_ap = tb_b[:, 2 + tslot, :]
                            pe_t_bf(wqb[:, sl, 128 * kk:128 * (kk + 1)], dst_ap,
                                    tslot,
                                    reads=[("wqb", sl, 0 if 128 * kk < 512
                                            else 1)],
                                    writes=[("wtt", tslot)])
                            S.dop("s", wqt_d[k, :, 128 * j:128 * (j + 1)],
                                  dst_ap, reads=[("wtt", tslot)])

                wq_jobs = []
                for (wsrc, dst) in ((wg_in, wqt_g), (wu_in, wqt_u)):
                    for i in range(0, IC, 2):
                        wq_jobs.append(lambda wsrc=wsrc, dst=dst, i=i:
                                       wq_chunk_gate(wsrc, dst, i))
                for j in range(CC):
                    wq_jobs.append(lambda j=j:
                                   wq_chunk_down(j, [(0, 1024), (1024, 1024)]))
                    wq_jobs.append(lambda j=j:
                                   wq_chunk_down(j, [(2048, 768)]))
                njobs = len(wq_jobs)
                ji = [0]

                def drain_wq(n):
                    while ji[0] < njobs and n > 0:
                        wq_jobs[ji[0]]()
                        ji[0] += 1
                        n -= 1

                for d in range(CC):
                    for (obuf, wT, wkeys, bcol, bkey) in (
                            (0, wiT, "wiT", bih, "bih"),
                            (1, wrT, "wrT", brh, "brh")):
                        for t4 in range(4):
                            slot = 2 * (t4 % 2) + obuf
                            pk = ("ps_a", slot)
                            for k in range(CC):
                                S.op("t", lambda e, k=k, t4=t4, wT=wT, d=d,
                                     slot=slot: e.matmul(
                                        ps_a[:, slot, :],
                                        wT[:, k, 128 * d:128 * (d + 1)],
                                        xnT[:, k, 512 * t4:512 * (t4 + 1)],
                                        start=(k == 0), stop=(k == CC - 1)),
                                     writes=[pk])
                            S.op("a", lambda e, t4=t4, d=d, bcol=bcol, obuf=obuf,
                                 slot=slot: e.activation(
                                    AB(obuf)[:, 512 * t4:512 * (t4 + 1)],
                                    ps_a[:, slot, :], AF.Tanh,
                                    scale=ccol[:, 0:1], bias=bcol[:, d:d + 1]),
                                 reads=[pk, bkey, "ccol"], writes=[("ab", obuf)])
                    S.op("a", lambda e: e.activation(AB(2)[:, :], AB(1)[:, :],
                                                     AF.Exp, scale=cvec[:, 0:1],
                                                     bias=cvec[:, 0:1]),
                         reads=[("ab", 1), "cvec"], writes=[("ab", 2)])
                    S.op("a", lambda e: e.activation(AB(3)[:, :], AB(1)[:, :],
                                                     AF.Exp, scale=cvec[:, 1:2],
                                                     bias=cvec[:, 1:2]),
                         reads=[("ab", 1), "cvec"], writes=[("ab", 3)])
                    S.op("a", lambda e: e.activation(AB(1)[:, :], AB(3)[:, :],
                                                     AF.Sqrt, scale=ccol[:, 1:2],
                                                     bias=ccol[:, 2:3]),
                         reads=[("ab", 3), "ccol"], writes=[("ab", 1)])
                    S.op("v", lambda e, d=d: e.scalar_tensor_tensor(
                            AB(4)[:, :], AB(0)[:, :], 1.0, xnT[:, d, :], OP.add,
                            OP.mult),
                         reads=[("ab", 0)], writes=[("ab", 4)])
                    S.op("v", lambda e: e.scalar_tensor_tensor(
                            AB(4)[:, :], AB(4)[:, :], 0.5, AB(1)[:, :], OP.mult,
                            OP.mult),
                         reads=[("ab", 4), ("ab", 1)], writes=[("ab", 4)])
                    S.op("v", lambda e: e.tensor_tensor_scan(
                            AB(3)[:, :], AB(2)[:, :], AB(4)[:, :], 0.0, OP.mult,
                            OP.add),
                         reads=[("ab", 2), ("ab", 4)], writes=[("ab", 3)])
                    if dump_stage in (11, 12, 13, 14, 15):
                        buf = {11: 0, 12: 1, 13: 2, 14: 4, 15: 3}[dump_stage]
                        dv = out_hbm.rearrange("(d p) c -> d p c", p=128)
                        S.dop("s", dv[2 * d, :, :], AB(buf)[:, 0:1024],
                              reads=[("ab", buf)])
                        S.dop("s", dv[2 * d + 1, :, :], AB(buf)[:, 1024:2048],
                              reads=[("ab", buf)])
                    for m in range(TM):
                        slot = m % 2
                        pk = ("ps_trf", slot)
                        S.op("t", lambda e, m=m, slot=slot: e.transpose(
                                ps_trf[:, slot, 0:128],
                                AB(3)[:, 128 * m:128 * (m + 1)], ident_f[:, :]),
                             reads=[("ab", 3), "identf"], writes=[pk])
                        S.op("v", lambda e, m=m, d=d, slot=slot: e.tensor_add(
                                x_s[:, m, 128 * d:128 * (d + 1)],
                                x_s[:, m, 128 * d:128 * (d + 1)],
                                ps_trf[:, slot, 0:128]),
                             reads=[pk, ("x_s", m)], writes=[("x_s", m)])
                    drain_wq(6)
                drain_wq(njobs)
        S.barrier()

        def ffn_phase(tag):
            st2 = contextlib.ExitStack()
            with st2:
                sb2 = lambda n, shp, dt: st2.enter_context(
                    nc.sbuf_tensor(n + tag, shp, dt))
                ps2 = lambda n, shp, dt: st2.enter_context(
                    nc.psum_tensor(n + tag, shp, dt))
                xn2 = sb2("xn2", [128, 2, C], F32)
                xq_t = sb2("xq_t", [128, 2, C], BF16)
                wqdT = sb2("wqdT", [128, IC, C], BF16)
                wcol = sb2("wcol", [128, 2, CC, 512], BF16)
                sg = sb2("sg", [128, 2, 512], F32)
                hsrc = sb2("hsrc", [128, 2, 512], F32)
                hqq = sb2("hqq", [128, 2, 512], BF16)
                hqT = sb2("hqT", [128, IC, 256], BF16)
                ps_g = ps2("ps_g", [128, 2, 512], F32)
                ps_d = ps2("ps_d", [128, 2, 512], F32)
                wqdr = wqt_d.rearrange("k p c -> p k c")
                for j in range(4):
                    k0, k1 = 6 * j, min(6 * (j + 1), IC)
                    S.dop("s", wqdT[:, k0:k1, :], wqdr[:, k0:k1, :],
                          writes=["wqdT"])
                for m in range(TM):
                    ln_stats(m)
                ln_finish()
                for m0 in range(0, TM, 2):
                    jobs = []
                    for m in (m0, m0 + 1):
                        sl = m % 2
                        ln_norm(m, xn2[:, sl, :], ("xn2", sl))
                        jobs.append((xn2[:, sl, 0:512], xq_t[:, sl, 0:512], 512,
                                     ("xn2", sl), ("xq", sl, 0), 2 * sl))
                        jobs.append((xn2[:, sl, 512:1024],
                                     xq_t[:, sl, 512:1024], 512,
                                     ("xn2", sl), ("xq", sl, 1), 2 * sl + 1))
                    run_quant(jobs)
                    for m in (m0, m0 + 1):
                        sl = m % 2
                        for k in range(CC):
                            pe_t_bf(xq_t[:, sl, 128 * k:128 * (k + 1)],
                                    xnT[:, k, 128 * m:128 * (m + 1)], k % 2,
                                    reads=[("xq", sl, k // 4)],
                                    writes=[("xqT", k, m)])
                icols = [(512 * n, min(512, I - 512 * n))
                         for n in range((I + 511) // 512)]
                wgr = wqt_g.rearrange("k p i -> p k i")
                wur = wqt_u.rearrange("k p i -> p k i")
                def down_mg(mg):
                    for mm in range(2):
                        m = 2 * mg + mm
                        for n2 in range(2):
                            pk = ("ps_d", n2)
                            hts = [("hqT", k2, mm) for k2 in range(IC)]
                            for k in range(IC):
                                S.op("t", lambda e, k=k, mm=mm, n2=n2:
                                     e.matmul(ps_d[:, n2, :],
                                              hqT[:, k, 128 * mm:128 * (mm + 1)],
                                              wqdT[:, k, 512 * n2:512 * (n2 + 1)],
                                              start=(k == 0), stop=(k == IC - 1)),
                                     reads=hts + ["wqdT"], writes=[pk])
                            S.op("v", lambda e, m=m, n2=n2: e.tensor_add(
                                    x_s[:, m, 512 * n2:512 * (n2 + 1)],
                                    x_s[:, m, 512 * n2:512 * (n2 + 1)],
                                    ps_d[:, n2, :]),
                                 reads=[pk, ("x_s", m)], writes=[("x_s", m)])

                sched = []
                for mg in range(8):
                    for (c0, cw) in icols:
                        sched.append(("g", mg, c0, cw))
                        sched.append(("u", mg, c0, cw))

                def load_col(idx, slot):
                    mat, _, c0, cw = sched[idx]
                    srcr = wgr if mat == "g" else wur
                    S.dop("s", wcol[:, slot, :, 0:cw], srcr[:, :, c0:c0 + cw],
                          writes=[("wcol", slot)])

                load_col(0, 0)
                for idx, (mat, mg, c0, cw) in enumerate(sched):
                    slot = idx % 2
                    if idx + 1 < len(sched):
                        load_col(idx + 1, 1 - slot)
                    if mat == "g":
                        for mm in range(2):
                            m = 2 * mg + mm
                            for k in range(CC):
                                S.op("t", lambda e, k=k, m=m, mm=mm, cw=cw,
                                     ws=slot: e.matmul(
                                        ps_g[:, mm, 0:cw],
                                        xnT[:, k, 128 * m:128 * (m + 1)],
                                        wcol[:, ws, k, 0:cw],
                                        start=(k == 0), stop=(k == CC - 1)),
                                     reads=[("wcol", slot), ("xqT", k, m)],
                                     writes=[("ps_g", mm)])
                            S.op("a", lambda e, mm=mm, cw=cw: e.activation(
                                    sg[:, mm, 0:cw], ps_g[:, mm, 0:cw], AF.Silu),
                                 reads=[("ps_g", mm)], writes=[("sg", mm)])
                    else:
                        for mm in range(2):
                            m = 2 * mg + mm
                            for k in range(CC):
                                S.op("t", lambda e, k=k, m=m, mm=mm, cw=cw,
                                     ws=slot: e.matmul(
                                        ps_g[:, mm, 0:cw],
                                        xnT[:, k, 128 * m:128 * (m + 1)],
                                        wcol[:, ws, k, 0:cw],
                                        start=(k == 0), stop=(k == CC - 1)),
                                     reads=[("wcol", slot), ("xqT", k, m),
                                            ("sg", mm)],
                                     writes=[("ps_g", mm)])
                            sl = mm % 2
                            S.op("v", lambda e, mm=mm, cw=cw, sl=sl:
                                 e.tensor_mul(hsrc[:, sl, 0:cw], sg[:, mm, 0:cw],
                                              ps_g[:, mm, 0:cw]),
                                 reads=[("sg", mm), ("ps_g", mm)],
                                 writes=[("hs", sl)])
                            if sl == 1:
                                qb0 = ((c0 // 512) % 2) * 2
                                run_quant([
                                    (hsrc[:, 0, 0:cw], hqq[:, 0, 0:cw], cw,
                                     ("hs", 0), ("hqq", 0), qb0),
                                    (hsrc[:, 1, 0:cw], hqq[:, 1, 0:cw], cw,
                                     ("hs", 1), ("hqq", 1), qb0 + 1)])
                                for q2 in range(2):
                                    mmq = mm - 1 + q2
                                    for kk in range(cw // 128):
                                        ic = (c0 // 128) + kk
                                        pe_t_bf(hqq[:, q2,
                                                    128 * kk:128 * (kk + 1)],
                                                hqT[:, ic,
                                                    128 * mmq:128 * (mmq + 1)],
                                                kk % 2, reads=[("hqq", q2)],
                                                writes=[("hqT", ic, mmq)])
                    if mat == "u" and (c0, cw) == icols[-1]:
                        down_mg(mg)
            S.barrier()

        ffn_phase("F1")
        if dump_stage == 2:
            for j in range(4):
                S.dop("s", out_hbm.rearrange("(m p) c -> p m c", p=128)
                      [:, 4 * j:4 * (j + 1), :], x_s[:, 4 * j:4 * (j + 1), :])

        st3 = contextlib.ExitStack()
        with st3:
            sb3 = lambda n, shp, dt: st3.enter_context(nc.sbuf_tensor(n, shp, dt))
            ps3 = lambda n, shp, dt: st3.enter_context(nc.psum_tensor(n, shp, dt))
            xn3 = sb3("xn3", [128, 2, C], BF16)
            wqkv_raw = sb3("wqkv_raw", [128, 2, C], F32)
            qkv_bf = sb3("qkv_bf", [128, 2, C], BF16)
            wqkvT = sb3("wqkvT", [128, CC, 192], BF16)
            wout_raw = sb3("wout_raw", [128, CC, 64], F32)
            wout_bf = sb3("wout_bf", [128, CC, 64], BF16)
            woutT = sb3("woutT", [64, C], BF16)
            qkv_sb = sb3("qkv_sb", [128, TM, 192], F32)
            q_bf = sb3("q_bf", [128, 64], BF16)
            qT_sb = sb3("qT_sb", [64, T], BF16)
            vo16 = sb3("vo16", [128, TM, 128], BF16)
            qb = sb3("qb", [128, 3, T], BF16)
            Pb = sb3("Pb", [128, 4, T], BF16)
            nd = sb3("nd", [128, T], F32)
            rec = sb3("rec", [64, T], F32)
            ohm = sb3("ohm", [64, T], BF16)
            ps_n = ps3("ps_n", [128, T], F32)

            for m in range(TM):
                ln_stats(m)
            ln_finish()
            for m in range(TM):
                sl = m % 2
                ln_norm(m, xn3[:, sl, :], ("xn3", sl))
                for k in range(CC):
                    pe_t_bf(xn3[:, sl, 128 * k:128 * (k + 1)],
                            xnT[:, k, 128 * m:128 * (m + 1)], k % 2,
                            reads=[("xn3", sl)], writes=[("xnT", k, m)])
            S.dop("s", wqkv_raw[:, 0, :], wqkv_in[0:128, :], writes=["qkvr"])
            S.dop("s", wqkv_raw[0:64, 1, :], wqkv_in[128:192, :], writes=["qkvr"])
            S.op("g", lambda e: e.tensor_copy(qkv_bf[:, :, :], wqkv_raw[:, :, :]),
                 reads=["qkvr"], writes=["qkv_bf"])
            for k in range(CC):
                pe_t_bf(qkv_bf[:, 0, 128 * k:128 * (k + 1)], wqkvT[:, k, 0:128],
                        k % 2, reads=["qkv_bf"], writes=[("wqkvT", k)])
                pe_t_bf(qkv_bf[0:64, 1, 128 * k:128 * (k + 1)],
                        wqkvT[:, k, 128:192], k % 2, reads=["qkv_bf"],
                        writes=[("wqkvT", k)])
            S.dop("s", wout_raw[:, :, :],
                  wout_in.rearrange("(cc p) h -> p cc h", p=128),
                  writes=["wout_raw"])
            S.op("g", lambda e: e.tensor_copy(wout_bf[:, :, :], wout_raw[:, :, :]),
                 reads=["wout_raw"], writes=["wout_bf"])
            for k in range(CC):
                pe_t_bf(wout_bf[:, k, 0:64], woutT[0:64, 128 * k:128 * (k + 1)],
                        k % 2, reads=["wout_bf"], writes=["woutT"])
            for m in range(TM):
                pkk = "ps_n"
                for k in range(CC):
                    S.op("t", lambda e, k=k, m=m: e.matmul(
                            ps_n[:, 0:192], xnT[:, k, 128 * m:128 * (m + 1)],
                            wqkvT[:, k, :], start=(k == 0), stop=(k == CC - 1)),
                         reads=[("xnT", k, m)] + [("wqkvT", kk) for kk in
                                                  range(CC)],
                         writes=[pkk])
                S.op("a", lambda e, m=m: e.activation(qkv_sb[:, m, 0:64],
                                                      ps_n[:, 0:64], AF.Copy),
                     reads=[pkk], writes=[("qkv", m)])
                S.op("a", lambda e, m=m: e.activation(qkv_sb[:, m, 64:128],
                                                      ps_n[:, 64:128], AF.Copy,
                                                      scale=ccol[:, 3:4]),
                     reads=[pkk, "ccol"], writes=[("qkv", m)])
                S.op("a", lambda e, m=m: e.activation(qkv_sb[:, m, 128:192],
                                                      ps_n[:, 128:192], AF.Copy),
                     reads=[pkk], writes=[("qkv", m)])
                S.op("v", lambda e, m=m: e.tensor_copy(q_bf[:, :],
                                                       qkv_sb[:, m, 0:64]),
                     reads=[("qkv", m)], writes=["q_bf"])
                pe_t_bf(q_bf[:, :], qT_sb[0:64, 128 * m:128 * (m + 1)], m % 2,
                        reads=["q_bf"], writes=["qT"])
            S.dop("s", qT_dram, qT_sb[0:64, :], reads=["qT"], writes=["qTd"])
            S.op("g", lambda e: e.memset(vo16[:, :, :], 0.0), writes=["vo"])
            for h in range(NH):
                if h > 0:
                    S.op("g", lambda e, h=h: e.memset(vo16[:, :, h - 1:h], 0.0),
                         writes=["vo"])
                    S.op("g", lambda e, h=h: e.memset(
                            vo16[:, :, 63 + h:64 + h], 0.0), writes=["vo"])
                S.op("v", lambda e, h=h: e.tensor_copy(
                        vo16[:, :, h:h + 1], qkv_sb[:, :, 128 + h:129 + h]),
                     reads=[("qkv", m2) for m2 in range(TM)], writes=["vo"])
                S.op("g", lambda e, h=h: e.memset(vo16[:, :, 64 + h:65 + h], 1.0),
                     writes=["vo"])
                qsl = h % 3
                S.dop("g", qb[:, qsl, :],
                      qT_dram[h:h + 1, :].partition_broadcast(128),
                      reads=["qTd"], writes=[("qb", qsl)])
                for b in range(TM):
                    ext = T - 128 * b
                    psl = (h * TM + b) % 4
                    S.op("a", lambda e, h=h, b=b, psl=psl, qsl=qsl, ext=ext:
                         e.activation(Pb[:, psl, 0:ext], qb[:, qsl, 128 * b:T],
                                      AF.Exp,
                                      scale=qkv_sb[:, b, 64 + h:65 + h]),
                         reads=[("qb", qsl), ("qkv", b)], writes=[("Pb", psl)])
                    S.op("v", lambda e, psl=psl: e.tensor_mul(
                            Pb[:, psl, 0:128], Pb[:, psl, 0:128], tri_b[:, :]),
                         reads=[("Pb", psl), "tri"], writes=[("Pb", psl)])
                    i0 = 128 * b
                    while i0 < T:
                        i1 = min((i0 // 512 + 1) * 512, T)
                        S.op("t", lambda e, b=b, psl=psl, i0=i0, i1=i1, h=h:
                             e.matmul(ps_n[:, i0:i1], vo16[:, b, :],
                                      Pb[:, psl, i0 - 128 * b:i1 - 128 * b],
                                      start=(h == 0 and b == 0),
                                      stop=(h == NH - 1)),
                             reads=["vo", ("Pb", psl)], writes=["ps_n"])
                        i0 = i1
            S.op("v", lambda e: e.tensor_copy(nd[:, :], ps_n[:, :]),
                 reads=["ps_n"], writes=["nd"])
            S.op("v", lambda e: e.reciprocal(rec[0:64, :], nd[64:128, :]),
                 reads=["nd"], writes=["rec"])
            S.op("v", lambda e: e.tensor_mul(ohm[0:64, :], nd[0:64, :],
                                             rec[0:64, :]),
                 reads=["nd", "rec"], writes=["ohm"])
            for m in range(TM):
                for n2 in range(2):
                    S.op("t", lambda e, m=m, n2=n2: e.matmul(
                            ps_n[:, 512 * n2:512 * (n2 + 1)],
                            ohm[0:64, 128 * m:128 * (m + 1)],
                            woutT[0:64, 512 * n2:512 * (n2 + 1)],
                            start=True, stop=True),
                         reads=["ohm", "woutT"], writes=["ps_n"])
                    S.op("v", lambda e, m=m, n2=n2: e.tensor_add(
                            x_s[:, m, 512 * n2:512 * (n2 + 1)],
                            x_s[:, m, 512 * n2:512 * (n2 + 1)],
                            ps_n[:, 512 * n2:512 * (n2 + 1)]),
                         reads=["ps_n", ("x_s", m)], writes=[("x_s", m)])
        S.barrier()
        if dump_stage == 3:
            for j in range(4):
                S.dop("s", out_hbm.rearrange("(m p) c -> p m c", p=128)
                      [:, 4 * j:4 * (j + 1), :], x_s[:, 4 * j:4 * (j + 1), :])

        ffn_phase("F2")

        if dump_stage is None:
            outr = out_hbm.rearrange("(m p) c -> p m c", p=128)
            for j in range(4):
                S.dop("s", outr[:, 4 * j:4 * (j + 1), :],
                      x_s[:, 4 * j:4 * (j + 1), :],
                      reads=[("x_s", m) for m in range(4 * j, 4 * j + 4)])
        S.eng["s"].wait_ge(S.sem["ds"], S.cnt["ds"])
        S.eng["g"].wait_ge(S.sem["dg"], S.cnt["dg"])
    return nc


_NC_CACHE = {}


def kernel(L0_attn_a_x, L0_attn_a_Wi, L0_attn_a_bi, L0_attn_a_Wr, L0_attn_a_br,
           L0_attn_a_cap_lambda, L0_attn_b_w_qkv, L0_attn_b_w_out,
           L0_ffn_w_gate, L0_ffn_w_up, L0_ffn_w_down):
    cc = np.ascontiguousarray
    x = cc(np.asarray(L0_attn_a_x, dtype=np.float32))
    wi = cc(np.asarray(L0_attn_a_Wi, dtype=np.float32))
    bi = cc(np.asarray(L0_attn_a_bi, dtype=np.float32)).reshape(1, C)
    wr = cc(np.asarray(L0_attn_a_Wr, dtype=np.float32))
    br = cc(np.asarray(L0_attn_a_br, dtype=np.float32)).reshape(1, C)
    lam = cc(np.asarray(L0_attn_a_cap_lambda, dtype=np.float32)).reshape(1, 1)
    wqkv = cc(np.asarray(L0_attn_b_w_qkv, dtype=np.float32))
    wout = cc(np.asarray(L0_attn_b_w_out, dtype=np.float32))
    wg = cc(np.asarray(L0_ffn_w_gate, dtype=np.float32))
    wu = cc(np.asarray(L0_ffn_w_up, dtype=np.float32))
    wd = cc(np.asarray(L0_ffn_w_down, dtype=np.float32))

    if "nc" not in _NC_CACHE:
        _NC_CACHE["nc"] = build_nc()
    nc = _NC_CACHE["nc"]
    in_maps = []
    for b in range(NCORES):
        in_maps.append({"x": cc(x[b]), "wi": wi, "bi": bi, "wr": wr, "br": br,
                        "lam": lam, "wqkv": wqkv, "wout": wout, "wg": wg,
                        "wu": wu, "wd": wd})
    res = run_bass_kernel_spmd(nc, in_maps, core_ids=list(range(NCORES)))
    return np.stack([r["out"] for r in res.results], axis=0)


# version 10
# speedup vs baseline: 1.3307x; 1.0446x over previous
import contextlib
import numpy as np
import concourse.bass as bass
import concourse.mybir as mybir
from concourse.bass_utils import run_bass_kernel_spmd

F32 = mybir.dt.float32
BF16 = mybir.dt.bfloat16
AF = mybir.ActivationFunctionType
OP = mybir.AluOpType
AX = mybir.AxisListType

T, C, I, NH = 2048, 1024, 2816, 64
TM, CC, IC = T // 128, C // 128, I // 128
NCORES = 8


class Sched:
    ENGS = ("v", "a", "t", "g", "s")

    def __init__(self, nc, stack):
        self.nc = nc
        self.eng = {"v": nc.vector, "a": nc.scalar, "t": nc.tensor,
                    "g": nc.gpsimd, "s": nc.sync}
        self.sem = {}
        self.cnt = {}
        self.wm = {k: {} for k in self.ENGS}
        for k in self.ENGS:
            self.sem[k] = stack.enter_context(nc.semaphore(f"S_{k}"))
            self.cnt[k] = 0
        for k in ("ds", "dg"):
            self.sem[k] = stack.enter_context(nc.semaphore(f"S_{k}"))
            self.cnt[k] = 0
        self.bar_sem = stack.enter_context(nc.semaphore("S_bar"))
        self.bar_n = 0
        self.bufs = {}

    def _wait(self, ek, deps, war=()):
        e = self.eng[ek]
        need = {}
        for d in deps:
            if d is None:
                continue
            sk, val = d
            if val > need.get(sk, 0):
                need[sk] = val
        for d in war:
            if d is None:
                continue
            sk, val = d
            if sk == ek:
                continue
            if val > need.get(sk, 0):
                need[sk] = val
        for sk in sorted(need):
            val = need[sk]
            if self.wm[ek].get(sk, -1) >= val:
                continue
            e.wait_ge(self.sem[sk], val)
            self.wm[ek][sk] = val

    def _slot(self, key):
        return self.bufs.setdefault(key, {"w": None, "r": []})

    def op(self, ek, fn, reads=(), writes=(), deps=()):
        dl = list(deps)
        war = []
        for k in reads:
            dl.append(self._slot(k)["w"])
        for k in writes:
            s = self._slot(k)
            war.extend(s["r"])
            war.append(s["w"])
        self._wait(ek, dl, war)
        inst = fn(self.eng[ek])
        self.cnt[ek] += 1
        inst.then_inc(self.sem[ek], 1)
        tok = (ek, self.cnt[ek])
        for k in reads:
            self._slot(k)["r"].append(tok)
        for k in writes:
            s = self._slot(k)
            s["w"] = tok
            s["r"] = []
        return tok

    def dop(self, qk, out_ap, in_ap, reads=(), writes=(), deps=(), **kw):
        dk = {"s": "ds", "g": "dg"}[qk]
        dl = list(deps)
        for k in reads:
            dl.append(self._slot(k)["w"])
        for k in writes:
            s = self._slot(k)
            dl.extend(s["r"])
            dl.append(s["w"])
        self._wait(qk, dl, ())
        self.eng[qk].dma_start(out_ap, in_ap, **kw).then_inc(self.sem[dk], 16)
        self.cnt[dk] += 16
        tok = (dk, self.cnt[dk])
        for k in reads:
            self._slot(k)["r"].append(tok)
        for k in writes:
            s = self._slot(k)
            s["w"] = tok
            s["r"] = []
        return tok

    def barrier(self):
        for qk, dk in (("s", "ds"), ("g", "dg")):
            if self.cnt[dk] > 0 and self.wm[qk].get(dk, -1) < self.cnt[dk]:
                self.eng[qk].wait_ge(self.sem[dk], self.cnt[dk])
                self.wm[qk][dk] = self.cnt[dk]
        self.bar_n += 1
        for ek in self.ENGS:
            self.eng[ek].nop().then_inc(self.bar_sem, 1)
        for ek in self.ENGS:
            self.eng[ek].wait_ge(self.bar_sem, 5 * self.bar_n)
        self.bufs = {}
        for ek in self.ENGS:
            self.wm[ek] = {}


def build_nc(dump_stage=None):
    nc = bass.Bass("TRN2", target_bir_lowering=False)
    ap = lambda t: t.ap()
    x_in = ap(nc.dram_tensor("x", [T, C], F32, kind="ExternalInput"))
    wi_in = ap(nc.dram_tensor("wi", [C, C], F32, kind="ExternalInput"))
    bi_in = ap(nc.dram_tensor("bi", [1, C], F32, kind="ExternalInput"))
    wr_in = ap(nc.dram_tensor("wr", [C, C], F32, kind="ExternalInput"))
    br_in = ap(nc.dram_tensor("br", [1, C], F32, kind="ExternalInput"))
    lam_in = ap(nc.dram_tensor("lam", [1, 1], F32, kind="ExternalInput"))
    wqkv_in = ap(nc.dram_tensor("wqkv", [192, C], F32, kind="ExternalInput"))
    wout_in = ap(nc.dram_tensor("wout", [C, 64], F32, kind="ExternalInput"))
    wg_in = ap(nc.dram_tensor("wg", [I, C], F32, kind="ExternalInput"))
    wu_in = ap(nc.dram_tensor("wu", [I, C], F32, kind="ExternalInput"))
    wd_in = ap(nc.dram_tensor("wd", [C, I], F32, kind="ExternalInput"))
    out_hbm = ap(nc.dram_tensor("out", [T, C], F32, kind="ExternalOutput"))
    wqt_g = ap(nc.dram_tensor("wqt_g", [CC, 128, I], BF16, kind="Internal"))
    wqt_u = ap(nc.dram_tensor("wqt_u", [CC, 128, I], BF16, kind="Internal"))
    wqt_d = ap(nc.dram_tensor("wqt_d", [IC, 128, C], BF16, kind="Internal"))
    qT_dram = ap(nc.dram_tensor("qT_d", [64, T], BF16, kind="Internal"))

    gstack = contextlib.ExitStack()
    with gstack:
        sb = lambda n, shp, dt: gstack.enter_context(nc.sbuf_tensor(n, shp, dt))
        S = Sched(nc, gstack)

        x_s = sb("x_s", [128, TM, C], F32)
        xnT = sb("xnT", [128, CC, T], BF16)
        ident_b = sb("ident_b", [128, 128], BF16)
        tri_b = sb("tri_b", [128, 128], BF16)
        cvec = sb("cvec", [128, 2], F32)
        ccol = sb("ccol", [128, 5], F32)
        bih = sb("bih", [128, CC], F32)
        brh = sb("brh", [128, CC], F32)
        lamt = sb("lamt", [1, 8], F32)
        stat = sb("stat", [128, 4 * TM], F32)
        qS = sb("qS_", [128, 4, 512], BF16)
        qA = sb("qA_", [128, 4, 512], BF16)
        qB = sb("qB_", [128, 4, 512], BF16)
        qC = sb("qC_", [128, 4, 512], BF16)
        qM = sb("qM_", [128, 4, 32], F32)
        qR = sb("qR_", [128, 4, 32], F32)
        qL = sb("qL_", [128, 4, 32], BF16)
        ps_tr = gstack.enter_context(nc.psum_tensor("ps_tr", [128, 2, 1024], BF16))

        sx = lambda m: stat[:, m:m + 1]
        ssq = lambda m: stat[:, TM + m:TM + m + 1]
        mu = lambda m: stat[:, 2 * TM + m:2 * TM + m + 1]
        rstd = lambda m: stat[:, 3 * TM + m:3 * TM + m + 1]

        def pe_t_bf(src_ap, dst_ap, slot, reads, writes, ev="a"):
            f = src_ap.shape[-1]
            p = src_ap.shape[0]
            pk = ("ps_tr", slot)
            S.op("t", lambda e: e.transpose(ps_tr[0:f, slot, 0:p], src_ap,
                                            ident_b[0:p, 0:p]),
                 reads=list(reads) + ["identb"], writes=[pk])
            if ev == "a":
                S.op("a", lambda e: e.activation(dst_ap, ps_tr[0:f, slot, 0:p],
                                                 AF.Copy),
                     reads=[pk], writes=list(writes))
            else:
                S.op("v", lambda e: e.tensor_copy(dst_ap, ps_tr[0:f, slot, 0:p]),
                     reads=[pk], writes=list(writes))

        def ln_stats(m):
            S.op("a", lambda e: e.activation(
                    qS.rearrange("p a b -> p (a b)")[:, 0:C], x_s[:, m, :],
                    AF.Square, accum_out=ssq(m)),
                 reads=[("x_s", m)],
                 writes=[("QS", 0, "S"), ("QS", 1, "S"), ("ssq", m)])
            S.op("a", lambda e: e.activation(
                    qS.rearrange("p a b -> p (a b)")[:, 0:C], x_s[:, m, :],
                    AF.Copy, accum_out=sx(m)),
                 reads=[("x_s", m)],
                 writes=[("QS", 0, "S"), ("QS", 1, "S"), ("sx", m)])

        def ln_finish():
            sxa = stat[:, 0:TM]
            ssqa = stat[:, TM:2 * TM]
            mua = stat[:, 2 * TM:3 * TM]
            rsa = stat[:, 3 * TM:4 * TM]
            rd = [("sx", m) for m in range(TM)] + [("ssq", m) for m in range(TM)]
            S.op("v", lambda e: e.tensor_scalar(mua, sxa, 1.0 / C, None, OP.mult),
                 reads=rd, writes=["mua"])
            S.op("v", lambda e: e.tensor_mul(rsa, mua, mua),
                 reads=["mua"], writes=["rsa"])
            S.op("v", lambda e: e.scalar_tensor_tensor(rsa, ssqa, 1.0 / C, rsa,
                                                       OP.mult, OP.subtract),
                 reads=rd + ["rsa"], writes=["rsa"])
            S.op("v", lambda e: e.tensor_scalar(rsa, rsa, 1e-5, None, OP.add),
                 reads=["rsa"], writes=["rsa"])
            S.op("a", lambda e: e.activation(rsa, rsa, AF.Sqrt),
                 reads=["rsa"], writes=["rsa"])
            S.op("v", lambda e: e.reciprocal(rsa, rsa),
                 reads=["rsa"], writes=["rsa"])

        def ln_norm(m, out_ap, out_key):
            S.op("v", lambda e: e.tensor_scalar(out_ap, x_s[:, m, :], mu(m),
                                                rstd(m), OP.subtract, OP.mult),
                 reads=[("x_s", m), "mua", "rsa"], writes=[out_key])

        def quant_steps(src_ap, out_ap, W, src_key, out_key, qs):
            B = W // 16
            src3 = src_ap.rearrange("p (b k) -> p b k", k=16)
            M = qM[:, qs, 0:B]
            R = qR[:, qs, 0:B]
            L = qL[:, qs, 0:B]
            Sf = qS[:, qs, 0:W]
            A = qA[:, qs, 0:W]
            Bm = qB[:, qs, 0:W]
            Cm = qC[:, qs, 0:W]
            K = lambda n: ("QS", qs, n)
            R_b = R.unsqueeze(2).broadcast_to([128, B, 16])
            L_b = L.unsqueeze(2).broadcast_to([128, B, 16])
            B3 = Bm.rearrange("p (b k) -> p b k", k=16)
            A3 = A.rearrange("p (b k) -> p b k", k=16)
            S3 = Sf.rearrange("p (b k) -> p b k", k=16)
            st = []
            st.append(("v", lambda e: e.tensor_reduce(M, src3, AX.X, OP.max,
                                                      apply_absolute_value=True),
                       [src_key], [K("M")]))
            st.append(("v", lambda e: e.tensor_scalar(M, M, 6e-12, None, OP.max),
                       [K("M")], [K("M")]))
            st.append(("v", lambda e: e.reciprocal(R, M), [K("M")], [K("R")]))
            st.append(("v", lambda e: e.tensor_scalar(R, R, 24.0, None, OP.mult),
                       [K("R")], [K("R")]))
            st.append(("v", lambda e: e.tensor_scalar(L, M, 1.0 / 12.0, None,
                                                      OP.mult),
                       [K("M")], [K("L")]))
            st.append(("v", lambda e: e.tensor_tensor(B3, src3, R_b, OP.mult),
                       [src_key, K("R")], [K("B")]))
            st.append(("a", lambda e: e.activation(Sf, Bm, AF.Square),
                       [K("B")], [K("S")]))
            st.append(("v", lambda e: e.tensor_scalar(A, Sf, 1.0, None, OP.is_gt),
                       [K("S")], [K("A")]))
            for thr in (9.0, 25.0, 49.0):
                st.append(("v", lambda e, th=thr: e.scalar_tensor_tensor(
                        A, Sf, th, A, OP.is_gt, OP.add), [K("S"), K("A")],
                        [K("A")]))
            st.append(("v", lambda e: e.tensor_scalar(Bm, Sf, 100.0, None,
                                                      OP.is_gt),
                       [K("S"), K("B")], [K("B")]))
            st.append(("v", lambda e: e.scalar_tensor_tensor(Bm, Sf, 196.0, Bm,
                                                             OP.is_gt, OP.add),
                       [K("S"), K("B")], [K("B")]))
            st.append(("a", lambda e: e.activation(Cm, Sf, AF.Sign,
                                                   bias=ccol[:, 4:5]),
                       [K("S"), "ccol"], [K("C")]))
            st.append(("v", lambda e: e.scalar_tensor_tensor(Bm, Cm, 1.0, Bm,
                                                             OP.add, OP.add),
                       [K("B"), K("C")], [K("B")]))
            st.append(("v", lambda e: e.scalar_tensor_tensor(A, Bm, 2.0, A,
                                                             OP.mult, OP.add),
                       [K("A"), K("B")], [K("A")]))
            st.append(("v", lambda e: e.tensor_tensor(S3, A3, L_b, OP.mult),
                       [K("A"), K("L")], [K("S")]))
            st.append(("a", lambda e: e.activation(Cm, src_ap, AF.Sign),
                       [src_key, K("C")], [K("C")]))
            st.append(("v", lambda e: e.tensor_mul(out_ap, Sf, Cm),
                       [K("S"), K("C")], [out_key]))
            return st

        def run_quant(jobs):
            steps = [quant_steps(*j) for j in jobs]
            n = max(len(s) for s in steps)
            for i in range(n):
                for s in steps:
                    if i < len(s):
                        ek, fn, rd, wr = s[i]
                        S.op(ek, fn, reads=rd, writes=wr)

        st1 = contextlib.ExitStack()
        with st1:
            sb1 = lambda n, shp, dt: st1.enter_context(nc.sbuf_tensor(n, shp, dt))
            ps1 = lambda n, shp, dt: st1.enter_context(nc.psum_tensor(n, shp, dt))
            wiT = sb1("wiT", [128, CC, C], BF16)
            wrT = sb1("wrT", [128, CC, C], BF16)
            ident_f = sb1("ident_f", [128, 128], F32)
            ones_f = sb1("ones_f", [128, 128], F32)
            onesrow = sb1("onesrow", [1, 128], F32)
            half_t = sb1("half_t", [1, 1], F32)
            ps_trf = ps1("ps_trf", [128, 2, 512], F32)
            ps_a = ps1("ps_a", [128, 4, 512], F32)
            ps_c = ps_a[:, 0, :]
            PSA0 = ("ps_a", 0)

            st1t = contextlib.ExitStack()
            with st1t:
                sb1t = lambda n, shp, dt: st1t.enter_context(
                    nc.sbuf_tensor(n, shp, dt))
                w_raw = sb1t("w_raw", [128, CC // 2, C], F32)
                wbf = sb1t("wbf", [128, CC, C], BF16)
                birow = sb1t("birow", [1, C], F32)
                brrow = sb1t("brrow", [1, C], F32)
                trif = sb1t("trif", [128, 128], F32)
                xn_t = sb1t("xn_t", [128, 2, C], BF16)

                xr = x_in.rearrange("(m p) c -> p m c", p=128)
                for j in range(4):
                    S.dop("g", x_s[:, 4 * j:4 * (j + 1), :],
                          xr[:, 4 * j:4 * (j + 1), :],
                          writes=[("x_s", m) for m in range(4 * j, 4 * j + 4)])
                S.dop("s", birow[:, :], bi_in, writes=["birow"])
                S.dop("s", brrow[:, :], br_in, writes=["brrow"])
                S.dop("s", lamt[:, 0:1], lam_in, writes=["lam0"])
                S.op("g", lambda e: e.memset(ones_f[:, :], 1.0), writes=["ones_f"])
                S.op("g", lambda e: e.memset(onesrow[:, :], 1.0), writes=["onesrow"])
                S.op("g", lambda e: e.memset(half_t[:, :], 0.5), writes=["half"])
                S.op("g", lambda e: e.memset(ccol[:, 0:1], 0.5), writes=["ccol"])
                S.op("g", lambda e: e.memset(ccol[:, 1:2], -1.0), writes=["ccol"])
                S.op("g", lambda e: e.memset(ccol[:, 2:3], 1.0), writes=["ccol"])
                S.op("g", lambda e: e.memset(ccol[:, 3:4], 0.125), writes=["ccol"])
                S.op("g", lambda e: e.memset(ccol[:, 4:5], -400.0), writes=["ccol"])
                S.op("g", lambda e: e.affine_select(ident_f[:, :], ones_f[:, :],
                                                    [[1, 128]], OP.is_equal, 0.0,
                                                    base=0, channel_multiplier=-1),
                     reads=["ones_f"], writes=["identf"])
                S.op("v", lambda e: e.tensor_copy(ident_b[:, :], ident_f[:, :]),
                     reads=["identf"], writes=["identb"])
                S.op("g", lambda e: e.affine_select(trif[:, :], ones_f[:, :],
                                                    [[1, 128]], OP.is_ge, 0.0,
                                                    base=0, channel_multiplier=-1),
                     reads=["ones_f"], writes=["trif"])
                S.op("v", lambda e: e.tensor_copy(tri_b[:, :], trif[:, :]),
                     reads=["trif"], writes=["tri"])
                S.op("a", lambda e: e.activation(lamt[:, 1:2], lamt[:, 0:1], AF.Exp,
                                                 scale=ccol[0:1, 1:2]),
                     reads=["lam0", "ccol"], writes=["lam1"])
                S.op("a", lambda e: e.activation(lamt[:, 2:3], lamt[:, 1:2], AF.Ln,
                                                 bias=ccol[0:1, 2:3]),
                     reads=["lam1", "ccol"], writes=["lam2"])
                S.op("v", lambda e: e.tensor_add(lamt[:, 3:4], lamt[:, 0:1],
                                                 lamt[:, 2:3]),
                     reads=["lam0", "lam2"], writes=["lam3"])
                S.op("v", lambda e: e.tensor_scalar(lamt[:, 4:5], lamt[:, 3:4],
                                                    -4.0, None, OP.mult),
                     reads=["lam3"], writes=["lam4"])
                S.op("v", lambda e: e.tensor_scalar(lamt[:, 5:6], lamt[:, 3:4],
                                                    -8.0, None, OP.mult),
                     reads=["lam3"], writes=["lam5"])
                S.op("t", lambda e: e.matmul(ps_c[:, 0:2], onesrow[:, :],
                                             lamt[:, 4:6], start=True, stop=True),
                     reads=["onesrow", "lam4", "lam5"], writes=[PSA0])
                S.op("v", lambda e: e.tensor_copy(cvec[:, :], ps_c[:, 0:2]),
                     reads=[PSA0], writes=["cvec"])
                for k in range(CC):
                    S.op("t", lambda e, k=k: e.matmul(
                            ps_c[:, 8 + k:9 + k],
                            birow[0:1, 128 * k:128 * (k + 1)], half_t[:, :],
                            start=True, stop=True),
                         reads=["birow", "half"], writes=[PSA0])
                S.op("v", lambda e: e.tensor_copy(bih[:, :], ps_c[:, 8:8 + CC]),
                     reads=[PSA0], writes=["bih"])
                for k in range(CC):
                    S.op("t", lambda e, k=k: e.matmul(
                            ps_c[:, 8 + k:9 + k],
                            brrow[0:1, 128 * k:128 * (k + 1)], half_t[:, :],
                            start=True, stop=True),
                         reads=["brrow", "half"], writes=[PSA0])
                S.op("v", lambda e: e.tensor_copy(brh[:, :], ps_c[:, 8:8 + CC]),
                     reads=[PSA0], writes=["brh"])

                for m in range(TM):
                    ln_stats(m)
                ln_finish()
                for m in range(TM):
                    sl = m % 2
                    ln_norm(m, xn_t[:, sl, :], ("xn_t", sl))
                    if dump_stage == 10:
                        S.op("v", lambda e, sl=sl: e.tensor_copy(
                                xsq[:, :], xn_t[:, sl, :]),
                             reads=[("xn_t", sl)], writes=["xsq"])
                        S.dop("s", out_hbm.rearrange("(m p) c -> p m c", p=128)
                              [:, m, :], xsq[:, :], reads=["xsq"])
                    for k in range(CC):
                        pe_t_bf(xn_t[:, sl, 128 * k:128 * (k + 1)],
                                xnT[:, k, 128 * m:128 * (m + 1)], k % 2,
                                reads=[("xn_t", sl)], writes=[("xnT", k, m)])
                for (w_src, dstT, dk) in ((wi_in, wiT, "wiT"),
                                          (wr_in, wrT, "wrT")):
                    wsr = w_src.rearrange("(d p) c -> p d c", p=128)
                    for j in range(2):
                        S.dop("s", w_raw[:, :, :], wsr[:, 4 * j:4 * (j + 1), :],
                              writes=["w_raw"])
                        S.op("g", lambda e, j=j: e.tensor_copy(
                                wbf[:, 4 * j:4 * (j + 1), :], w_raw[:, :, :]),
                             reads=["w_raw"], writes=[("wbf", j)])
                    for dcj in range(CC):
                        for k in range(CC):
                            pe_t_bf(wbf[:, dcj, 128 * k:128 * (k + 1)],
                                    dstT[:, k, 128 * dcj:128 * (dcj + 1)], k % 2,
                                    reads=[("wbf", dcj // 4)],
                                    writes=[(dk, k, dcj)])
                S.barrier()

            st1b = contextlib.ExitStack()
            with st1b:
                at_bufs = st1b.enter_context(
                    nc.sbuf_tensor("atb", [128, 5, T], F32))
                wraw = st1b.enter_context(nc.sbuf_tensor("wraw", [128, 3, C], F32))
                wct = [0]
                wqb = st1b.enter_context(nc.sbuf_tensor("wqb", [128, 2, C], BF16))
                stw = st1b.enter_context(
                    nc.sbuf_tensor("stw", [128, 2, 8, 128], BF16))
                AB = lambda i: at_bufs[:, i, :]

                def wq_chunk_gate(wsrc, dst, i):
                    jobs = []
                    for ii in (i, i + 1):
                        if ii >= IC:
                            break
                        sl = ii % 2
                        wsl = wct[0] % 3
                        wct[0] += 1
                        S.dop("s", wraw[:, wsl, :],
                              wsrc[128 * ii:128 * (ii + 1), :],
                              writes=[("wraw", wsl)])
                        jobs.append((wraw[:, wsl, 0:512], wqb[:, sl, 0:512], 512,
                                     ("wraw", wsl), ("wqb", sl, 0), 2 * sl))
                        jobs.append((wraw[:, wsl, 512:1024],
                                     wqb[:, sl, 512:1024], 512,
                                     ("wraw", wsl), ("wqb", sl, 1), 2 * sl + 1))
                    run_quant(jobs)
                    for ii in (i, i + 1):
                        if ii >= IC:
                            break
                        sl = ii % 2
                        for k in range(CC):
                            pe_t_bf(wqb[:, sl, 128 * k:128 * (k + 1)],
                                    stw[:, sl, k, :], k % 2,
                                    reads=[("wqb", sl, k // 4)],
                                    writes=[("stw", sl)])
                        S.dop("s",
                              dst.rearrange("k p i -> p k i")
                              [:, :, 128 * ii:128 * (ii + 1)],
                              stw[:, sl, :, :], reads=[("stw", sl)])

                def wq_chunk_down(j, pieces_grp):
                    jobs = []
                    for (p0, pw) in pieces_grp:
                        sl = (p0 // 1024) % 2
                        wsl = wct[0] % 3
                        wct[0] += 1
                        S.dop("s", wraw[:, wsl, 0:pw],
                              wd_in[128 * j:128 * (j + 1), p0:p0 + pw],
                              writes=[("wraw", wsl)])
                        w0 = 512
                        jobs.append((wraw[:, wsl, 0:w0], wqb[:, sl, 0:w0], w0,
                                     ("wraw", wsl), ("wqb", sl, 0), 2 * sl))
                        jobs.append((wraw[:, wsl, w0:pw], wqb[:, sl, w0:pw],
                                     pw - w0, ("wraw", wsl), ("wqb", sl, 1),
                                     2 * sl + 1))
                    run_quant(jobs)
                    for (p0, pw) in pieces_grp:
                        sl = (p0 // 1024) % 2
                        nt = pw // 128
                        for kk in range(nt):
                            pe_t_bf(wqb[:, sl, 128 * kk:128 * (kk + 1)],
                                    stw[:, sl, kk, :], kk % 2,
                                    reads=[("wqb", sl, 0 if 128 * kk < 512
                                            else 1)],
                                    writes=[("stw", sl)])
                        k0 = p0 // 128
                        S.dop("s",
                              wqt_d.rearrange("k p c -> p k c")
                              [:, k0:k0 + nt, 128 * j:128 * (j + 1)],
                              stw[:, sl, 0:nt, :], reads=[("stw", sl)])

                wq_jobs = []
                for (wsrc, dst) in ((wg_in, wqt_g), (wu_in, wqt_u)):
                    for i in range(0, IC, 2):
                        wq_jobs.append(lambda wsrc=wsrc, dst=dst, i=i:
                                       wq_chunk_gate(wsrc, dst, i))
                for j in range(CC):
                    wq_jobs.append(lambda j=j:
                                   wq_chunk_down(j, [(0, 1024), (1024, 1024)]))
                    wq_jobs.append(lambda j=j:
                                   wq_chunk_down(j, [(2048, 768)]))
                njobs = len(wq_jobs)
                ji = [0]

                def drain_wq(n):
                    while ji[0] < njobs and n > 0:
                        wq_jobs[ji[0]]()
                        ji[0] += 1
                        n -= 1

                for d in range(CC):
                    for (obuf, wT, wkeys, bcol, bkey) in (
                            (0, wiT, "wiT", bih, "bih"),
                            (1, wrT, "wrT", brh, "brh")):
                        for t4 in range(4):
                            slot = 2 * (t4 % 2) + obuf
                            pk = ("ps_a", slot)
                            for k in range(CC):
                                S.op("t", lambda e, k=k, t4=t4, wT=wT, d=d,
                                     slot=slot: e.matmul(
                                        ps_a[:, slot, :],
                                        wT[:, k, 128 * d:128 * (d + 1)],
                                        xnT[:, k, 512 * t4:512 * (t4 + 1)],
                                        start=(k == 0), stop=(k == CC - 1)),
                                     writes=[pk])
                            S.op("a", lambda e, t4=t4, d=d, bcol=bcol, obuf=obuf,
                                 slot=slot: e.activation(
                                    AB(obuf)[:, 512 * t4:512 * (t4 + 1)],
                                    ps_a[:, slot, :], AF.Tanh,
                                    scale=ccol[:, 0:1], bias=bcol[:, d:d + 1]),
                                 reads=[pk, bkey, "ccol"], writes=[("ab", obuf)])
                    S.op("a", lambda e: e.activation(AB(2)[:, :], AB(1)[:, :],
                                                     AF.Exp, scale=cvec[:, 0:1],
                                                     bias=cvec[:, 0:1]),
                         reads=[("ab", 1), "cvec"], writes=[("ab", 2)])
                    S.op("a", lambda e: e.activation(AB(3)[:, :], AB(1)[:, :],
                                                     AF.Exp, scale=cvec[:, 1:2],
                                                     bias=cvec[:, 1:2]),
                         reads=[("ab", 1), "cvec"], writes=[("ab", 3)])
                    S.op("a", lambda e: e.activation(AB(1)[:, :], AB(3)[:, :],
                                                     AF.Sqrt, scale=ccol[:, 1:2],
                                                     bias=ccol[:, 2:3]),
                         reads=[("ab", 3), "ccol"], writes=[("ab", 1)])
                    S.op("v", lambda e, d=d: e.scalar_tensor_tensor(
                            AB(4)[:, :], AB(0)[:, :], 1.0, xnT[:, d, :], OP.add,
                            OP.mult),
                         reads=[("ab", 0)], writes=[("ab", 4)])
                    S.op("v", lambda e: e.scalar_tensor_tensor(
                            AB(4)[:, :], AB(4)[:, :], 0.5, AB(1)[:, :], OP.mult,
                            OP.mult),
                         reads=[("ab", 4), ("ab", 1)], writes=[("ab", 4)])
                    S.op("v", lambda e: e.tensor_tensor_scan(
                            AB(3)[:, :], AB(2)[:, :], AB(4)[:, :], 0.0, OP.mult,
                            OP.add),
                         reads=[("ab", 2), ("ab", 4)], writes=[("ab", 3)])
                    if dump_stage in (11, 12, 13, 14, 15):
                        buf = {11: 0, 12: 1, 13: 2, 14: 4, 15: 3}[dump_stage]
                        dv = out_hbm.rearrange("(d p) c -> d p c", p=128)
                        S.dop("s", dv[2 * d, :, :], AB(buf)[:, 0:1024],
                              reads=[("ab", buf)])
                        S.dop("s", dv[2 * d + 1, :, :], AB(buf)[:, 1024:2048],
                              reads=[("ab", buf)])
                    for m in range(TM):
                        slot = m % 2
                        pk = ("ps_trf", slot)
                        S.op("t", lambda e, m=m, slot=slot: e.transpose(
                                ps_trf[:, slot, 0:128],
                                AB(3)[:, 128 * m:128 * (m + 1)], ident_f[:, :]),
                             reads=[("ab", 3), "identf"], writes=[pk])
                        S.op("v", lambda e, m=m, d=d, slot=slot: e.tensor_add(
                                x_s[:, m, 128 * d:128 * (d + 1)],
                                x_s[:, m, 128 * d:128 * (d + 1)],
                                ps_trf[:, slot, 0:128]),
                             reads=[pk, ("x_s", m)], writes=[("x_s", m)])
                    drain_wq(6)
                drain_wq(njobs)
        S.barrier()

        def ffn_phase(tag):
            st2 = contextlib.ExitStack()
            with st2:
                sb2 = lambda n, shp, dt: st2.enter_context(
                    nc.sbuf_tensor(n + tag, shp, dt))
                ps2 = lambda n, shp, dt: st2.enter_context(
                    nc.psum_tensor(n + tag, shp, dt))
                xn2 = sb2("xn2", [128, 2, C], F32)
                xq_t = sb2("xq_t", [128, 2, C], BF16)
                wqdT = sb2("wqdT", [128, IC, C], BF16)
                wcol = sb2("wcol", [128, 2, CC, 512], BF16)
                sg = sb2("sg", [128, 2, 512], F32)
                hsrc = sb2("hsrc", [128, 2, 512], F32)
                hqq = sb2("hqq", [128, 2, 512], BF16)
                hqT = sb2("hqT", [128, IC, 256], BF16)
                ps_g = ps2("ps_g", [128, 2, 512], F32)
                ps_d = ps2("ps_d", [128, 2, 512], F32)
                wqdr = wqt_d.rearrange("k p c -> p k c")
                for j in range(4):
                    k0, k1 = 6 * j, min(6 * (j + 1), IC)
                    S.dop("s", wqdT[:, k0:k1, :], wqdr[:, k0:k1, :],
                          writes=["wqdT"])
                for m in range(TM):
                    ln_stats(m)
                ln_finish()
                for m0 in range(0, TM, 2):
                    jobs = []
                    for m in (m0, m0 + 1):
                        sl = m % 2
                        ln_norm(m, xn2[:, sl, :], ("xn2", sl))
                        jobs.append((xn2[:, sl, 0:512], xq_t[:, sl, 0:512], 512,
                                     ("xn2", sl), ("xq", sl, 0), 2 * sl))
                        jobs.append((xn2[:, sl, 512:1024],
                                     xq_t[:, sl, 512:1024], 512,
                                     ("xn2", sl), ("xq", sl, 1), 2 * sl + 1))
                    run_quant(jobs)
                    for m in (m0, m0 + 1):
                        sl = m % 2
                        for k in range(CC):
                            pe_t_bf(xq_t[:, sl, 128 * k:128 * (k + 1)],
                                    xnT[:, k, 128 * m:128 * (m + 1)], k % 2,
                                    reads=[("xq", sl, k // 4)],
                                    writes=[("xqT", k, m)])
                icols = [(512 * n, min(512, I - 512 * n))
                         for n in range((I + 511) // 512)]
                wgr = wqt_g.rearrange("k p i -> p k i")
                wur = wqt_u.rearrange("k p i -> p k i")
                def down_mg(mg):
                    for mm in range(2):
                        m = 2 * mg + mm
                        for n2 in range(2):
                            pk = ("ps_d", n2)
                            hts = [("hqT", k2, mm) for k2 in range(IC)]
                            for k in range(IC):
                                S.op("t", lambda e, k=k, mm=mm, n2=n2:
                                     e.matmul(ps_d[:, n2, :],
                                              hqT[:, k, 128 * mm:128 * (mm + 1)],
                                              wqdT[:, k, 512 * n2:512 * (n2 + 1)],
                                              start=(k == 0), stop=(k == IC - 1)),
                                     reads=hts + ["wqdT"], writes=[pk])
                            S.op("v", lambda e, m=m, n2=n2: e.tensor_add(
                                    x_s[:, m, 512 * n2:512 * (n2 + 1)],
                                    x_s[:, m, 512 * n2:512 * (n2 + 1)],
                                    ps_d[:, n2, :]),
                                 reads=[pk, ("x_s", m)], writes=[("x_s", m)])

                sched = []
                for mg in range(8):
                    for (c0, cw) in icols:
                        sched.append(("g", mg, c0, cw))
                        sched.append(("u", mg, c0, cw))

                def load_col(idx, slot):
                    mat, _, c0, cw = sched[idx]
                    srcr = wgr if mat == "g" else wur
                    S.dop("s", wcol[:, slot, :, 0:cw], srcr[:, :, c0:c0 + cw],
                          writes=[("wcol", slot)])

                load_col(0, 0)
                for idx, (mat, mg, c0, cw) in enumerate(sched):
                    slot = idx % 2
                    if idx + 1 < len(sched):
                        load_col(idx + 1, 1 - slot)
                    if mat == "g":
                        for mm in range(2):
                            m = 2 * mg + mm
                            for k in range(CC):
                                S.op("t", lambda e, k=k, m=m, mm=mm, cw=cw,
                                     ws=slot: e.matmul(
                                        ps_g[:, mm, 0:cw],
                                        xnT[:, k, 128 * m:128 * (m + 1)],
                                        wcol[:, ws, k, 0:cw],
                                        start=(k == 0), stop=(k == CC - 1)),
                                     reads=[("wcol", slot), ("xqT", k, m)],
                                     writes=[("ps_g", mm)])
                            S.op("a", lambda e, mm=mm, cw=cw: e.activation(
                                    sg[:, mm, 0:cw], ps_g[:, mm, 0:cw], AF.Silu),
                                 reads=[("ps_g", mm)], writes=[("sg", mm)])
                    else:
                        for mm in range(2):
                            m = 2 * mg + mm
                            for k in range(CC):
                                S.op("t", lambda e, k=k, m=m, mm=mm, cw=cw,
                                     ws=slot: e.matmul(
                                        ps_g[:, mm, 0:cw],
                                        xnT[:, k, 128 * m:128 * (m + 1)],
                                        wcol[:, ws, k, 0:cw],
                                        start=(k == 0), stop=(k == CC - 1)),
                                     reads=[("wcol", slot), ("xqT", k, m),
                                            ("sg", mm)],
                                     writes=[("ps_g", mm)])
                            sl = mm % 2
                            S.op("v", lambda e, mm=mm, cw=cw, sl=sl:
                                 e.tensor_mul(hsrc[:, sl, 0:cw], sg[:, mm, 0:cw],
                                              ps_g[:, mm, 0:cw]),
                                 reads=[("sg", mm), ("ps_g", mm)],
                                 writes=[("hs", sl)])
                            if sl == 1:
                                qb0 = ((c0 // 512) % 2) * 2
                                run_quant([
                                    (hsrc[:, 0, 0:cw], hqq[:, 0, 0:cw], cw,
                                     ("hs", 0), ("hqq", 0), qb0),
                                    (hsrc[:, 1, 0:cw], hqq[:, 1, 0:cw], cw,
                                     ("hs", 1), ("hqq", 1), qb0 + 1)])
                                for q2 in range(2):
                                    mmq = mm - 1 + q2
                                    for kk in range(cw // 128):
                                        ic = (c0 // 128) + kk
                                        pe_t_bf(hqq[:, q2,
                                                    128 * kk:128 * (kk + 1)],
                                                hqT[:, ic,
                                                    128 * mmq:128 * (mmq + 1)],
                                                kk % 2, reads=[("hqq", q2)],
                                                writes=[("hqT", ic, mmq)])
                    if mat == "u" and (c0, cw) == icols[-1]:
                        down_mg(mg)
            S.barrier()

        ffn_phase("F1")
        if dump_stage == 2:
            for j in range(4):
                S.dop("s", out_hbm.rearrange("(m p) c -> p m c", p=128)
                      [:, 4 * j:4 * (j + 1), :], x_s[:, 4 * j:4 * (j + 1), :])

        st3 = contextlib.ExitStack()
        with st3:
            sb3 = lambda n, shp, dt: st3.enter_context(nc.sbuf_tensor(n, shp, dt))
            ps3 = lambda n, shp, dt: st3.enter_context(nc.psum_tensor(n, shp, dt))
            xn3 = sb3("xn3", [128, 2, C], BF16)
            wqkv_raw = sb3("wqkv_raw", [128, 2, C], F32)
            qkv_bf = sb3("qkv_bf", [128, 2, C], BF16)
            wqkvT = sb3("wqkvT", [128, CC, 192], BF16)
            wout_raw = sb3("wout_raw", [128, CC, 64], F32)
            wout_bf = sb3("wout_bf", [128, CC, 64], BF16)
            woutT = sb3("woutT", [64, C], BF16)
            qkv_sb = sb3("qkv_sb", [128, TM, 192], F32)
            q_bf = sb3("q_bf", [128, 64], BF16)
            qT_sb = sb3("qT_sb", [64, T], BF16)
            vo16 = sb3("vo16", [128, TM, 128], BF16)
            qb = sb3("qb", [128, 4, T], BF16)
            Pb = sb3("Pb", [128, 4, T], BF16)
            nd = sb3("nd", [128, T], F32)
            ohm = qA.rearrange("p a b -> p (a b)")[0:64, 0:T]
            rec0 = qB.rearrange("p a b -> p (a b)")[0:64, 0:T]
            ps_n = ps3("ps_n", [128, T], F32)

            for m in range(TM):
                ln_stats(m)
            ln_finish()
            for m in range(TM):
                sl = m % 2
                ln_norm(m, xn3[:, sl, :], ("xn3", sl))
                for k in range(CC):
                    pe_t_bf(xn3[:, sl, 128 * k:128 * (k + 1)],
                            xnT[:, k, 128 * m:128 * (m + 1)], k % 2,
                            reads=[("xn3", sl)], writes=[("xnT", k, m)])
            S.dop("s", wqkv_raw[:, 0, :], wqkv_in[0:128, :], writes=["qkvr"])
            S.dop("s", wqkv_raw[0:64, 1, :], wqkv_in[128:192, :], writes=["qkvr"])
            S.op("g", lambda e: e.tensor_copy(qkv_bf[:, :, :], wqkv_raw[:, :, :]),
                 reads=["qkvr"], writes=["qkv_bf"])
            for k in range(CC):
                pe_t_bf(qkv_bf[:, 0, 128 * k:128 * (k + 1)], wqkvT[:, k, 0:128],
                        k % 2, reads=["qkv_bf"], writes=[("wqkvT", k)])
                pe_t_bf(qkv_bf[0:64, 1, 128 * k:128 * (k + 1)],
                        wqkvT[:, k, 128:192], k % 2, reads=["qkv_bf"],
                        writes=[("wqkvT", k)])
            S.dop("s", wout_raw[:, :, :],
                  wout_in.rearrange("(cc p) h -> p cc h", p=128),
                  writes=["wout_raw"])
            S.op("g", lambda e: e.tensor_copy(wout_bf[:, :, :], wout_raw[:, :, :]),
                 reads=["wout_raw"], writes=["wout_bf"])
            for k in range(CC):
                pe_t_bf(wout_bf[:, k, 0:64], woutT[0:64, 128 * k:128 * (k + 1)],
                        k % 2, reads=["wout_bf"], writes=["woutT"])
            for m in range(TM):
                pkk = "ps_n"
                for k in range(CC):
                    S.op("t", lambda e, k=k, m=m: e.matmul(
                            ps_n[:, 0:192], xnT[:, k, 128 * m:128 * (m + 1)],
                            wqkvT[:, k, :], start=(k == 0), stop=(k == CC - 1)),
                         reads=[("xnT", k, m)] + [("wqkvT", kk) for kk in
                                                  range(CC)],
                         writes=[pkk])
                S.op("a", lambda e, m=m: e.activation(qkv_sb[:, m, 0:64],
                                                      ps_n[:, 0:64], AF.Copy),
                     reads=[pkk], writes=[("qkv", m)])
                S.op("a", lambda e, m=m: e.activation(qkv_sb[:, m, 64:128],
                                                      ps_n[:, 64:128], AF.Copy,
                                                      scale=ccol[:, 3:4]),
                     reads=[pkk, "ccol"], writes=[("qkv", m)])
                S.op("a", lambda e, m=m: e.activation(qkv_sb[:, m, 128:192],
                                                      ps_n[:, 128:192], AF.Copy),
                     reads=[pkk], writes=[("qkv", m)])
                S.op("v", lambda e, m=m: e.tensor_copy(q_bf[:, :],
                                                       qkv_sb[:, m, 0:64]),
                     reads=[("qkv", m)], writes=["q_bf"])
                pe_t_bf(q_bf[:, :], qT_sb[0:64, 128 * m:128 * (m + 1)], m % 2,
                        reads=["q_bf"], writes=["qT"])
            S.dop("s", qT_dram, qT_sb[0:64, :], reads=["qT"], writes=["qTd"])
            S.op("g", lambda e: e.memset(vo16[:, :, :], 0.0), writes=["vo"])
            for h in range(NH):
                if h > 0:
                    S.op("g", lambda e, h=h: e.memset(vo16[:, :, h - 1:h], 0.0),
                         writes=["vo"])
                    S.op("g", lambda e, h=h: e.memset(
                            vo16[:, :, 63 + h:64 + h], 0.0), writes=["vo"])
                S.op("v", lambda e, h=h: e.tensor_copy(
                        vo16[:, :, h:h + 1], qkv_sb[:, :, 128 + h:129 + h]),
                     reads=[("qkv", m2) for m2 in range(TM)], writes=["vo"])
                S.op("g", lambda e, h=h: e.memset(vo16[:, :, 64 + h:65 + h], 1.0),
                     writes=["vo"])
                qsl = h % 4
                S.dop("g", qb[:, qsl, :],
                      qT_dram[h:h + 1, :].partition_broadcast(128),
                      reads=["qTd"], writes=[("qb", qsl)])
                for b in range(TM):
                    ext = T - 128 * b
                    psl = (h * TM + b) % 4
                    S.op("a", lambda e, h=h, b=b, psl=psl, qsl=qsl, ext=ext:
                         e.activation(Pb[:, psl, 0:ext], qb[:, qsl, 128 * b:T],
                                      AF.Exp,
                                      scale=qkv_sb[:, b, 64 + h:65 + h]),
                         reads=[("qb", qsl), ("qkv", b)], writes=[("Pb", psl)])
                    S.op("v", lambda e, psl=psl: e.tensor_mul(
                            Pb[:, psl, 0:128], Pb[:, psl, 0:128], tri_b[:, :]),
                         reads=[("Pb", psl), "tri"], writes=[("Pb", psl)])
                    i0 = 128 * b
                    while i0 < T:
                        i1 = min((i0 // 512 + 1) * 512, T)
                        S.op("t", lambda e, b=b, psl=psl, i0=i0, i1=i1, h=h:
                             e.matmul(ps_n[:, i0:i1], vo16[:, b, :],
                                      Pb[:, psl, i0 - 128 * b:i1 - 128 * b],
                                      start=(h == 0 and b == 0),
                                      stop=(h == NH - 1)),
                             reads=["vo", ("Pb", psl)], writes=["ps_n"])
                        i0 = i1
            S.op("v", lambda e: e.tensor_copy(nd[:, :], ps_n[:, :]),
                 reads=["ps_n"], writes=["nd"])
            def _recip(e):
                with nc.allow_low_precision(
                        reason="softmax denom reciprocal stored bf16; "
                               "~0.4% on a small additive term"):
                    return e.reciprocal(rec0, nd[64:128, :])
            S.op("v", _recip, reads=["nd"], writes=["rec0"])
            S.op("v", lambda e: e.tensor_mul(ohm, nd[0:64, :], rec0),
                 reads=["nd", "rec0"], writes=["ohm"])
            for m in range(TM):
                for n2 in range(2):
                    S.op("t", lambda e, m=m, n2=n2: e.matmul(
                            ps_n[:, 512 * n2:512 * (n2 + 1)],
                            ohm[:, 128 * m:128 * (m + 1)],
                            woutT[0:64, 512 * n2:512 * (n2 + 1)],
                            start=True, stop=True),
                         reads=["ohm", "woutT"], writes=["ps_n"])
                    S.op("v", lambda e, m=m, n2=n2: e.tensor_add(
                            x_s[:, m, 512 * n2:512 * (n2 + 1)],
                            x_s[:, m, 512 * n2:512 * (n2 + 1)],
                            ps_n[:, 512 * n2:512 * (n2 + 1)]),
                         reads=["ps_n", ("x_s", m)], writes=[("x_s", m)])
        S.barrier()
        if dump_stage == 3:
            for j in range(4):
                S.dop("s", out_hbm.rearrange("(m p) c -> p m c", p=128)
                      [:, 4 * j:4 * (j + 1), :], x_s[:, 4 * j:4 * (j + 1), :])

        ffn_phase("F2")

        if dump_stage is None:
            outr = out_hbm.rearrange("(m p) c -> p m c", p=128)
            for j in range(4):
                S.dop("s", outr[:, 4 * j:4 * (j + 1), :],
                      x_s[:, 4 * j:4 * (j + 1), :],
                      reads=[("x_s", m) for m in range(4 * j, 4 * j + 4)])
        S.eng["s"].wait_ge(S.sem["ds"], S.cnt["ds"])
        S.eng["g"].wait_ge(S.sem["dg"], S.cnt["dg"])
    return nc


_NC_CACHE = {}


def kernel(L0_attn_a_x, L0_attn_a_Wi, L0_attn_a_bi, L0_attn_a_Wr, L0_attn_a_br,
           L0_attn_a_cap_lambda, L0_attn_b_w_qkv, L0_attn_b_w_out,
           L0_ffn_w_gate, L0_ffn_w_up, L0_ffn_w_down):
    cc = np.ascontiguousarray
    x = cc(np.asarray(L0_attn_a_x, dtype=np.float32))
    wi = cc(np.asarray(L0_attn_a_Wi, dtype=np.float32))
    bi = cc(np.asarray(L0_attn_a_bi, dtype=np.float32)).reshape(1, C)
    wr = cc(np.asarray(L0_attn_a_Wr, dtype=np.float32))
    br = cc(np.asarray(L0_attn_a_br, dtype=np.float32)).reshape(1, C)
    lam = cc(np.asarray(L0_attn_a_cap_lambda, dtype=np.float32)).reshape(1, 1)
    wqkv = cc(np.asarray(L0_attn_b_w_qkv, dtype=np.float32))
    wout = cc(np.asarray(L0_attn_b_w_out, dtype=np.float32))
    wg = cc(np.asarray(L0_ffn_w_gate, dtype=np.float32))
    wu = cc(np.asarray(L0_ffn_w_up, dtype=np.float32))
    wd = cc(np.asarray(L0_ffn_w_down, dtype=np.float32))

    if "nc" not in _NC_CACHE:
        _NC_CACHE["nc"] = build_nc()
    nc = _NC_CACHE["nc"]
    in_maps = []
    for b in range(NCORES):
        in_maps.append({"x": cc(x[b]), "wi": wi, "bi": bi, "wr": wr, "br": br,
                        "lam": lam, "wqkv": wqkv, "wout": wout, "wg": wg,
                        "wu": wu, "wd": wd})
    res = run_bass_kernel_spmd(nc, in_maps, core_ids=list(range(NCORES)))
    return np.stack([r["out"] for r in res.results], axis=0)


# version 11
# speedup vs baseline: 1.4530x; 1.0919x over previous
import contextlib
import numpy as np
import concourse.bass as bass
import concourse.mybir as mybir
from concourse.bass_utils import run_bass_kernel_spmd

F32 = mybir.dt.float32
BF16 = mybir.dt.bfloat16
AF = mybir.ActivationFunctionType
OP = mybir.AluOpType
AX = mybir.AxisListType

T, C, I, NH = 2048, 1024, 2816, 64
TM, CC, IC = T // 128, C // 128, I // 128
NCORES = 8


class Sched:
    ENGS = ("v", "a", "t", "g", "s")

    def __init__(self, nc, stack):
        self.nc = nc
        self.eng = {"v": nc.vector, "a": nc.scalar, "t": nc.tensor,
                    "g": nc.gpsimd, "s": nc.sync}
        self.sem = {}
        self.cnt = {}
        self.wm = {k: {} for k in self.ENGS}
        for k in self.ENGS:
            self.sem[k] = stack.enter_context(nc.semaphore(f"S_{k}"))
            self.cnt[k] = 0
        for k in ("ds", "dg"):
            self.sem[k] = stack.enter_context(nc.semaphore(f"S_{k}"))
            self.cnt[k] = 0
        self.bar_sem = stack.enter_context(nc.semaphore("S_bar"))
        self.bar_n = 0
        self.bufs = {}

    def _wait(self, ek, deps, war=()):
        e = self.eng[ek]
        need = {}
        for d in deps:
            if d is None:
                continue
            sk, val = d
            if val > need.get(sk, 0):
                need[sk] = val
        for d in war:
            if d is None:
                continue
            sk, val = d
            if sk == ek:
                continue
            if val > need.get(sk, 0):
                need[sk] = val
        for sk in sorted(need):
            val = need[sk]
            if self.wm[ek].get(sk, -1) >= val:
                continue
            e.wait_ge(self.sem[sk], val)
            self.wm[ek][sk] = val

    def _slot(self, key):
        return self.bufs.setdefault(key, {"w": None, "r": []})

    def op(self, ek, fn, reads=(), writes=(), deps=()):
        dl = list(deps)
        war = []
        for k in reads:
            dl.append(self._slot(k)["w"])
        for k in writes:
            s = self._slot(k)
            war.extend(s["r"])
            war.append(s["w"])
        self._wait(ek, dl, war)
        inst = fn(self.eng[ek])
        self.cnt[ek] += 1
        inst.then_inc(self.sem[ek], 1)
        tok = (ek, self.cnt[ek])
        for k in reads:
            self._slot(k)["r"].append(tok)
        for k in writes:
            s = self._slot(k)
            s["w"] = tok
            s["r"] = []
        return tok

    def dop(self, qk, out_ap, in_ap, reads=(), writes=(), deps=(), **kw):
        dk = {"s": "ds", "g": "dg"}[qk]
        dl = list(deps)
        for k in reads:
            dl.append(self._slot(k)["w"])
        for k in writes:
            s = self._slot(k)
            dl.extend(s["r"])
            dl.append(s["w"])
        self._wait(qk, dl, ())
        self.eng[qk].dma_start(out_ap, in_ap, **kw).then_inc(self.sem[dk], 16)
        self.cnt[dk] += 16
        tok = (dk, self.cnt[dk])
        for k in reads:
            self._slot(k)["r"].append(tok)
        for k in writes:
            s = self._slot(k)
            s["w"] = tok
            s["r"] = []
        return tok

    def barrier(self):
        for qk, dk in (("s", "ds"), ("g", "dg")):
            if self.cnt[dk] > 0 and self.wm[qk].get(dk, -1) < self.cnt[dk]:
                self.eng[qk].wait_ge(self.sem[dk], self.cnt[dk])
                self.wm[qk][dk] = self.cnt[dk]
        self.bar_n += 1
        for ek in self.ENGS:
            self.eng[ek].nop().then_inc(self.bar_sem, 1)
        for ek in self.ENGS:
            self.eng[ek].wait_ge(self.bar_sem, 5 * self.bar_n)
        self.bufs = {}
        for ek in self.ENGS:
            self.wm[ek] = {}


def build_nc(dump_stage=None):
    nc = bass.Bass("TRN2", target_bir_lowering=False)
    ap = lambda t: t.ap()
    x_in = ap(nc.dram_tensor("x", [T, C], F32, kind="ExternalInput"))
    wi_in = ap(nc.dram_tensor("wi", [C, C], F32, kind="ExternalInput"))
    bi_in = ap(nc.dram_tensor("bi", [1, C], F32, kind="ExternalInput"))
    wr_in = ap(nc.dram_tensor("wr", [C, C], F32, kind="ExternalInput"))
    br_in = ap(nc.dram_tensor("br", [1, C], F32, kind="ExternalInput"))
    lam_in = ap(nc.dram_tensor("lam", [1, 1], F32, kind="ExternalInput"))
    wqkv_in = ap(nc.dram_tensor("wqkv", [192, C], F32, kind="ExternalInput"))
    wout_in = ap(nc.dram_tensor("wout", [C, 64], F32, kind="ExternalInput"))
    wg_in = ap(nc.dram_tensor("wg", [I, C], F32, kind="ExternalInput"))
    wu_in = ap(nc.dram_tensor("wu", [I, C], F32, kind="ExternalInput"))
    wd_in = ap(nc.dram_tensor("wd", [C, I], F32, kind="ExternalInput"))
    out_hbm = ap(nc.dram_tensor("out", [T, C], F32, kind="ExternalOutput"))
    wqt_g = ap(nc.dram_tensor("wqt_g", [CC, 128, I], BF16, kind="Internal"))
    wqt_u = ap(nc.dram_tensor("wqt_u", [CC, 128, I], BF16, kind="Internal"))
    wqt_d = ap(nc.dram_tensor("wqt_d", [IC, 128, C], BF16, kind="Internal"))
    qT_dram = ap(nc.dram_tensor("qT_d", [64, T], BF16, kind="Internal"))

    gstack = contextlib.ExitStack()
    with gstack:
        sb = lambda n, shp, dt: gstack.enter_context(nc.sbuf_tensor(n, shp, dt))
        S = Sched(nc, gstack)

        x_s = sb("x_s", [128, TM, C], F32)
        xnT = sb("xnT", [128, CC, T], BF16)
        ident_b = sb("ident_b", [128, 128], BF16)
        tri_b = sb("tri_b", [128, 128], BF16)
        cvec = sb("cvec", [128, 2], F32)
        ccol = sb("ccol", [128, 5], F32)
        bih = sb("bih", [128, CC], F32)
        brh = sb("brh", [128, CC], F32)
        lamt = sb("lamt", [1, 8], F32)
        stat = sb("stat", [128, 4 * TM], F32)
        qS = sb("qS_", [128, 4, 512], BF16)
        qA = sb("qA_", [128, 4, 512], BF16)
        qB = sb("qB_", [128, 4, 512], BF16)
        qC = sb("qC_", [128, 4, 512], BF16)
        qM = sb("qM_", [128, 4, 32], F32)
        qR = sb("qR_", [128, 4, 32], F32)
        qL = sb("qL_", [128, 4, 32], BF16)
        ps_tr = gstack.enter_context(nc.psum_tensor("ps_tr", [128, 2, 1024], BF16))

        sx = lambda m: stat[:, m:m + 1]
        ssq = lambda m: stat[:, TM + m:TM + m + 1]
        mu = lambda m: stat[:, 2 * TM + m:2 * TM + m + 1]
        rstd = lambda m: stat[:, 3 * TM + m:3 * TM + m + 1]

        def pe_t_bf(src_ap, dst_ap, slot, reads, writes, ev="a"):
            f = src_ap.shape[-1]
            p = src_ap.shape[0]
            pk = ("ps_tr", slot)
            S.op("t", lambda e: e.transpose(ps_tr[0:f, slot, 0:p], src_ap,
                                            ident_b[0:p, 0:p]),
                 reads=list(reads) + ["identb"], writes=[pk])
            if ev == "a":
                S.op("a", lambda e: e.activation(dst_ap, ps_tr[0:f, slot, 0:p],
                                                 AF.Copy),
                     reads=[pk], writes=list(writes))
            else:
                S.op("v", lambda e: e.tensor_copy(dst_ap, ps_tr[0:f, slot, 0:p]),
                     reads=[pk], writes=list(writes))

        def ln_stats(m):
            S.op("a", lambda e: e.activation(
                    qS.rearrange("p a b -> p (a b)")[:, 0:C], x_s[:, m, :],
                    AF.Square, accum_out=ssq(m)),
                 reads=[("x_s", m)],
                 writes=[("QS", 0, "S"), ("QS", 1, "S"), ("ssq", m)])
            S.op("a", lambda e: e.activation(
                    qS.rearrange("p a b -> p (a b)")[:, 0:C], x_s[:, m, :],
                    AF.Copy, accum_out=sx(m)),
                 reads=[("x_s", m)],
                 writes=[("QS", 0, "S"), ("QS", 1, "S"), ("sx", m)])

        def ln_finish():
            sxa = stat[:, 0:TM]
            ssqa = stat[:, TM:2 * TM]
            mua = stat[:, 2 * TM:3 * TM]
            rsa = stat[:, 3 * TM:4 * TM]
            rd = [("sx", m) for m in range(TM)] + [("ssq", m) for m in range(TM)]
            S.op("v", lambda e: e.tensor_scalar(mua, sxa, 1.0 / C, None, OP.mult),
                 reads=rd, writes=["mua"])
            S.op("v", lambda e: e.tensor_mul(rsa, mua, mua),
                 reads=["mua"], writes=["rsa"])
            S.op("v", lambda e: e.scalar_tensor_tensor(rsa, ssqa, 1.0 / C, rsa,
                                                       OP.mult, OP.subtract),
                 reads=rd + ["rsa"], writes=["rsa"])
            S.op("v", lambda e: e.tensor_scalar(rsa, rsa, 1e-5, None, OP.add),
                 reads=["rsa"], writes=["rsa"])
            S.op("a", lambda e: e.activation(rsa, rsa, AF.Sqrt),
                 reads=["rsa"], writes=["rsa"])
            S.op("v", lambda e: e.reciprocal(rsa, rsa),
                 reads=["rsa"], writes=["rsa"])

        def ln_norm(m, out_ap, out_key):
            S.op("v", lambda e: e.tensor_scalar(out_ap, x_s[:, m, :], mu(m),
                                                rstd(m), OP.subtract, OP.mult),
                 reads=[("x_s", m), "mua", "rsa"], writes=[out_key])

        def quant_steps(src_ap, out_ap, W, src_key, out_key, qs):
            B = W // 16
            src3 = src_ap.rearrange("p (b k) -> p b k", k=16)
            M = qM[:, qs, 0:B]
            R = qR[:, qs, 0:B]
            L = qL[:, qs, 0:B]
            Sf = qS[:, qs, 0:W]
            A = qA[:, qs, 0:W]
            Bm = qB[:, qs, 0:W]
            Cm = qC[:, qs, 0:W]
            K = lambda n: ("QS", qs, n)
            R_b = R.unsqueeze(2).broadcast_to([128, B, 16])
            L_b = L.unsqueeze(2).broadcast_to([128, B, 16])
            B3 = Bm.rearrange("p (b k) -> p b k", k=16)
            A3 = A.rearrange("p (b k) -> p b k", k=16)
            S3 = Sf.rearrange("p (b k) -> p b k", k=16)
            st = []
            st.append(("v", lambda e: e.tensor_reduce(M, src3, AX.X, OP.max,
                                                      apply_absolute_value=True),
                       [src_key], [K("M")]))
            st.append(("v", lambda e: e.tensor_scalar(M, M, 6e-12, None, OP.max),
                       [K("M")], [K("M")]))
            st.append(("v", lambda e: e.reciprocal(R, M), [K("M")], [K("R")]))
            st.append(("v", lambda e: e.tensor_scalar(R, R, 24.0, None, OP.mult),
                       [K("R")], [K("R")]))
            st.append(("v", lambda e: e.tensor_scalar(L, M, 1.0 / 12.0, None,
                                                      OP.mult),
                       [K("M")], [K("L")]))
            st.append(("v", lambda e: e.tensor_tensor(B3, src3, R_b, OP.mult),
                       [src_key, K("R")], [K("B")]))
            st.append(("a", lambda e: e.activation(Sf, Bm, AF.Square),
                       [K("B")], [K("S")]))
            st.append(("v", lambda e: e.tensor_scalar(A, Sf, 1.0, None, OP.is_gt),
                       [K("S")], [K("A")]))
            for thr in (9.0, 25.0, 49.0):
                st.append(("v", lambda e, th=thr: e.scalar_tensor_tensor(
                        A, Sf, th, A, OP.is_gt, OP.add), [K("S"), K("A")],
                        [K("A")]))
            st.append(("v", lambda e: e.tensor_scalar(Bm, Sf, 100.0, None,
                                                      OP.is_gt),
                       [K("S"), K("B")], [K("B")]))
            st.append(("v", lambda e: e.scalar_tensor_tensor(Bm, Sf, 196.0, Bm,
                                                             OP.is_gt, OP.add),
                       [K("S"), K("B")], [K("B")]))
            st.append(("a", lambda e: e.activation(Cm, Sf, AF.Sign,
                                                   bias=ccol[:, 4:5]),
                       [K("S"), "ccol"], [K("C")]))
            st.append(("v", lambda e: e.scalar_tensor_tensor(Bm, Cm, 1.0, Bm,
                                                             OP.add, OP.add),
                       [K("B"), K("C")], [K("B")]))
            st.append(("v", lambda e: e.scalar_tensor_tensor(A, Bm, 2.0, A,
                                                             OP.mult, OP.add),
                       [K("A"), K("B")], [K("A")]))
            st.append(("v", lambda e: e.tensor_tensor(S3, A3, L_b, OP.mult),
                       [K("A"), K("L")], [K("S")]))
            st.append(("a", lambda e: e.activation(Cm, src_ap, AF.Sign),
                       [src_key, K("C")], [K("C")]))
            st.append(("v", lambda e: e.tensor_mul(out_ap, Sf, Cm),
                       [K("S"), K("C")], [out_key]))
            return st

        def run_quant(jobs):
            steps = [quant_steps(*j) for j in jobs]
            n = max(len(s) for s in steps)
            for i in range(n):
                for s in steps:
                    if i < len(s):
                        ek, fn, rd, wr = s[i]
                        S.op(ek, fn, reads=rd, writes=wr)

        st1 = contextlib.ExitStack()
        with st1:
            sb1 = lambda n, shp, dt: st1.enter_context(nc.sbuf_tensor(n, shp, dt))
            ps1 = lambda n, shp, dt: st1.enter_context(nc.psum_tensor(n, shp, dt))
            wiT = sb1("wiT", [128, CC, C], BF16)
            wrT = sb1("wrT", [128, CC, C], BF16)
            ident_f = sb1("ident_f", [128, 128], F32)
            ones_f = sb1("ones_f", [128, 128], F32)
            onesrow = sb1("onesrow", [1, 128], F32)
            half_t = sb1("half_t", [1, 1], F32)
            ps_trf = ps1("ps_trf", [128, 2, 512], F32)
            ps_a = ps1("ps_a", [128, 4, 512], F32)
            ps_c = ps_a[:, 0, :]
            PSA0 = ("ps_a", 0)

            st1t = contextlib.ExitStack()
            with st1t:
                sb1t = lambda n, shp, dt: st1t.enter_context(
                    nc.sbuf_tensor(n, shp, dt))
                w_raw = sb1t("w_raw", [128, CC // 2, C], F32)
                wbf = sb1t("wbf", [128, CC, C], BF16)
                birow = sb1t("birow", [1, C], F32)
                brrow = sb1t("brrow", [1, C], F32)
                trif = sb1t("trif", [128, 128], F32)
                xn_t = sb1t("xn_t", [128, 2, C], BF16)

                xr = x_in.rearrange("(m p) c -> p m c", p=128)
                for j in range(4):
                    S.dop("g", x_s[:, 4 * j:4 * (j + 1), :],
                          xr[:, 4 * j:4 * (j + 1), :],
                          writes=[("x_s", m) for m in range(4 * j, 4 * j + 4)])
                S.dop("s", birow[:, :], bi_in, writes=["birow"])
                S.dop("s", brrow[:, :], br_in, writes=["brrow"])
                S.dop("s", lamt[:, 0:1], lam_in, writes=["lam0"])
                S.op("g", lambda e: e.memset(ones_f[:, :], 1.0), writes=["ones_f"])
                S.op("g", lambda e: e.memset(onesrow[:, :], 1.0), writes=["onesrow"])
                S.op("g", lambda e: e.memset(half_t[:, :], 0.5), writes=["half"])
                S.op("g", lambda e: e.memset(ccol[:, 0:1], 0.5), writes=["ccol"])
                S.op("g", lambda e: e.memset(ccol[:, 1:2], -1.0), writes=["ccol"])
                S.op("g", lambda e: e.memset(ccol[:, 2:3], 1.0), writes=["ccol"])
                S.op("g", lambda e: e.memset(ccol[:, 3:4], 0.125), writes=["ccol"])
                S.op("g", lambda e: e.memset(ccol[:, 4:5], -400.0), writes=["ccol"])
                S.op("g", lambda e: e.affine_select(ident_f[:, :], ones_f[:, :],
                                                    [[1, 128]], OP.is_equal, 0.0,
                                                    base=0, channel_multiplier=-1),
                     reads=["ones_f"], writes=["identf"])
                S.op("v", lambda e: e.tensor_copy(ident_b[:, :], ident_f[:, :]),
                     reads=["identf"], writes=["identb"])
                S.op("g", lambda e: e.affine_select(trif[:, :], ones_f[:, :],
                                                    [[1, 128]], OP.is_ge, 0.0,
                                                    base=0, channel_multiplier=-1),
                     reads=["ones_f"], writes=["trif"])
                S.op("v", lambda e: e.tensor_copy(tri_b[:, :], trif[:, :]),
                     reads=["trif"], writes=["tri"])
                S.op("a", lambda e: e.activation(lamt[:, 1:2], lamt[:, 0:1], AF.Exp,
                                                 scale=ccol[0:1, 1:2]),
                     reads=["lam0", "ccol"], writes=["lam1"])
                S.op("a", lambda e: e.activation(lamt[:, 2:3], lamt[:, 1:2], AF.Ln,
                                                 bias=ccol[0:1, 2:3]),
                     reads=["lam1", "ccol"], writes=["lam2"])
                S.op("v", lambda e: e.tensor_add(lamt[:, 3:4], lamt[:, 0:1],
                                                 lamt[:, 2:3]),
                     reads=["lam0", "lam2"], writes=["lam3"])
                S.op("v", lambda e: e.tensor_scalar(lamt[:, 4:5], lamt[:, 3:4],
                                                    -4.0, None, OP.mult),
                     reads=["lam3"], writes=["lam4"])
                S.op("v", lambda e: e.tensor_scalar(lamt[:, 5:6], lamt[:, 3:4],
                                                    -8.0, None, OP.mult),
                     reads=["lam3"], writes=["lam5"])
                S.op("t", lambda e: e.matmul(ps_c[:, 0:2], onesrow[:, :],
                                             lamt[:, 4:6], start=True, stop=True),
                     reads=["onesrow", "lam4", "lam5"], writes=[PSA0])
                S.op("v", lambda e: e.tensor_copy(cvec[:, :], ps_c[:, 0:2]),
                     reads=[PSA0], writes=["cvec"])
                for k in range(CC):
                    S.op("t", lambda e, k=k: e.matmul(
                            ps_c[:, 8 + k:9 + k],
                            birow[0:1, 128 * k:128 * (k + 1)], half_t[:, :],
                            start=True, stop=True),
                         reads=["birow", "half"], writes=[PSA0])
                S.op("v", lambda e: e.tensor_copy(bih[:, :], ps_c[:, 8:8 + CC]),
                     reads=[PSA0], writes=["bih"])
                for k in range(CC):
                    S.op("t", lambda e, k=k: e.matmul(
                            ps_c[:, 8 + k:9 + k],
                            brrow[0:1, 128 * k:128 * (k + 1)], half_t[:, :],
                            start=True, stop=True),
                         reads=["brrow", "half"], writes=[PSA0])
                S.op("v", lambda e: e.tensor_copy(brh[:, :], ps_c[:, 8:8 + CC]),
                     reads=[PSA0], writes=["brh"])

                for m in range(TM):
                    ln_stats(m)
                ln_finish()
                for m in range(TM):
                    sl = m % 2
                    ln_norm(m, xn_t[:, sl, :], ("xn_t", sl))
                    if dump_stage == 10:
                        S.op("v", lambda e, sl=sl: e.tensor_copy(
                                xsq[:, :], xn_t[:, sl, :]),
                             reads=[("xn_t", sl)], writes=["xsq"])
                        S.dop("s", out_hbm.rearrange("(m p) c -> p m c", p=128)
                              [:, m, :], xsq[:, :], reads=["xsq"])
                    for k in range(CC):
                        pe_t_bf(xn_t[:, sl, 128 * k:128 * (k + 1)],
                                xnT[:, k, 128 * m:128 * (m + 1)], k % 2,
                                reads=[("xn_t", sl)], writes=[("xnT", k, m)])
                for (w_src, dstT, dk) in ((wi_in, wiT, "wiT"),
                                          (wr_in, wrT, "wrT")):
                    wsr = w_src.rearrange("(d p) c -> p d c", p=128)
                    for j in range(2):
                        S.dop("s", w_raw[:, :, :], wsr[:, 4 * j:4 * (j + 1), :],
                              writes=["w_raw"])
                        S.op("g", lambda e, j=j: e.tensor_copy(
                                wbf[:, 4 * j:4 * (j + 1), :], w_raw[:, :, :]),
                             reads=["w_raw"], writes=[("wbf", j)])
                    for dcj in range(CC):
                        for k in range(CC):
                            pe_t_bf(wbf[:, dcj, 128 * k:128 * (k + 1)],
                                    dstT[:, k, 128 * dcj:128 * (dcj + 1)], k % 2,
                                    reads=[("wbf", dcj // 4)],
                                    writes=[(dk, k, dcj)])
                S.barrier()

            st1b = contextlib.ExitStack()
            with st1b:
                at_bufs = st1b.enter_context(
                    nc.sbuf_tensor("atb", [128, 5, T], F32))
                wraw = st1b.enter_context(nc.sbuf_tensor("wraw", [128, 3, C], F32))
                wct = [0]
                wqb = st1b.enter_context(nc.sbuf_tensor("wqb", [128, 2, C], BF16))
                stw = st1b.enter_context(
                    nc.sbuf_tensor("stw", [128, 2, 8, 128], BF16))
                AB = lambda i: at_bufs[:, i, :]

                def wq_chunk_gate(wsrc, dst, i):
                    jobs = []
                    for ii in (i, i + 1):
                        if ii >= IC:
                            break
                        sl = ii % 2
                        wsl = wct[0] % 3
                        wct[0] += 1
                        S.dop("s", wraw[:, wsl, :],
                              wsrc[128 * ii:128 * (ii + 1), :],
                              writes=[("wraw", wsl)])
                        jobs.append((wraw[:, wsl, 0:512], wqb[:, sl, 0:512], 512,
                                     ("wraw", wsl), ("wqb", sl, 0), 2 * sl))
                        jobs.append((wraw[:, wsl, 512:1024],
                                     wqb[:, sl, 512:1024], 512,
                                     ("wraw", wsl), ("wqb", sl, 1), 2 * sl + 1))
                    run_quant(jobs)
                    for ii in (i, i + 1):
                        if ii >= IC:
                            break
                        sl = ii % 2
                        for k in range(CC):
                            pe_t_bf(wqb[:, sl, 128 * k:128 * (k + 1)],
                                    stw[:, sl, k, :], k % 2,
                                    reads=[("wqb", sl, k // 4)],
                                    writes=[("stw", sl)])
                        S.dop("s",
                              dst.rearrange("k p i -> p k i")
                              [:, :, 128 * ii:128 * (ii + 1)],
                              stw[:, sl, :, :], reads=[("stw", sl)])

                def wq_chunk_down(j, pieces_grp):
                    jobs = []
                    for (p0, pw) in pieces_grp:
                        sl = (p0 // 1024) % 2
                        wsl = wct[0] % 3
                        wct[0] += 1
                        S.dop("s", wraw[:, wsl, 0:pw],
                              wd_in[128 * j:128 * (j + 1), p0:p0 + pw],
                              writes=[("wraw", wsl)])
                        w0 = 512
                        jobs.append((wraw[:, wsl, 0:w0], wqb[:, sl, 0:w0], w0,
                                     ("wraw", wsl), ("wqb", sl, 0), 2 * sl))
                        jobs.append((wraw[:, wsl, w0:pw], wqb[:, sl, w0:pw],
                                     pw - w0, ("wraw", wsl), ("wqb", sl, 1),
                                     2 * sl + 1))
                    run_quant(jobs)
                    for (p0, pw) in pieces_grp:
                        sl = (p0 // 1024) % 2
                        nt = pw // 128
                        for kk in range(nt):
                            pe_t_bf(wqb[:, sl, 128 * kk:128 * (kk + 1)],
                                    stw[:, sl, kk, :], kk % 2,
                                    reads=[("wqb", sl, 0 if 128 * kk < 512
                                            else 1)],
                                    writes=[("stw", sl)])
                        k0 = p0 // 128
                        S.dop("s",
                              wqt_d.rearrange("k p c -> p k c")
                              [:, k0:k0 + nt, 128 * j:128 * (j + 1)],
                              stw[:, sl, 0:nt, :], reads=[("stw", sl)])

                wq_jobs = []
                for (wsrc, dst) in ((wg_in, wqt_g), (wu_in, wqt_u)):
                    for i in range(0, IC, 2):
                        wq_jobs.append(lambda wsrc=wsrc, dst=dst, i=i:
                                       wq_chunk_gate(wsrc, dst, i))
                for j in range(CC):
                    wq_jobs.append(lambda j=j:
                                   wq_chunk_down(j, [(0, 1024), (1024, 1024)]))
                    wq_jobs.append(lambda j=j:
                                   wq_chunk_down(j, [(2048, 768)]))
                njobs = len(wq_jobs)
                ji = [0]

                def drain_wq(n):
                    while ji[0] < njobs and n > 0:
                        wq_jobs[ji[0]]()
                        ji[0] += 1
                        n -= 1

                for d in range(CC):
                    for (obuf, wT, wkeys, bcol, bkey) in (
                            (0, wiT, "wiT", bih, "bih"),
                            (1, wrT, "wrT", brh, "brh")):
                        for t4 in range(4):
                            slot = 2 * (t4 % 2) + obuf
                            pk = ("ps_a", slot)
                            for k in range(CC):
                                S.op("t", lambda e, k=k, t4=t4, wT=wT, d=d,
                                     slot=slot: e.matmul(
                                        ps_a[:, slot, :],
                                        wT[:, k, 128 * d:128 * (d + 1)],
                                        xnT[:, k, 512 * t4:512 * (t4 + 1)],
                                        start=(k == 0), stop=(k == CC - 1)),
                                     writes=[pk])
                            S.op("a", lambda e, t4=t4, d=d, bcol=bcol, obuf=obuf,
                                 slot=slot: e.activation(
                                    AB(obuf)[:, 512 * t4:512 * (t4 + 1)],
                                    ps_a[:, slot, :], AF.Tanh,
                                    scale=ccol[:, 0:1], bias=bcol[:, d:d + 1]),
                                 reads=[pk, bkey, "ccol"], writes=[("ab", obuf)])
                    S.op("a", lambda e: e.activation(AB(2)[:, :], AB(1)[:, :],
                                                     AF.Exp, scale=cvec[:, 0:1],
                                                     bias=cvec[:, 0:1]),
                         reads=[("ab", 1), "cvec"], writes=[("ab", 2)])
                    S.op("a", lambda e: e.activation(AB(3)[:, :], AB(1)[:, :],
                                                     AF.Exp, scale=cvec[:, 1:2],
                                                     bias=cvec[:, 1:2]),
                         reads=[("ab", 1), "cvec"], writes=[("ab", 3)])
                    S.op("a", lambda e: e.activation(AB(1)[:, :], AB(3)[:, :],
                                                     AF.Sqrt, scale=ccol[:, 1:2],
                                                     bias=ccol[:, 2:3]),
                         reads=[("ab", 3), "ccol"], writes=[("ab", 1)])
                    S.op("v", lambda e, d=d: e.scalar_tensor_tensor(
                            AB(4)[:, :], AB(0)[:, :], 1.0, xnT[:, d, :], OP.add,
                            OP.mult),
                         reads=[("ab", 0)], writes=[("ab", 4)])
                    S.op("v", lambda e: e.scalar_tensor_tensor(
                            AB(4)[:, :], AB(4)[:, :], 0.5, AB(1)[:, :], OP.mult,
                            OP.mult),
                         reads=[("ab", 4), ("ab", 1)], writes=[("ab", 4)])
                    S.op("v", lambda e: e.tensor_tensor_scan(
                            AB(3)[:, :], AB(2)[:, :], AB(4)[:, :], 0.0, OP.mult,
                            OP.add),
                         reads=[("ab", 2), ("ab", 4)], writes=[("ab", 3)])
                    if dump_stage in (11, 12, 13, 14, 15):
                        buf = {11: 0, 12: 1, 13: 2, 14: 4, 15: 3}[dump_stage]
                        dv = out_hbm.rearrange("(d p) c -> d p c", p=128)
                        S.dop("s", dv[2 * d, :, :], AB(buf)[:, 0:1024],
                              reads=[("ab", buf)])
                        S.dop("s", dv[2 * d + 1, :, :], AB(buf)[:, 1024:2048],
                              reads=[("ab", buf)])
                    for m in range(TM):
                        slot = m % 2
                        pk = ("ps_trf", slot)
                        S.op("t", lambda e, m=m, slot=slot: e.transpose(
                                ps_trf[:, slot, 0:128],
                                AB(3)[:, 128 * m:128 * (m + 1)], ident_f[:, :]),
                             reads=[("ab", 3), "identf"], writes=[pk])
                        S.op("v", lambda e, m=m, d=d, slot=slot: e.tensor_add(
                                x_s[:, m, 128 * d:128 * (d + 1)],
                                x_s[:, m, 128 * d:128 * (d + 1)],
                                ps_trf[:, slot, 0:128]),
                             reads=[pk, ("x_s", m)], writes=[("x_s", m)])
                    drain_wq(6)
                drain_wq(njobs)
        S.barrier()

        def ffn_phase(tag):
            st2 = contextlib.ExitStack()
            with st2:
                sb2 = lambda n, shp, dt: st2.enter_context(
                    nc.sbuf_tensor(n + tag, shp, dt))
                ps2 = lambda n, shp, dt: st2.enter_context(
                    nc.psum_tensor(n + tag, shp, dt))
                xn2 = sb2("xn2", [128, 2, C], F32)
                xq_t = sb2("xq_t", [128, 2, C], BF16)
                wqdT = sb2("wqdT", [128, IC, C], BF16)
                wcol = sb2("wcol", [128, 2, CC, 512], BF16)
                sg = sb2("sg", [128, 2, 512], F32)
                hsrc = sb2("hsrc", [128, 2, 512], F32)
                hqq = sb2("hqq", [128, 2, 512], BF16)
                hqT = sb2("hqT", [128, IC, 256], BF16)
                ps_g = ps2("ps_g", [128, 2, 512], F32)
                ps_d = ps2("ps_d", [128, 2, 512], F32)
                wqdr = wqt_d.rearrange("k p c -> p k c")
                for j in range(4):
                    k0, k1 = 6 * j, min(6 * (j + 1), IC)
                    S.dop("s", wqdT[:, k0:k1, :], wqdr[:, k0:k1, :],
                          writes=["wqdT"])
                for m in range(TM):
                    ln_stats(m)
                ln_finish()
                for m0 in range(0, TM, 2):
                    jobs = []
                    for m in (m0, m0 + 1):
                        sl = m % 2
                        ln_norm(m, xn2[:, sl, :], ("xn2", sl))
                        jobs.append((xn2[:, sl, 0:512], xq_t[:, sl, 0:512], 512,
                                     ("xn2", sl), ("xq", sl, 0), 2 * sl))
                        jobs.append((xn2[:, sl, 512:1024],
                                     xq_t[:, sl, 512:1024], 512,
                                     ("xn2", sl), ("xq", sl, 1), 2 * sl + 1))
                    run_quant(jobs)
                    for m in (m0, m0 + 1):
                        sl = m % 2
                        for k in range(CC):
                            pe_t_bf(xq_t[:, sl, 128 * k:128 * (k + 1)],
                                    xnT[:, k, 128 * m:128 * (m + 1)], k % 2,
                                    reads=[("xq", sl, k // 4)],
                                    writes=[("xqT", k, m)])
                icols = [(512 * n, min(512, I - 512 * n))
                         for n in range((I + 511) // 512)]
                wgr = wqt_g.rearrange("k p i -> p k i")
                wur = wqt_u.rearrange("k p i -> p k i")
                def down_mg(mg):
                    for mm in range(2):
                        m = 2 * mg + mm
                        for n2 in range(2):
                            pk = ("ps_d", n2)
                            hts = [("hqT", k2, mm) for k2 in range(IC)]
                            for k in range(IC):
                                S.op("t", lambda e, k=k, mm=mm, n2=n2:
                                     e.matmul(ps_d[:, n2, :],
                                              hqT[:, k, 128 * mm:128 * (mm + 1)],
                                              wqdT[:, k, 512 * n2:512 * (n2 + 1)],
                                              start=(k == 0), stop=(k == IC - 1)),
                                     reads=hts + ["wqdT"], writes=[pk])
                            S.op("v", lambda e, m=m, n2=n2: e.tensor_add(
                                    x_s[:, m, 512 * n2:512 * (n2 + 1)],
                                    x_s[:, m, 512 * n2:512 * (n2 + 1)],
                                    ps_d[:, n2, :]),
                                 reads=[pk, ("x_s", m)], writes=[("x_s", m)])

                sched = []
                for mg in range(8):
                    for (c0, cw) in icols:
                        sched.append(("g", mg, c0, cw))
                        sched.append(("u", mg, c0, cw))

                def load_col(idx, slot):
                    mat, _, c0, cw = sched[idx]
                    srcr = wgr if mat == "g" else wur
                    S.dop("s", wcol[:, slot, :, 0:cw], srcr[:, :, c0:c0 + cw],
                          writes=[("wcol", slot)])

                load_col(0, 0)
                pending = []
                for idx, (mat, mg, c0, cw) in enumerate(sched):
                    slot = idx % 2
                    if idx + 1 < len(sched):
                        load_col(idx + 1, 1 - slot)
                    if mat == "g":
                        for mm in range(2):
                            m = 2 * mg + mm
                            for k in range(CC):
                                S.op("t", lambda e, k=k, m=m, mm=mm, cw=cw,
                                     ws=slot: e.matmul(
                                        ps_g[:, mm, 0:cw],
                                        xnT[:, k, 128 * m:128 * (m + 1)],
                                        wcol[:, ws, k, 0:cw],
                                        start=(k == 0), stop=(k == CC - 1)),
                                     reads=[("wcol", slot), ("xqT", k, m)],
                                     writes=[("ps_g", mm)])
                            S.op("a", lambda e, mm=mm, cw=cw: e.activation(
                                    sg[:, mm, 0:cw], ps_g[:, mm, 0:cw], AF.Silu),
                                 reads=[("ps_g", mm)], writes=[("sg", mm)])
                    else:
                        for mm in range(2):
                            m = 2 * mg + mm
                            for k in range(CC):
                                S.op("t", lambda e, k=k, m=m, mm=mm, cw=cw,
                                     ws=slot: e.matmul(
                                        ps_g[:, mm, 0:cw],
                                        xnT[:, k, 128 * m:128 * (m + 1)],
                                        wcol[:, ws, k, 0:cw],
                                        start=(k == 0), stop=(k == CC - 1)),
                                     reads=[("wcol", slot), ("xqT", k, m),
                                            ("sg", mm)],
                                     writes=[("ps_g", mm)])
                        hb = (c0 // 512) % 2
                        for mm in range(2):
                            S.op("v", lambda e, mm=mm, cw=cw:
                                 e.tensor_mul(hsrc[:, mm, 0:cw], sg[:, mm, 0:cw],
                                              ps_g[:, mm, 0:cw]),
                                 reads=[("sg", mm), ("ps_g", mm)],
                                 writes=[("hs", mm)])
                        def qjob(c0=c0, cw=cw, mg=mg, hb=hb):
                            qb0 = hb * 2
                            run_quant([
                                (hsrc[:, 0, 0:cw], hqq[:, 0, 0:cw], cw,
                                 ("hs", 0), ("hqq", 0), qb0),
                                (hsrc[:, 1, 0:cw], hqq[:, 1, 0:cw], cw,
                                 ("hs", 1), ("hqq", 1), qb0 + 1)])
                            for q2 in range(2):
                                for kk in range(cw // 128):
                                    ic = (c0 // 128) + kk
                                    pe_t_bf(hqq[:, q2, 128 * kk:128 * (kk + 1)],
                                            hqT[:, ic, 128 * q2:128 * (q2 + 1)],
                                            kk % 2, reads=[("hqq", q2)],
                                            writes=[("hqT", ic, q2)])
                        pending.append(qjob)
                    if pending and (mat == "g" or (c0, cw) == icols[-1]):
                        pending.pop(0)()
                    if mat == "u" and (c0, cw) == icols[-1]:
                        while pending:
                            pending.pop(0)()
                        down_mg(mg)
            S.barrier()

        ffn_phase("F1")
        if dump_stage == 2:
            for j in range(4):
                S.dop("s", out_hbm.rearrange("(m p) c -> p m c", p=128)
                      [:, 4 * j:4 * (j + 1), :], x_s[:, 4 * j:4 * (j + 1), :])

        st3 = contextlib.ExitStack()
        with st3:
            sb3 = lambda n, shp, dt: st3.enter_context(nc.sbuf_tensor(n, shp, dt))
            ps3 = lambda n, shp, dt: st3.enter_context(nc.psum_tensor(n, shp, dt))
            xn3 = sb3("xn3", [128, 2, C], BF16)
            wqkv_raw = sb3("wqkv_raw", [128, 2, C], F32)
            qkv_bf = sb3("qkv_bf", [128, 2, C], BF16)
            wqkvT = sb3("wqkvT", [128, CC, 192], BF16)
            wout_raw = sb3("wout_raw", [128, CC, 64], F32)
            wout_bf = sb3("wout_bf", [128, CC, 64], BF16)
            woutT = sb3("woutT", [64, C], BF16)
            qkv_sb = sb3("qkv_sb", [128, TM, 192], F32)
            q_bf = sb3("q_bf", [128, 64], BF16)
            qT_sb = sb3("qT_sb", [64, T], BF16)
            vo16 = sb3("vo16", [128, TM, 128], BF16)
            qb = sb3("qb", [128, 4, T], BF16)
            Pb = sb3("Pb", [128, 4, T], BF16)
            nd = sb3("nd", [128, T], F32)
            ohm = qA.rearrange("p a b -> p (a b)")[0:64, 0:T]
            rec0 = qB.rearrange("p a b -> p (a b)")[0:64, 0:T]
            ps_n = ps3("ps_n", [128, T], F32)

            for m in range(TM):
                ln_stats(m)
            ln_finish()
            for m in range(TM):
                sl = m % 2
                ln_norm(m, xn3[:, sl, :], ("xn3", sl))
                for k in range(CC):
                    pe_t_bf(xn3[:, sl, 128 * k:128 * (k + 1)],
                            xnT[:, k, 128 * m:128 * (m + 1)], k % 2,
                            reads=[("xn3", sl)], writes=[("xnT", k, m)])
            S.dop("s", wqkv_raw[:, 0, :], wqkv_in[0:128, :], writes=["qkvr"])
            S.dop("s", wqkv_raw[0:64, 1, :], wqkv_in[128:192, :], writes=["qkvr"])
            S.op("g", lambda e: e.tensor_copy(qkv_bf[:, :, :], wqkv_raw[:, :, :]),
                 reads=["qkvr"], writes=["qkv_bf"])
            for k in range(CC):
                pe_t_bf(qkv_bf[:, 0, 128 * k:128 * (k + 1)], wqkvT[:, k, 0:128],
                        k % 2, reads=["qkv_bf"], writes=[("wqkvT", k)])
                pe_t_bf(qkv_bf[0:64, 1, 128 * k:128 * (k + 1)],
                        wqkvT[:, k, 128:192], k % 2, reads=["qkv_bf"],
                        writes=[("wqkvT", k)])
            S.dop("s", wout_raw[:, :, :],
                  wout_in.rearrange("(cc p) h -> p cc h", p=128),
                  writes=["wout_raw"])
            S.op("g", lambda e: e.tensor_copy(wout_bf[:, :, :], wout_raw[:, :, :]),
                 reads=["wout_raw"], writes=["wout_bf"])
            for k in range(CC):
                pe_t_bf(wout_bf[:, k, 0:64], woutT[0:64, 128 * k:128 * (k + 1)],
                        k % 2, reads=["wout_bf"], writes=["woutT"])
            for m in range(TM):
                pkk = "ps_n"
                for k in range(CC):
                    S.op("t", lambda e, k=k, m=m: e.matmul(
                            ps_n[:, 0:192], xnT[:, k, 128 * m:128 * (m + 1)],
                            wqkvT[:, k, :], start=(k == 0), stop=(k == CC - 1)),
                         reads=[("xnT", k, m)] + [("wqkvT", kk) for kk in
                                                  range(CC)],
                         writes=[pkk])
                S.op("a", lambda e, m=m: e.activation(qkv_sb[:, m, 0:64],
                                                      ps_n[:, 0:64], AF.Copy),
                     reads=[pkk], writes=[("qkv", m)])
                S.op("a", lambda e, m=m: e.activation(qkv_sb[:, m, 64:128],
                                                      ps_n[:, 64:128], AF.Copy,
                                                      scale=ccol[:, 3:4]),
                     reads=[pkk, "ccol"], writes=[("qkv", m)])
                S.op("a", lambda e, m=m: e.activation(qkv_sb[:, m, 128:192],
                                                      ps_n[:, 128:192], AF.Copy),
                     reads=[pkk], writes=[("qkv", m)])
                S.op("v", lambda e, m=m: e.tensor_copy(q_bf[:, :],
                                                       qkv_sb[:, m, 0:64]),
                     reads=[("qkv", m)], writes=["q_bf"])
                pe_t_bf(q_bf[:, :], qT_sb[0:64, 128 * m:128 * (m + 1)], m % 2,
                        reads=["q_bf"], writes=["qT"])
            S.dop("s", qT_dram, qT_sb[0:64, :], reads=["qT"], writes=["qTd"])
            S.op("g", lambda e: e.memset(vo16[:, :, :], 0.0), writes=["vo"])
            def load_qb(h):
                S.dop("s", qb[:, h % 4, :],
                      qT_dram[h:h + 1, :].partition_broadcast(128),
                      reads=["qTd"], writes=[("qb", h % 4)])

            for h in range(3):
                load_qb(h)
            for h in range(NH):
                if h + 3 < NH:
                    load_qb(h + 3)
                if h > 0:
                    S.op("g", lambda e, h=h: e.memset(vo16[:, :, h - 1:h], 0.0),
                         writes=["vo"])
                    S.op("g", lambda e, h=h: e.memset(
                            vo16[:, :, 63 + h:64 + h], 0.0), writes=["vo"])
                S.op("v", lambda e, h=h: e.tensor_copy(
                        vo16[:, :, h:h + 1], qkv_sb[:, :, 128 + h:129 + h]),
                     reads=[("qkv", m2) for m2 in range(TM)], writes=["vo"])
                S.op("g", lambda e, h=h: e.memset(vo16[:, :, 64 + h:65 + h], 1.0),
                     writes=["vo"])
                qsl = h % 4
                for b in range(TM):
                    ext = T - 128 * b
                    psl = (h * TM + b) % 4
                    S.op("a", lambda e, h=h, b=b, psl=psl, qsl=qsl, ext=ext:
                         e.activation(Pb[:, psl, 0:ext], qb[:, qsl, 128 * b:T],
                                      AF.Exp,
                                      scale=qkv_sb[:, b, 64 + h:65 + h]),
                         reads=[("qb", qsl), ("qkv", b)], writes=[("Pb", psl)])
                    S.op("v", lambda e, psl=psl: e.tensor_mul(
                            Pb[:, psl, 0:128], Pb[:, psl, 0:128], tri_b[:, :]),
                         reads=[("Pb", psl), "tri"], writes=[("Pb", psl)])
                    i0 = 128 * b
                    while i0 < T:
                        i1 = min((i0 // 512 + 1) * 512, T)
                        S.op("t", lambda e, b=b, psl=psl, i0=i0, i1=i1, h=h:
                             e.matmul(ps_n[:, i0:i1], vo16[:, b, :],
                                      Pb[:, psl, i0 - 128 * b:i1 - 128 * b],
                                      start=(h == 0 and b == 0),
                                      stop=(h == NH - 1)),
                             reads=["vo", ("Pb", psl)], writes=["ps_n"])
                        i0 = i1
            S.op("v", lambda e: e.tensor_copy(nd[:, :], ps_n[:, :]),
                 reads=["ps_n"], writes=["nd"])
            def _recip(e):
                with nc.allow_low_precision(
                        reason="softmax denom reciprocal stored bf16; "
                               "~0.4% on a small additive term"):
                    return e.reciprocal(rec0, nd[64:128, :])
            S.op("v", _recip, reads=["nd"], writes=["rec0"])
            S.op("v", lambda e: e.tensor_mul(ohm, nd[0:64, :], rec0),
                 reads=["nd", "rec0"], writes=["ohm"])
            for m in range(TM):
                for n2 in range(2):
                    S.op("t", lambda e, m=m, n2=n2: e.matmul(
                            ps_n[:, 512 * n2:512 * (n2 + 1)],
                            ohm[:, 128 * m:128 * (m + 1)],
                            woutT[0:64, 512 * n2:512 * (n2 + 1)],
                            start=True, stop=True),
                         reads=["ohm", "woutT"], writes=["ps_n"])
                    S.op("v", lambda e, m=m, n2=n2: e.tensor_add(
                            x_s[:, m, 512 * n2:512 * (n2 + 1)],
                            x_s[:, m, 512 * n2:512 * (n2 + 1)],
                            ps_n[:, 512 * n2:512 * (n2 + 1)]),
                         reads=["ps_n", ("x_s", m)], writes=[("x_s", m)])
        S.barrier()
        if dump_stage == 3:
            for j in range(4):
                S.dop("s", out_hbm.rearrange("(m p) c -> p m c", p=128)
                      [:, 4 * j:4 * (j + 1), :], x_s[:, 4 * j:4 * (j + 1), :])

        ffn_phase("F2")

        if dump_stage is None:
            outr = out_hbm.rearrange("(m p) c -> p m c", p=128)
            for j in range(4):
                S.dop("s", outr[:, 4 * j:4 * (j + 1), :],
                      x_s[:, 4 * j:4 * (j + 1), :],
                      reads=[("x_s", m) for m in range(4 * j, 4 * j + 4)])
        S.eng["s"].wait_ge(S.sem["ds"], S.cnt["ds"])
        S.eng["g"].wait_ge(S.sem["dg"], S.cnt["dg"])
    return nc


_NC_CACHE = {}


def kernel(L0_attn_a_x, L0_attn_a_Wi, L0_attn_a_bi, L0_attn_a_Wr, L0_attn_a_br,
           L0_attn_a_cap_lambda, L0_attn_b_w_qkv, L0_attn_b_w_out,
           L0_ffn_w_gate, L0_ffn_w_up, L0_ffn_w_down):
    cc = np.ascontiguousarray
    x = cc(np.asarray(L0_attn_a_x, dtype=np.float32))
    wi = cc(np.asarray(L0_attn_a_Wi, dtype=np.float32))
    bi = cc(np.asarray(L0_attn_a_bi, dtype=np.float32)).reshape(1, C)
    wr = cc(np.asarray(L0_attn_a_Wr, dtype=np.float32))
    br = cc(np.asarray(L0_attn_a_br, dtype=np.float32)).reshape(1, C)
    lam = cc(np.asarray(L0_attn_a_cap_lambda, dtype=np.float32)).reshape(1, 1)
    wqkv = cc(np.asarray(L0_attn_b_w_qkv, dtype=np.float32))
    wout = cc(np.asarray(L0_attn_b_w_out, dtype=np.float32))
    wg = cc(np.asarray(L0_ffn_w_gate, dtype=np.float32))
    wu = cc(np.asarray(L0_ffn_w_up, dtype=np.float32))
    wd = cc(np.asarray(L0_ffn_w_down, dtype=np.float32))

    if "nc" not in _NC_CACHE:
        _NC_CACHE["nc"] = build_nc()
    nc = _NC_CACHE["nc"]
    in_maps = []
    for b in range(NCORES):
        in_maps.append({"x": cc(x[b]), "wi": wi, "bi": bi, "wr": wr, "br": br,
                        "lam": lam, "wqkv": wqkv, "wout": wout, "wg": wg,
                        "wu": wu, "wd": wd})
    res = run_bass_kernel_spmd(nc, in_maps, core_ids=list(range(NCORES)))
    return np.stack([r["out"] for r in res.results], axis=0)


# version 12
# speedup vs baseline: 1.5306x; 1.0534x over previous
import contextlib
import numpy as np
import concourse.bass as bass
import concourse.mybir as mybir
from concourse.bass_utils import run_bass_kernel_spmd

F32 = mybir.dt.float32
BF16 = mybir.dt.bfloat16
AF = mybir.ActivationFunctionType
OP = mybir.AluOpType
AX = mybir.AxisListType

T, C, I, NH = 2048, 1024, 2816, 64
TM, CC, IC = T // 128, C // 128, I // 128
NCORES = 8


class Sched:
    ENGS = ("v", "a", "t", "g", "s")

    def __init__(self, nc, stack):
        self.nc = nc
        self.eng = {"v": nc.vector, "a": nc.scalar, "t": nc.tensor,
                    "g": nc.gpsimd, "s": nc.sync}
        self.sem = {}
        self.cnt = {}
        self.wm = {k: {} for k in self.ENGS}
        for k in self.ENGS:
            self.sem[k] = stack.enter_context(nc.semaphore(f"S_{k}"))
            self.cnt[k] = 0
        for k in ("ds", "dg"):
            self.sem[k] = stack.enter_context(nc.semaphore(f"S_{k}"))
            self.cnt[k] = 0
        self.bar_sem = stack.enter_context(nc.semaphore("S_bar"))
        self.bar_n = 0
        self.bufs = {}

    def _wait(self, ek, deps, war=()):
        e = self.eng[ek]
        need = {}
        for d in deps:
            if d is None:
                continue
            sk, val = d
            if val > need.get(sk, 0):
                need[sk] = val
        for d in war:
            if d is None:
                continue
            sk, val = d
            if sk == ek:
                continue
            if val > need.get(sk, 0):
                need[sk] = val
        for sk in sorted(need):
            val = need[sk]
            if self.wm[ek].get(sk, -1) >= val:
                continue
            e.wait_ge(self.sem[sk], val)
            self.wm[ek][sk] = val

    def _slot(self, key):
        return self.bufs.setdefault(key, {"w": None, "r": []})

    def op(self, ek, fn, reads=(), writes=(), deps=()):
        dl = list(deps)
        war = []
        for k in reads:
            dl.append(self._slot(k)["w"])
        for k in writes:
            s = self._slot(k)
            war.extend(s["r"])
            war.append(s["w"])
        self._wait(ek, dl, war)
        inst = fn(self.eng[ek])
        self.cnt[ek] += 1
        inst.then_inc(self.sem[ek], 1)
        tok = (ek, self.cnt[ek])
        for k in reads:
            self._slot(k)["r"].append(tok)
        for k in writes:
            s = self._slot(k)
            s["w"] = tok
            s["r"] = []
        return tok

    def dop(self, qk, out_ap, in_ap, reads=(), writes=(), deps=(), **kw):
        dk = {"s": "ds", "g": "dg"}[qk]
        dl = list(deps)
        for k in reads:
            dl.append(self._slot(k)["w"])
        for k in writes:
            s = self._slot(k)
            dl.extend(s["r"])
            dl.append(s["w"])
        self._wait(qk, dl, ())
        self.eng[qk].dma_start(out_ap, in_ap, **kw).then_inc(self.sem[dk], 16)
        self.cnt[dk] += 16
        tok = (dk, self.cnt[dk])
        for k in reads:
            self._slot(k)["r"].append(tok)
        for k in writes:
            s = self._slot(k)
            s["w"] = tok
            s["r"] = []
        return tok

    def barrier(self):
        for qk, dk in (("s", "ds"), ("g", "dg")):
            if self.cnt[dk] > 0 and self.wm[qk].get(dk, -1) < self.cnt[dk]:
                self.eng[qk].wait_ge(self.sem[dk], self.cnt[dk])
                self.wm[qk][dk] = self.cnt[dk]
        self.bar_n += 1
        for ek in self.ENGS:
            self.eng[ek].nop().then_inc(self.bar_sem, 1)
        for ek in self.ENGS:
            self.eng[ek].wait_ge(self.bar_sem, 5 * self.bar_n)
        self.bufs = {}
        for ek in self.ENGS:
            self.wm[ek] = {}


def build_nc(dump_stage=None, defer_down=False, defer_q=True, pf_w=True):
    nc = bass.Bass("TRN2", target_bir_lowering=False)
    ap = lambda t: t.ap()
    x_in = ap(nc.dram_tensor("x", [T, C], F32, kind="ExternalInput"))
    wi_in = ap(nc.dram_tensor("wi", [C, C], F32, kind="ExternalInput"))
    bi_in = ap(nc.dram_tensor("bi", [1, C], F32, kind="ExternalInput"))
    wr_in = ap(nc.dram_tensor("wr", [C, C], F32, kind="ExternalInput"))
    br_in = ap(nc.dram_tensor("br", [1, C], F32, kind="ExternalInput"))
    lam_in = ap(nc.dram_tensor("lam", [1, 1], F32, kind="ExternalInput"))
    wqkv_in = ap(nc.dram_tensor("wqkv", [192, C], F32, kind="ExternalInput"))
    wout_in = ap(nc.dram_tensor("wout", [C, 64], F32, kind="ExternalInput"))
    wg_in = ap(nc.dram_tensor("wg", [I, C], F32, kind="ExternalInput"))
    wu_in = ap(nc.dram_tensor("wu", [I, C], F32, kind="ExternalInput"))
    wd_in = ap(nc.dram_tensor("wd", [C, I], F32, kind="ExternalInput"))
    out_hbm = ap(nc.dram_tensor("out", [T, C], F32, kind="ExternalOutput"))
    wqt_g = ap(nc.dram_tensor("wqt_g", [CC, 128, I], BF16, kind="Internal"))
    wqt_u = ap(nc.dram_tensor("wqt_u", [CC, 128, I], BF16, kind="Internal"))
    wqt_d = ap(nc.dram_tensor("wqt_d", [IC, 128, C], BF16, kind="Internal"))
    qT_dram = ap(nc.dram_tensor("qT_d", [64, T], BF16, kind="Internal"))

    gstack = contextlib.ExitStack()
    with gstack:
        sb = lambda n, shp, dt: gstack.enter_context(nc.sbuf_tensor(n, shp, dt))
        S = Sched(nc, gstack)

        x_s = sb("x_s", [128, TM, C], F32)
        xnT = sb("xnT", [128, CC, T], BF16)
        ident_b = sb("ident_b", [128, 128], BF16)
        tri_b = sb("tri_b", [128, 128], BF16)
        cvec = sb("cvec", [128, 2], F32)
        ccol = sb("ccol", [128, 5], F32)
        bih = sb("bih", [128, CC], F32)
        brh = sb("brh", [128, CC], F32)
        lamt = sb("lamt", [1, 8], F32)
        stat = sb("stat", [128, 4 * TM], F32)
        qS = sb("qS_", [128, 4, 512], BF16)
        qA = sb("qA_", [128, 4, 512], BF16)
        qB = sb("qB_", [128, 4, 512], BF16)
        qC = sb("qC_", [128, 4, 512], BF16)
        qM = sb("qM_", [128, 4, 32], F32)
        qR = sb("qR_", [128, 4, 32], F32)
        qL = sb("qL_", [128, 4, 32], BF16)
        ps_tr = gstack.enter_context(nc.psum_tensor("ps_tr", [128, 2, 1024], BF16))

        sx = lambda m: stat[:, m:m + 1]
        ssq = lambda m: stat[:, TM + m:TM + m + 1]
        mu = lambda m: stat[:, 2 * TM + m:2 * TM + m + 1]
        rstd = lambda m: stat[:, 3 * TM + m:3 * TM + m + 1]

        def pe_t_bf(src_ap, dst_ap, slot, reads, writes, ev="a"):
            f = src_ap.shape[-1]
            p = src_ap.shape[0]
            pk = ("ps_tr", slot)
            S.op("t", lambda e: e.transpose(ps_tr[0:f, slot, 0:p], src_ap,
                                            ident_b[0:p, 0:p]),
                 reads=list(reads) + ["identb"], writes=[pk])
            if ev == "a":
                S.op("a", lambda e: e.activation(dst_ap, ps_tr[0:f, slot, 0:p],
                                                 AF.Copy),
                     reads=[pk], writes=list(writes))
            else:
                S.op("v", lambda e: e.tensor_copy(dst_ap, ps_tr[0:f, slot, 0:p]),
                     reads=[pk], writes=list(writes))

        def ln_stats(m):
            S.op("a", lambda e: e.activation(
                    qS.rearrange("p a b -> p (a b)")[:, 0:C], x_s[:, m, :],
                    AF.Square, accum_out=ssq(m)),
                 reads=[("x_s", m)],
                 writes=[("QS", 0, "S"), ("QS", 1, "S"), ("ssq", m)])
            S.op("a", lambda e: e.activation(
                    qS.rearrange("p a b -> p (a b)")[:, 0:C], x_s[:, m, :],
                    AF.Copy, accum_out=sx(m)),
                 reads=[("x_s", m)],
                 writes=[("QS", 0, "S"), ("QS", 1, "S"), ("sx", m)])

        def ln_finish():
            sxa = stat[:, 0:TM]
            ssqa = stat[:, TM:2 * TM]
            mua = stat[:, 2 * TM:3 * TM]
            rsa = stat[:, 3 * TM:4 * TM]
            rd = [("sx", m) for m in range(TM)] + [("ssq", m) for m in range(TM)]
            S.op("v", lambda e: e.tensor_scalar(mua, sxa, 1.0 / C, None, OP.mult),
                 reads=rd, writes=["mua"])
            S.op("v", lambda e: e.tensor_mul(rsa, mua, mua),
                 reads=["mua"], writes=["rsa"])
            S.op("v", lambda e: e.scalar_tensor_tensor(rsa, ssqa, 1.0 / C, rsa,
                                                       OP.mult, OP.subtract),
                 reads=rd + ["rsa"], writes=["rsa"])
            S.op("v", lambda e: e.tensor_scalar(rsa, rsa, 1e-5, None, OP.add),
                 reads=["rsa"], writes=["rsa"])
            S.op("a", lambda e: e.activation(rsa, rsa, AF.Sqrt),
                 reads=["rsa"], writes=["rsa"])
            S.op("v", lambda e: e.reciprocal(rsa, rsa),
                 reads=["rsa"], writes=["rsa"])

        def ln_norm(m, out_ap, out_key):
            S.op("v", lambda e: e.tensor_scalar(out_ap, x_s[:, m, :], mu(m),
                                                rstd(m), OP.subtract, OP.mult),
                 reads=[("x_s", m), "mua", "rsa"], writes=[out_key])

        def quant_steps(src_ap, out_ap, W, src_key, out_key, qs):
            B = W // 16
            src3 = src_ap.rearrange("p (b k) -> p b k", k=16)
            M = qM[:, qs, 0:B]
            R = qR[:, qs, 0:B]
            L = qL[:, qs, 0:B]
            Sf = qS[:, qs, 0:W]
            A = qA[:, qs, 0:W]
            Bm = qB[:, qs, 0:W]
            Cm = qC[:, qs, 0:W]
            K = lambda n: ("QS", qs, n)
            R_b = R.unsqueeze(2).broadcast_to([128, B, 16])
            L_b = L.unsqueeze(2).broadcast_to([128, B, 16])
            B3 = Bm.rearrange("p (b k) -> p b k", k=16)
            A3 = A.rearrange("p (b k) -> p b k", k=16)
            S3 = Sf.rearrange("p (b k) -> p b k", k=16)
            st = []
            st.append(("v", lambda e: e.tensor_reduce(M, src3, AX.X, OP.max,
                                                      apply_absolute_value=True),
                       [src_key], [K("M")]))
            st.append(("v", lambda e: e.tensor_scalar(M, M, 6e-12, None, OP.max),
                       [K("M")], [K("M")]))
            st.append(("v", lambda e: e.reciprocal(R, M), [K("M")], [K("R")]))
            st.append(("v", lambda e: e.tensor_scalar(R, R, 24.0, None, OP.mult),
                       [K("R")], [K("R")]))
            st.append(("v", lambda e: e.tensor_scalar(L, M, 1.0 / 12.0, None,
                                                      OP.mult),
                       [K("M")], [K("L")]))
            st.append(("v", lambda e: e.tensor_tensor(B3, src3, R_b, OP.mult),
                       [src_key, K("R")], [K("B")]))
            st.append(("a", lambda e: e.activation(Sf, Bm, AF.Square),
                       [K("B")], [K("S")]))
            st.append(("v", lambda e: e.tensor_scalar(A, Sf, 1.0, None, OP.is_gt),
                       [K("S")], [K("A")]))
            for thr in (9.0, 25.0, 49.0):
                st.append(("v", lambda e, th=thr: e.scalar_tensor_tensor(
                        A, Sf, th, A, OP.is_gt, OP.add), [K("S"), K("A")],
                        [K("A")]))
            st.append(("v", lambda e: e.tensor_scalar(Bm, Sf, 100.0, None,
                                                      OP.is_gt),
                       [K("S"), K("B")], [K("B")]))
            st.append(("v", lambda e: e.scalar_tensor_tensor(Bm, Sf, 196.0, Bm,
                                                             OP.is_gt, OP.add),
                       [K("S"), K("B")], [K("B")]))
            st.append(("a", lambda e: e.activation(Cm, Sf, AF.Sign,
                                                   bias=ccol[:, 4:5]),
                       [K("S"), "ccol"], [K("C")]))
            st.append(("v", lambda e: e.scalar_tensor_tensor(Bm, Cm, 1.0, Bm,
                                                             OP.add, OP.add),
                       [K("B"), K("C")], [K("B")]))
            st.append(("v", lambda e: e.scalar_tensor_tensor(A, Bm, 2.0, A,
                                                             OP.mult, OP.add),
                       [K("A"), K("B")], [K("A")]))
            st.append(("v", lambda e: e.tensor_tensor(S3, A3, L_b, OP.mult),
                       [K("A"), K("L")], [K("S")]))
            st.append(("a", lambda e: e.activation(Cm, src_ap, AF.Sign),
                       [src_key, K("C")], [K("C")]))
            st.append(("v", lambda e: e.tensor_mul(out_ap, Sf, Cm),
                       [K("S"), K("C")], [out_key]))
            return st

        def run_quant(jobs):
            steps = [quant_steps(*j) for j in jobs]
            n = max(len(s) for s in steps)
            for i in range(n):
                for s in steps:
                    if i < len(s):
                        ek, fn, rd, wr = s[i]
                        S.op(ek, fn, reads=rd, writes=wr)

        st1 = contextlib.ExitStack()
        with st1:
            sb1 = lambda n, shp, dt: st1.enter_context(nc.sbuf_tensor(n, shp, dt))
            ps1 = lambda n, shp, dt: st1.enter_context(nc.psum_tensor(n, shp, dt))
            wiT = sb1("wiT", [128, CC, C], BF16)
            wrT = sb1("wrT", [128, CC, C], BF16)
            ident_f = sb1("ident_f", [128, 128], F32)
            ones_f = sb1("ones_f", [128, 128], F32)
            onesrow = sb1("onesrow", [1, 128], F32)
            half_t = sb1("half_t", [1, 1], F32)
            ps_trf = ps1("ps_trf", [128, 2, 512], F32)
            ps_a = ps1("ps_a", [128, 4, 512], F32)
            ps_c = ps_a[:, 0, :]
            PSA0 = ("ps_a", 0)

            st1t = contextlib.ExitStack()
            with st1t:
                sb1t = lambda n, shp, dt: st1t.enter_context(
                    nc.sbuf_tensor(n, shp, dt))
                w_raw = sb1t("w_raw", [128, CC // 2, C], F32)
                wbf = sb1t("wbf", [128, CC, C], BF16)
                birow = sb1t("birow", [1, C], F32)
                brrow = sb1t("brrow", [1, C], F32)
                trif = sb1t("trif", [128, 128], F32)
                xn_t = sb1t("xn_t", [128, 2, C], BF16)

                xr = x_in.rearrange("(m p) c -> p m c", p=128)
                for j in range(4):
                    S.dop("g", x_s[:, 4 * j:4 * (j + 1), :],
                          xr[:, 4 * j:4 * (j + 1), :],
                          writes=[("x_s", m) for m in range(4 * j, 4 * j + 4)])
                S.dop("s", birow[:, :], bi_in, writes=["birow"])
                S.dop("s", brrow[:, :], br_in, writes=["brrow"])
                S.dop("s", lamt[:, 0:1], lam_in, writes=["lam0"])
                S.op("g", lambda e: e.memset(ones_f[:, :], 1.0), writes=["ones_f"])
                S.op("g", lambda e: e.memset(onesrow[:, :], 1.0), writes=["onesrow"])
                S.op("g", lambda e: e.memset(half_t[:, :], 0.5), writes=["half"])
                S.op("g", lambda e: e.memset(ccol[:, 0:1], 0.5), writes=["ccol"])
                S.op("g", lambda e: e.memset(ccol[:, 1:2], -1.0), writes=["ccol"])
                S.op("g", lambda e: e.memset(ccol[:, 2:3], 1.0), writes=["ccol"])
                S.op("g", lambda e: e.memset(ccol[:, 3:4], 0.125), writes=["ccol"])
                S.op("g", lambda e: e.memset(ccol[:, 4:5], -400.0), writes=["ccol"])
                S.op("g", lambda e: e.affine_select(ident_f[:, :], ones_f[:, :],
                                                    [[1, 128]], OP.is_equal, 0.0,
                                                    base=0, channel_multiplier=-1),
                     reads=["ones_f"], writes=["identf"])
                S.op("v", lambda e: e.tensor_copy(ident_b[:, :], ident_f[:, :]),
                     reads=["identf"], writes=["identb"])
                S.op("g", lambda e: e.affine_select(trif[:, :], ones_f[:, :],
                                                    [[1, 128]], OP.is_ge, 0.0,
                                                    base=0, channel_multiplier=-1),
                     reads=["ones_f"], writes=["trif"])
                S.op("v", lambda e: e.tensor_copy(tri_b[:, :], trif[:, :]),
                     reads=["trif"], writes=["tri"])
                S.op("a", lambda e: e.activation(lamt[:, 1:2], lamt[:, 0:1], AF.Exp,
                                                 scale=ccol[0:1, 1:2]),
                     reads=["lam0", "ccol"], writes=["lam1"])
                S.op("a", lambda e: e.activation(lamt[:, 2:3], lamt[:, 1:2], AF.Ln,
                                                 bias=ccol[0:1, 2:3]),
                     reads=["lam1", "ccol"], writes=["lam2"])
                S.op("v", lambda e: e.tensor_add(lamt[:, 3:4], lamt[:, 0:1],
                                                 lamt[:, 2:3]),
                     reads=["lam0", "lam2"], writes=["lam3"])
                S.op("v", lambda e: e.tensor_scalar(lamt[:, 4:5], lamt[:, 3:4],
                                                    -4.0, None, OP.mult),
                     reads=["lam3"], writes=["lam4"])
                S.op("v", lambda e: e.tensor_scalar(lamt[:, 5:6], lamt[:, 3:4],
                                                    -8.0, None, OP.mult),
                     reads=["lam3"], writes=["lam5"])
                S.op("t", lambda e: e.matmul(ps_c[:, 0:2], onesrow[:, :],
                                             lamt[:, 4:6], start=True, stop=True),
                     reads=["onesrow", "lam4", "lam5"], writes=[PSA0])
                S.op("v", lambda e: e.tensor_copy(cvec[:, :], ps_c[:, 0:2]),
                     reads=[PSA0], writes=["cvec"])
                for k in range(CC):
                    S.op("t", lambda e, k=k: e.matmul(
                            ps_c[:, 8 + k:9 + k],
                            birow[0:1, 128 * k:128 * (k + 1)], half_t[:, :],
                            start=True, stop=True),
                         reads=["birow", "half"], writes=[PSA0])
                S.op("v", lambda e: e.tensor_copy(bih[:, :], ps_c[:, 8:8 + CC]),
                     reads=[PSA0], writes=["bih"])
                for k in range(CC):
                    S.op("t", lambda e, k=k: e.matmul(
                            ps_c[:, 8 + k:9 + k],
                            brrow[0:1, 128 * k:128 * (k + 1)], half_t[:, :],
                            start=True, stop=True),
                         reads=["brrow", "half"], writes=[PSA0])
                S.op("v", lambda e: e.tensor_copy(brh[:, :], ps_c[:, 8:8 + CC]),
                     reads=[PSA0], writes=["brh"])

                for m in range(TM):
                    ln_stats(m)
                ln_finish()
                for m in range(TM):
                    sl = m % 2
                    ln_norm(m, xn_t[:, sl, :], ("xn_t", sl))
                    if dump_stage == 10:
                        S.op("v", lambda e, sl=sl: e.tensor_copy(
                                xsq[:, :], xn_t[:, sl, :]),
                             reads=[("xn_t", sl)], writes=["xsq"])
                        S.dop("s", out_hbm.rearrange("(m p) c -> p m c", p=128)
                              [:, m, :], xsq[:, :], reads=["xsq"])
                    for k in range(CC):
                        pe_t_bf(xn_t[:, sl, 128 * k:128 * (k + 1)],
                                xnT[:, k, 128 * m:128 * (m + 1)], k % 2,
                                reads=[("xn_t", sl)], writes=[("xnT", k, m)])
                for (w_src, dstT, dk) in ((wi_in, wiT, "wiT"),
                                          (wr_in, wrT, "wrT")):
                    wsr = w_src.rearrange("(d p) c -> p d c", p=128)
                    for j in range(2):
                        S.dop("s", w_raw[:, :, :], wsr[:, 4 * j:4 * (j + 1), :],
                              writes=["w_raw"])
                        S.op("g", lambda e, j=j: e.tensor_copy(
                                wbf[:, 4 * j:4 * (j + 1), :], w_raw[:, :, :]),
                             reads=["w_raw"], writes=[("wbf", j)])
                    for dcj in range(CC):
                        for k in range(CC):
                            pe_t_bf(wbf[:, dcj, 128 * k:128 * (k + 1)],
                                    dstT[:, k, 128 * dcj:128 * (dcj + 1)], k % 2,
                                    reads=[("wbf", dcj // 4)],
                                    writes=[(dk, k, dcj)])
                S.barrier()

            st1b = contextlib.ExitStack()
            with st1b:
                at_bufs = st1b.enter_context(
                    nc.sbuf_tensor("atb", [128, 4, T], F32))
                wraw = st1b.enter_context(nc.sbuf_tensor("wraw", [128, 4, C], F32))
                wct = [0]
                wqb = st1b.enter_context(nc.sbuf_tensor("wqb", [128, 2, C], BF16))
                stw = st1b.enter_context(
                    nc.sbuf_tensor("stw", [128, 2, 8, 128], BF16))
                AB = lambda i: at_bufs[:, i, :]

                def wq_load_gate(wsrc, i):
                    out = []
                    for ii in (i, i + 1):
                        if ii >= IC:
                            break
                        wsl = wct[0] % 4
                        wct[0] += 1
                        S.dop("s", wraw[:, wsl, :],
                              wsrc[128 * ii:128 * (ii + 1), :],
                              writes=[("wraw", wsl)])
                        out.append(wsl)
                    return out

                def wq_chunk_gate(wsrc, dst, i, wsls):
                    jobs = []
                    for n_, ii in enumerate((i, i + 1)):
                        if ii >= IC:
                            break
                        sl = ii % 2
                        wsl = wsls[n_]
                        jobs.append((wraw[:, wsl, 0:512], wqb[:, sl, 0:512], 512,
                                     ("wraw", wsl), ("wqb", sl, 0), 2 * sl))
                        jobs.append((wraw[:, wsl, 512:1024],
                                     wqb[:, sl, 512:1024], 512,
                                     ("wraw", wsl), ("wqb", sl, 1), 2 * sl + 1))
                    run_quant(jobs)
                    for ii in (i, i + 1):
                        if ii >= IC:
                            break
                        sl = ii % 2
                        for k in range(CC):
                            pe_t_bf(wqb[:, sl, 128 * k:128 * (k + 1)],
                                    stw[:, sl, k, :], k % 2,
                                    reads=[("wqb", sl, k // 4)],
                                    writes=[("stw", sl)])
                        S.dop("s",
                              dst.rearrange("k p i -> p k i")
                              [:, :, 128 * ii:128 * (ii + 1)],
                              stw[:, sl, :, :], reads=[("stw", sl)])

                def wq_load_down(j, pieces_grp):
                    out = []
                    for (p0, pw) in pieces_grp:
                        wsl = wct[0] % 4
                        wct[0] += 1
                        S.dop("s", wraw[:, wsl, 0:pw],
                              wd_in[128 * j:128 * (j + 1), p0:p0 + pw],
                              writes=[("wraw", wsl)])
                        out.append(wsl)
                    return out

                def wq_chunk_down(j, pieces_grp, wsls):
                    jobs = []
                    for n_, (p0, pw) in enumerate(pieces_grp):
                        sl = (p0 // 1024) % 2
                        wsl = wsls[n_]
                        w0 = 512
                        jobs.append((wraw[:, wsl, 0:w0], wqb[:, sl, 0:w0], w0,
                                     ("wraw", wsl), ("wqb", sl, 0), 2 * sl))
                        jobs.append((wraw[:, wsl, w0:pw], wqb[:, sl, w0:pw],
                                     pw - w0, ("wraw", wsl), ("wqb", sl, 1),
                                     2 * sl + 1))
                    run_quant(jobs)
                    for (p0, pw) in pieces_grp:
                        sl = (p0 // 1024) % 2
                        nt = pw // 128
                        for kk in range(nt):
                            pe_t_bf(wqb[:, sl, 128 * kk:128 * (kk + 1)],
                                    stw[:, sl, kk, :], kk % 2,
                                    reads=[("wqb", sl, 0 if 128 * kk < 512
                                            else 1)],
                                    writes=[("stw", sl)])
                        k0 = p0 // 128
                        S.dop("s",
                              wqt_d.rearrange("k p c -> p k c")
                              [:, k0:k0 + nt, 128 * j:128 * (j + 1)],
                              stw[:, sl, 0:nt, :], reads=[("stw", sl)])

                wq_loads = []
                wq_jobs = []
                for (wsrc, dst) in ((wg_in, wqt_g), (wu_in, wqt_u)):
                    for i in range(0, IC, 2):
                        wq_loads.append(lambda wsrc=wsrc, i=i:
                                        wq_load_gate(wsrc, i))
                        wq_jobs.append(lambda w, wsrc=wsrc, dst=dst, i=i:
                                       wq_chunk_gate(wsrc, dst, i, w))
                for j in range(CC):
                    g1 = [(0, 1024), (1024, 1024)]
                    g2 = [(2048, 768)]
                    wq_loads.append(lambda j=j, g=g1: wq_load_down(j, g))
                    wq_jobs.append(lambda w, j=j, g=g1:
                                   wq_chunk_down(j, g, w))
                    wq_loads.append(lambda j=j, g=g2: wq_load_down(j, g))
                    wq_jobs.append(lambda w, j=j, g=g2:
                                   wq_chunk_down(j, g, w))
                njobs = len(wq_jobs)
                ji = [0]
                pending_w = []

                def drain_wq(n):
                    while ji[0] < njobs and n > 0:
                        if len(pending_w) <= ji[0]:
                            pending_w.append(wq_loads[ji[0]]())
                        if pf_w and ji[0] + 1 < njobs and \
                                len(pending_w) <= ji[0] + 1:
                            pending_w.append(wq_loads[ji[0] + 1]())
                        wq_jobs[ji[0]](w=pending_w[ji[0]])
                        ji[0] += 1
                        n -= 1

                for d in range(CC):
                    for (obuf, wT, wkeys, bcol, bkey) in (
                            (0, wiT, "wiT", bih, "bih"),):
                        for t4 in range(4):
                            slot = 2 * (t4 % 2)
                            pk = ("ps_a", slot)
                            for k in range(CC):
                                S.op("t", lambda e, k=k, t4=t4, wT=wT, d=d,
                                     slot=slot: e.matmul(
                                        ps_a[:, slot, :],
                                        wT[:, k, 128 * d:128 * (d + 1)],
                                        xnT[:, k, 512 * t4:512 * (t4 + 1)],
                                        start=(k == 0), stop=(k == CC - 1)),
                                     writes=[pk])
                            S.op("a", lambda e, t4=t4, d=d, bcol=bcol,
                                 slot=slot: e.activation(
                                    AB(0)[:, 512 * t4:512 * (t4 + 1)],
                                    ps_a[:, slot, :], AF.Tanh,
                                    scale=ccol[:, 0:1], bias=bcol[:, d:d + 1]),
                                 reads=[pk, bkey, "ccol"], writes=[("ab", 0)])
                    S.op("v", lambda e, d=d: e.scalar_tensor_tensor(
                            AB(1)[:, :], AB(0)[:, :], 1.0, xnT[:, d, :], OP.add,
                            OP.mult),
                         reads=[("ab", 0)], writes=[("ab", 1)])
                    for (obuf, wT, wkeys, bcol, bkey) in (
                            (0, wrT, "wrT", brh, "brh"),):
                        for t4 in range(4):
                            slot = 2 * (t4 % 2) + 1
                            pk = ("ps_a", slot)
                            for k in range(CC):
                                S.op("t", lambda e, k=k, t4=t4, wT=wT, d=d,
                                     slot=slot: e.matmul(
                                        ps_a[:, slot, :],
                                        wT[:, k, 128 * d:128 * (d + 1)],
                                        xnT[:, k, 512 * t4:512 * (t4 + 1)],
                                        start=(k == 0), stop=(k == CC - 1)),
                                     writes=[pk])
                            S.op("a", lambda e, t4=t4, d=d, bcol=bcol,
                                 slot=slot: e.activation(
                                    AB(0)[:, 512 * t4:512 * (t4 + 1)],
                                    ps_a[:, slot, :], AF.Tanh,
                                    scale=ccol[:, 0:1], bias=bcol[:, d:d + 1]),
                                 reads=[pk, bkey, "ccol", ("ab", 1)],
                                 writes=[("ab", 0)])
                    S.op("a", lambda e: e.activation(AB(2)[:, :], AB(0)[:, :],
                                                     AF.Exp, scale=cvec[:, 0:1],
                                                     bias=cvec[:, 0:1]),
                         reads=[("ab", 0), "cvec"], writes=[("ab", 2)])
                    S.op("a", lambda e: e.activation(AB(3)[:, :], AB(0)[:, :],
                                                     AF.Exp, scale=cvec[:, 1:2],
                                                     bias=cvec[:, 1:2]),
                         reads=[("ab", 0), "cvec"], writes=[("ab", 3)])
                    S.op("a", lambda e: e.activation(AB(0)[:, :], AB(3)[:, :],
                                                     AF.Sqrt, scale=ccol[:, 1:2],
                                                     bias=ccol[:, 2:3]),
                         reads=[("ab", 3), "ccol"], writes=[("ab", 0)])
                    S.op("v", lambda e: e.scalar_tensor_tensor(
                            AB(1)[:, :], AB(1)[:, :], 0.5, AB(0)[:, :], OP.mult,
                            OP.mult),
                         reads=[("ab", 1), ("ab", 0)], writes=[("ab", 1)])
                    S.op("v", lambda e: e.tensor_tensor_scan(
                            AB(3)[:, :], AB(2)[:, :], AB(1)[:, :], 0.0, OP.mult,
                            OP.add),
                         reads=[("ab", 2), ("ab", 1)], writes=[("ab", 3)])
                    if dump_stage in (11, 12, 13, 14, 15):
                        buf = {11: 0, 12: 1, 13: 2, 14: 4, 15: 3}[dump_stage]
                        dv = out_hbm.rearrange("(d p) c -> d p c", p=128)
                        S.dop("s", dv[2 * d, :, :], AB(buf)[:, 0:1024],
                              reads=[("ab", buf)])
                        S.dop("s", dv[2 * d + 1, :, :], AB(buf)[:, 1024:2048],
                              reads=[("ab", buf)])
                    for m in range(TM):
                        slot = m % 2
                        pk = ("ps_trf", slot)
                        S.op("t", lambda e, m=m, slot=slot: e.transpose(
                                ps_trf[:, slot, 0:128],
                                AB(3)[:, 128 * m:128 * (m + 1)], ident_f[:, :]),
                             reads=[("ab", 3), "identf"], writes=[pk])
                        S.op("v", lambda e, m=m, d=d, slot=slot: e.tensor_add(
                                x_s[:, m, 128 * d:128 * (d + 1)],
                                x_s[:, m, 128 * d:128 * (d + 1)],
                                ps_trf[:, slot, 0:128]),
                             reads=[pk, ("x_s", m)], writes=[("x_s", m)])
                    drain_wq(0)
                if dump_stage == 16:
                    for j in range(4):
                        S.dop("s", out_hbm.rearrange("(m p) c -> p m c", p=128)
                              [:, 4 * j:4 * (j + 1), :],
                              x_s[:, 4 * j:4 * (j + 1), :],
                              reads=[("x_s", m2) for m2 in range(4 * j, 4 * j + 4)])
                drain_wq(njobs)
                if dump_stage == 18:
                    wdbg = wqb[:, 0, :]
                    wdbf = wraw[:, 0, :]
                    for k in range(CC):
                        S.dop("s", wdbg, wqt_g[k, :, 0:1024], writes=["wdbg"])
                        S.op("v", lambda e: e.tensor_copy(wdbf, wdbg),
                             reads=["wdbg"], writes=["wdbf"])
                        S.dop("s", out_hbm[128 * k:128 * (k + 1), :],
                              wdbf, reads=["wdbf"])
                if dump_stage in (1, 17):
                    for j in range(4):
                        S.dop("s", out_hbm.rearrange("(m p) c -> p m c", p=128)
                              [:, 4 * j:4 * (j + 1), :],
                              x_s[:, 4 * j:4 * (j + 1), :],
                              reads=[("x_s", m2) for m2 in range(4 * j, 4 * j + 4)])
        S.barrier()

        def ffn_phase(tag):
            st2 = contextlib.ExitStack()
            with st2:
                sb2 = lambda n, shp, dt: st2.enter_context(
                    nc.sbuf_tensor(n + tag, shp, dt))
                ps2 = lambda n, shp, dt: st2.enter_context(
                    nc.psum_tensor(n + tag, shp, dt))
                xn2 = sb2("xn2", [128, 2, C], F32)
                xq_t = sb2("xq_t", [128, 2, C], BF16)
                wqdT = sb2("wqdT", [128, IC, C], BF16)
                wcol = sb2("wcol", [128, 2, CC, 512], BF16)
                sg = sb2("sg", [128, 2, 512], F32)
                hsrc = sb2("hsrc", [128, 2, 512], F32)
                hqq = sb2("hqq", [128, 2, 512], BF16)
                hqT = sb2("hqT", [128, IC, 256], BF16)
                ps_g = ps2("ps_g", [128, 2, 512], F32)
                ps_d = ps2("ps_d", [128, 2, 512], F32)
                wqdr = wqt_d.rearrange("k p c -> p k c")
                for j in range(4):
                    k0, k1 = 6 * j, min(6 * (j + 1), IC)
                    S.dop("s", wqdT[:, k0:k1, :], wqdr[:, k0:k1, :],
                          writes=["wqdT"])
                for m in range(TM):
                    ln_stats(m)
                ln_finish()
                for m0 in range(0, TM, 2):
                    jobs = []
                    for m in (m0, m0 + 1):
                        sl = m % 2
                        ln_norm(m, xn2[:, sl, :], ("xn2", sl))
                        jobs.append((xn2[:, sl, 0:512], xq_t[:, sl, 0:512], 512,
                                     ("xn2", sl), ("xq", sl, 0), 2 * sl))
                        jobs.append((xn2[:, sl, 512:1024],
                                     xq_t[:, sl, 512:1024], 512,
                                     ("xn2", sl), ("xq", sl, 1), 2 * sl + 1))
                    run_quant(jobs)
                    for m in (m0, m0 + 1):
                        sl = m % 2
                        for k in range(CC):
                            pe_t_bf(xq_t[:, sl, 128 * k:128 * (k + 1)],
                                    xnT[:, k, 128 * m:128 * (m + 1)], k % 2,
                                    reads=[("xq", sl, k // 4)],
                                    writes=[("xqT", k, m)])
                icols = [(512 * n, min(512, I - 512 * n))
                         for n in range((I + 511) // 512)]
                wgr = wqt_g.rearrange("k p i -> p k i")
                wur = wqt_u.rearrange("k p i -> p k i")
                def down_mg(mg):
                    for mm in range(2):
                        m = 2 * mg + mm
                        for n2 in range(2):
                            pk = ("ps_d", n2)
                            hts = [("hqT", k2, mm) for k2 in range(IC)]
                            for k in range(IC):
                                S.op("t", lambda e, k=k, mm=mm, n2=n2:
                                     e.matmul(ps_d[:, n2, :],
                                              hqT[:, k, 128 * mm:128 * (mm + 1)],
                                              wqdT[:, k, 512 * n2:512 * (n2 + 1)],
                                              start=(k == 0), stop=(k == IC - 1)),
                                     reads=hts + ["wqdT"], writes=[pk])
                            S.op("v", lambda e, m=m, n2=n2: e.tensor_add(
                                    x_s[:, m, 512 * n2:512 * (n2 + 1)],
                                    x_s[:, m, 512 * n2:512 * (n2 + 1)],
                                    ps_d[:, n2, :]),
                                 reads=[pk, ("x_s", m)], writes=[("x_s", m)])

                sched = []
                for mg in range(8):
                    for (c0, cw) in icols:
                        sched.append(("g", mg, c0, cw))
                        sched.append(("u", mg, c0, cw))

                def load_col(idx, slot):
                    mat, _, c0, cw = sched[idx]
                    srcr = wgr if mat == "g" else wur
                    S.dop("s", wcol[:, slot, :, 0:cw], srcr[:, :, c0:c0 + cw],
                          writes=[("wcol", slot)])

                load_col(0, 0)
                pending = []
                pend_down = []
                for idx, (mat, mg, c0, cw) in enumerate(sched):
                    slot = idx % 2
                    if idx + 1 < len(sched):
                        load_col(idx + 1, 1 - slot)
                    if mat == "g":
                        for mm in range(2):
                            m = 2 * mg + mm
                            for k in range(CC):
                                S.op("t", lambda e, k=k, m=m, mm=mm, cw=cw,
                                     ws=slot: e.matmul(
                                        ps_g[:, mm, 0:cw],
                                        xnT[:, k, 128 * m:128 * (m + 1)],
                                        wcol[:, ws, k, 0:cw],
                                        start=(k == 0), stop=(k == CC - 1)),
                                     reads=[("wcol", slot), ("xqT", k, m)],
                                     writes=[("ps_g", mm)])
                            S.op("a", lambda e, mm=mm, cw=cw: e.activation(
                                    sg[:, mm, 0:cw], ps_g[:, mm, 0:cw], AF.Silu),
                                 reads=[("ps_g", mm)], writes=[("sg", mm)])
                    else:
                        for mm in range(2):
                            m = 2 * mg + mm
                            for k in range(CC):
                                S.op("t", lambda e, k=k, m=m, mm=mm, cw=cw,
                                     ws=slot: e.matmul(
                                        ps_g[:, mm, 0:cw],
                                        xnT[:, k, 128 * m:128 * (m + 1)],
                                        wcol[:, ws, k, 0:cw],
                                        start=(k == 0), stop=(k == CC - 1)),
                                     reads=[("wcol", slot), ("xqT", k, m),
                                            ("sg", mm)],
                                     writes=[("ps_g", mm)])
                        hb = (c0 // 512) % 2
                        for mm in range(2):
                            S.op("v", lambda e, mm=mm, cw=cw:
                                 e.tensor_mul(hsrc[:, mm, 0:cw], sg[:, mm, 0:cw],
                                              ps_g[:, mm, 0:cw]),
                                 reads=[("sg", mm), ("ps_g", mm)],
                                 writes=[("hs", mm)])
                        def qjob(c0=c0, cw=cw, mg=mg, hb=hb):
                            qb0 = hb * 2
                            run_quant([
                                (hsrc[:, 0, 0:cw], hqq[:, 0, 0:cw], cw,
                                 ("hs", 0), ("hqq", 0), qb0),
                                (hsrc[:, 1, 0:cw], hqq[:, 1, 0:cw], cw,
                                 ("hs", 1), ("hqq", 1), qb0 + 1)])
                            for q2 in range(2):
                                for kk in range(cw // 128):
                                    ic = (c0 // 128) + kk
                                    pe_t_bf(hqq[:, q2, 128 * kk:128 * (kk + 1)],
                                            hqT[:, ic, 128 * q2:128 * (q2 + 1)],
                                            kk % 2, reads=[("hqq", q2)],
                                            writes=[("hqT", ic, q2)])
                        pending.append(qjob)
                        if not defer_q:
                            pending.pop(0)()
                    if pend_down and (not defer_down or
                                      (mat == "g" and c0 == 0)):
                        pend_down.pop(0)()
                    if pending and (mat == "g" or (c0, cw) == icols[-1]):
                        pending.pop(0)()
                    if mat == "u" and (c0, cw) == icols[-1]:
                        while pending:
                            pending.pop(0)()
                        if defer_down:
                            pend_down.append(lambda mg=mg: down_mg(mg))
                        else:
                            down_mg(mg)
            S.barrier()

        ffn_phase("F1")
        if dump_stage == 2:
            for j in range(4):
                S.dop("s", out_hbm.rearrange("(m p) c -> p m c", p=128)
                      [:, 4 * j:4 * (j + 1), :], x_s[:, 4 * j:4 * (j + 1), :])

        st3 = contextlib.ExitStack()
        with st3:
            sb3 = lambda n, shp, dt: st3.enter_context(nc.sbuf_tensor(n, shp, dt))
            ps3 = lambda n, shp, dt: st3.enter_context(nc.psum_tensor(n, shp, dt))
            xn3 = sb3("xn3", [128, 2, C], BF16)
            wqkv_raw = sb3("wqkv_raw", [128, 2, C], F32)
            qkv_bf = sb3("qkv_bf", [128, 2, C], BF16)
            wqkvT = sb3("wqkvT", [128, CC, 192], BF16)
            wout_raw = sb3("wout_raw", [128, CC, 64], F32)
            wout_bf = sb3("wout_bf", [128, CC, 64], BF16)
            woutT = sb3("woutT", [64, C], BF16)
            qkv_sb = sb3("qkv_sb", [128, TM, 192], F32)
            q_bf = sb3("q_bf", [128, 64], BF16)
            qT_sb = sb3("qT_sb", [64, T], BF16)
            vo16 = sb3("vo16", [128, TM, 128], BF16)
            qb = sb3("qb", [128, 4, T], BF16)
            Pb = sb3("Pb", [128, 4, T], BF16)
            nd = sb3("nd", [128, T], F32)
            ohm = qA.rearrange("p a b -> p (a b)")[0:64, 0:T]
            rec0 = qB.rearrange("p a b -> p (a b)")[0:64, 0:T]
            ps_n = ps3("ps_n", [128, T], F32)

            for m in range(TM):
                ln_stats(m)
            ln_finish()
            for m in range(TM):
                sl = m % 2
                ln_norm(m, xn3[:, sl, :], ("xn3", sl))
                for k in range(CC):
                    pe_t_bf(xn3[:, sl, 128 * k:128 * (k + 1)],
                            xnT[:, k, 128 * m:128 * (m + 1)], k % 2,
                            reads=[("xn3", sl)], writes=[("xnT", k, m)])
            S.dop("s", wqkv_raw[:, 0, :], wqkv_in[0:128, :], writes=["qkvr"])
            S.dop("s", wqkv_raw[0:64, 1, :], wqkv_in[128:192, :], writes=["qkvr"])
            S.op("g", lambda e: e.tensor_copy(qkv_bf[:, :, :], wqkv_raw[:, :, :]),
                 reads=["qkvr"], writes=["qkv_bf"])
            for k in range(CC):
                pe_t_bf(qkv_bf[:, 0, 128 * k:128 * (k + 1)], wqkvT[:, k, 0:128],
                        k % 2, reads=["qkv_bf"], writes=[("wqkvT", k)])
                pe_t_bf(qkv_bf[0:64, 1, 128 * k:128 * (k + 1)],
                        wqkvT[:, k, 128:192], k % 2, reads=["qkv_bf"],
                        writes=[("wqkvT", k)])
            S.dop("s", wout_raw[:, :, :],
                  wout_in.rearrange("(cc p) h -> p cc h", p=128),
                  writes=["wout_raw"])
            S.op("g", lambda e: e.tensor_copy(wout_bf[:, :, :], wout_raw[:, :, :]),
                 reads=["wout_raw"], writes=["wout_bf"])
            for k in range(CC):
                pe_t_bf(wout_bf[:, k, 0:64], woutT[0:64, 128 * k:128 * (k + 1)],
                        k % 2, reads=["wout_bf"], writes=["woutT"])
            for m in range(TM):
                pkk = "ps_n"
                for k in range(CC):
                    S.op("t", lambda e, k=k, m=m: e.matmul(
                            ps_n[:, 0:192], xnT[:, k, 128 * m:128 * (m + 1)],
                            wqkvT[:, k, :], start=(k == 0), stop=(k == CC - 1)),
                         reads=[("xnT", k, m)] + [("wqkvT", kk) for kk in
                                                  range(CC)],
                         writes=[pkk])
                S.op("a", lambda e, m=m: e.activation(qkv_sb[:, m, 0:64],
                                                      ps_n[:, 0:64], AF.Copy),
                     reads=[pkk], writes=[("qkv", m)])
                S.op("a", lambda e, m=m: e.activation(qkv_sb[:, m, 64:128],
                                                      ps_n[:, 64:128], AF.Copy,
                                                      scale=ccol[:, 3:4]),
                     reads=[pkk, "ccol"], writes=[("qkv", m)])
                S.op("a", lambda e, m=m: e.activation(qkv_sb[:, m, 128:192],
                                                      ps_n[:, 128:192], AF.Copy),
                     reads=[pkk], writes=[("qkv", m)])
                S.op("v", lambda e, m=m: e.tensor_copy(q_bf[:, :],
                                                       qkv_sb[:, m, 0:64]),
                     reads=[("qkv", m)], writes=["q_bf"])
                pe_t_bf(q_bf[:, :], qT_sb[0:64, 128 * m:128 * (m + 1)], m % 2,
                        reads=["q_bf"], writes=["qT"])
            S.dop("s", qT_dram, qT_sb[0:64, :], reads=["qT"], writes=["qTd"])
            S.op("g", lambda e: e.memset(vo16[:, :, :], 0.0), writes=["vo"])
            def load_qb(h):
                S.dop("s", qb[:, h % 4, :],
                      qT_dram[h:h + 1, :].partition_broadcast(128),
                      reads=["qTd"], writes=[("qb", h % 4)])

            for h in range(3):
                load_qb(h)
            for h in range(NH):
                if h + 3 < NH:
                    load_qb(h + 3)
                if h > 0:
                    S.op("g", lambda e, h=h: e.memset(vo16[:, :, h - 1:h], 0.0),
                         writes=["vo"])
                    S.op("g", lambda e, h=h: e.memset(
                            vo16[:, :, 63 + h:64 + h], 0.0), writes=["vo"])
                S.op("v", lambda e, h=h: e.tensor_copy(
                        vo16[:, :, h:h + 1], qkv_sb[:, :, 128 + h:129 + h]),
                     reads=[("qkv", m2) for m2 in range(TM)], writes=["vo"])
                S.op("g", lambda e, h=h: e.memset(vo16[:, :, 64 + h:65 + h], 1.0),
                     writes=["vo"])
                qsl = h % 4
                for b in range(TM):
                    ext = T - 128 * b
                    psl = (h * TM + b) % 4
                    S.op("a", lambda e, h=h, b=b, psl=psl, qsl=qsl, ext=ext:
                         e.activation(Pb[:, psl, 0:ext], qb[:, qsl, 128 * b:T],
                                      AF.Exp,
                                      scale=qkv_sb[:, b, 64 + h:65 + h]),
                         reads=[("qb", qsl), ("qkv", b)], writes=[("Pb", psl)])
                    S.op("v", lambda e, psl=psl: e.tensor_mul(
                            Pb[:, psl, 0:128], Pb[:, psl, 0:128], tri_b[:, :]),
                         reads=[("Pb", psl), "tri"], writes=[("Pb", psl)])
                    i0 = 128 * b
                    while i0 < T:
                        i1 = min((i0 // 512 + 1) * 512, T)
                        S.op("t", lambda e, b=b, psl=psl, i0=i0, i1=i1, h=h:
                             e.matmul(ps_n[:, i0:i1], vo16[:, b, :],
                                      Pb[:, psl, i0 - 128 * b:i1 - 128 * b],
                                      start=(h == 0 and b == 0),
                                      stop=(h == NH - 1)),
                             reads=["vo", ("Pb", psl)], writes=["ps_n"])
                        i0 = i1
            S.op("v", lambda e: e.tensor_copy(nd[:, :], ps_n[:, :]),
                 reads=["ps_n"], writes=["nd"])
            def _recip(e):
                with nc.allow_low_precision(
                        reason="softmax denom reciprocal stored bf16; "
                               "~0.4% on a small additive term"):
                    return e.reciprocal(rec0, nd[64:128, :])
            S.op("v", _recip, reads=["nd"], writes=["rec0"])
            S.op("v", lambda e: e.tensor_mul(ohm, nd[0:64, :], rec0),
                 reads=["nd", "rec0"], writes=["ohm"])
            for m in range(TM):
                for n2 in range(2):
                    S.op("t", lambda e, m=m, n2=n2: e.matmul(
                            ps_n[:, 512 * n2:512 * (n2 + 1)],
                            ohm[:, 128 * m:128 * (m + 1)],
                            woutT[0:64, 512 * n2:512 * (n2 + 1)],
                            start=True, stop=True),
                         reads=["ohm", "woutT"], writes=["ps_n"])
                    S.op("v", lambda e, m=m, n2=n2: e.tensor_add(
                            x_s[:, m, 512 * n2:512 * (n2 + 1)],
                            x_s[:, m, 512 * n2:512 * (n2 + 1)],
                            ps_n[:, 512 * n2:512 * (n2 + 1)]),
                         reads=["ps_n", ("x_s", m)], writes=[("x_s", m)])
        S.barrier()
        if dump_stage == 3:
            for j in range(4):
                S.dop("s", out_hbm.rearrange("(m p) c -> p m c", p=128)
                      [:, 4 * j:4 * (j + 1), :], x_s[:, 4 * j:4 * (j + 1), :])

        ffn_phase("F2")

        if dump_stage is None:
            outr = out_hbm.rearrange("(m p) c -> p m c", p=128)
            for j in range(4):
                S.dop("s", outr[:, 4 * j:4 * (j + 1), :],
                      x_s[:, 4 * j:4 * (j + 1), :],
                      reads=[("x_s", m) for m in range(4 * j, 4 * j + 4)])
        S.eng["s"].wait_ge(S.sem["ds"], S.cnt["ds"])
        S.eng["g"].wait_ge(S.sem["dg"], S.cnt["dg"])
    return nc


_NC_CACHE = {}


def kernel(L0_attn_a_x, L0_attn_a_Wi, L0_attn_a_bi, L0_attn_a_Wr, L0_attn_a_br,
           L0_attn_a_cap_lambda, L0_attn_b_w_qkv, L0_attn_b_w_out,
           L0_ffn_w_gate, L0_ffn_w_up, L0_ffn_w_down):
    cc = np.ascontiguousarray
    x = cc(np.asarray(L0_attn_a_x, dtype=np.float32))
    wi = cc(np.asarray(L0_attn_a_Wi, dtype=np.float32))
    bi = cc(np.asarray(L0_attn_a_bi, dtype=np.float32)).reshape(1, C)
    wr = cc(np.asarray(L0_attn_a_Wr, dtype=np.float32))
    br = cc(np.asarray(L0_attn_a_br, dtype=np.float32)).reshape(1, C)
    lam = cc(np.asarray(L0_attn_a_cap_lambda, dtype=np.float32)).reshape(1, 1)
    wqkv = cc(np.asarray(L0_attn_b_w_qkv, dtype=np.float32))
    wout = cc(np.asarray(L0_attn_b_w_out, dtype=np.float32))
    wg = cc(np.asarray(L0_ffn_w_gate, dtype=np.float32))
    wu = cc(np.asarray(L0_ffn_w_up, dtype=np.float32))
    wd = cc(np.asarray(L0_ffn_w_down, dtype=np.float32))

    if "nc" not in _NC_CACHE:
        _NC_CACHE["nc"] = build_nc()
    nc = _NC_CACHE["nc"]
    in_maps = []
    for b in range(NCORES):
        in_maps.append({"x": cc(x[b]), "wi": wi, "bi": bi, "wr": wr, "br": br,
                        "lam": lam, "wqkv": wqkv, "wout": wout, "wg": wg,
                        "wu": wu, "wd": wd})
    res = run_bass_kernel_spmd(nc, in_maps, core_ids=list(range(NCORES)))
    return np.stack([r["out"] for r in res.results], axis=0)


# version 13 (final)
# speedup vs baseline: 1.5374x; 1.0044x over previous
import contextlib
import numpy as np
import concourse.bass as bass
import concourse.mybir as mybir
from concourse.bass_utils import run_bass_kernel_spmd

F32 = mybir.dt.float32
BF16 = mybir.dt.bfloat16
AF = mybir.ActivationFunctionType
OP = mybir.AluOpType
AX = mybir.AxisListType

T, C, I, NH = 2048, 1024, 2816, 64
TM, CC, IC = T // 128, C // 128, I // 128
NCORES = 8


class Sched:
    ENGS = ("v", "a", "t", "g", "s")

    def __init__(self, nc, stack):
        self.nc = nc
        self.eng = {"v": nc.vector, "a": nc.scalar, "t": nc.tensor,
                    "g": nc.gpsimd, "s": nc.sync}
        self.sem = {}
        self.cnt = {}
        self.wm = {k: {} for k in self.ENGS}
        for k in self.ENGS:
            self.sem[k] = stack.enter_context(nc.semaphore(f"S_{k}"))
            self.cnt[k] = 0
        for k in ("ds", "dg"):
            self.sem[k] = stack.enter_context(nc.semaphore(f"S_{k}"))
            self.cnt[k] = 0
        self.bar_sem = stack.enter_context(nc.semaphore("S_bar"))
        self.bar_n = 0
        self.bufs = {}

    def _wait(self, ek, deps, war=()):
        e = self.eng[ek]
        need = {}
        for d in deps:
            if d is None:
                continue
            sk, val = d
            if val > need.get(sk, 0):
                need[sk] = val
        for d in war:
            if d is None:
                continue
            sk, val = d
            if sk == ek:
                continue
            if val > need.get(sk, 0):
                need[sk] = val
        for sk in sorted(need):
            val = need[sk]
            if self.wm[ek].get(sk, -1) >= val:
                continue
            e.wait_ge(self.sem[sk], val)
            self.wm[ek][sk] = val

    def _slot(self, key):
        return self.bufs.setdefault(key, {"w": None, "r": []})

    def op(self, ek, fn, reads=(), writes=(), deps=()):
        dl = list(deps)
        war = []
        for k in reads:
            dl.append(self._slot(k)["w"])
        for k in writes:
            s = self._slot(k)
            war.extend(s["r"])
            war.append(s["w"])
        self._wait(ek, dl, war)
        inst = fn(self.eng[ek])
        self.cnt[ek] += 1
        inst.then_inc(self.sem[ek], 1)
        tok = (ek, self.cnt[ek])
        for k in reads:
            self._slot(k)["r"].append(tok)
        for k in writes:
            s = self._slot(k)
            s["w"] = tok
            s["r"] = []
        return tok

    def dop(self, qk, out_ap, in_ap, reads=(), writes=(), deps=(), **kw):
        dk = {"s": "ds", "g": "dg"}[qk]
        dl = list(deps)
        for k in reads:
            dl.append(self._slot(k)["w"])
        for k in writes:
            s = self._slot(k)
            dl.extend(s["r"])
            dl.append(s["w"])
        self._wait(qk, dl, ())
        self.eng[qk].dma_start(out_ap, in_ap, **kw).then_inc(self.sem[dk], 16)
        self.cnt[dk] += 16
        tok = (dk, self.cnt[dk])
        for k in reads:
            self._slot(k)["r"].append(tok)
        for k in writes:
            s = self._slot(k)
            s["w"] = tok
            s["r"] = []
        return tok

    def barrier(self):
        for qk, dk in (("s", "ds"), ("g", "dg")):
            if self.cnt[dk] > 0 and self.wm[qk].get(dk, -1) < self.cnt[dk]:
                self.eng[qk].wait_ge(self.sem[dk], self.cnt[dk])
                self.wm[qk][dk] = self.cnt[dk]
        self.bar_n += 1
        for ek in self.ENGS:
            self.eng[ek].nop().then_inc(self.bar_sem, 1)
        for ek in self.ENGS:
            self.eng[ek].wait_ge(self.bar_sem, 5 * self.bar_n)
        self.bufs = {}
        for ek in self.ENGS:
            self.wm[ek] = {}


def build_nc(dump_stage=None, defer_down=True, defer_q=True, pf_w=True):
    nc = bass.Bass("TRN2", target_bir_lowering=False)
    ap = lambda t: t.ap()
    x_in = ap(nc.dram_tensor("x", [T, C], F32, kind="ExternalInput"))
    wi_in = ap(nc.dram_tensor("wi", [C, C], F32, kind="ExternalInput"))
    bi_in = ap(nc.dram_tensor("bi", [1, C], F32, kind="ExternalInput"))
    wr_in = ap(nc.dram_tensor("wr", [C, C], F32, kind="ExternalInput"))
    br_in = ap(nc.dram_tensor("br", [1, C], F32, kind="ExternalInput"))
    lam_in = ap(nc.dram_tensor("lam", [1, 1], F32, kind="ExternalInput"))
    wqkv_in = ap(nc.dram_tensor("wqkv", [192, C], F32, kind="ExternalInput"))
    wout_in = ap(nc.dram_tensor("wout", [C, 64], F32, kind="ExternalInput"))
    wg_in = ap(nc.dram_tensor("wg", [I, C], F32, kind="ExternalInput"))
    wu_in = ap(nc.dram_tensor("wu", [I, C], F32, kind="ExternalInput"))
    wd_in = ap(nc.dram_tensor("wd", [C, I], F32, kind="ExternalInput"))
    out_hbm = ap(nc.dram_tensor("out", [T, C], F32, kind="ExternalOutput"))
    wqt_g = ap(nc.dram_tensor("wqt_g", [CC, 128, I], BF16, kind="Internal"))
    wqt_u = ap(nc.dram_tensor("wqt_u", [CC, 128, I], BF16, kind="Internal"))
    wqt_d = ap(nc.dram_tensor("wqt_d", [IC, 128, C], BF16, kind="Internal"))
    qT_dram = ap(nc.dram_tensor("qT_d", [64, T], BF16, kind="Internal"))

    gstack = contextlib.ExitStack()
    with gstack:
        sb = lambda n, shp, dt: gstack.enter_context(nc.sbuf_tensor(n, shp, dt))
        S = Sched(nc, gstack)

        x_s = sb("x_s", [128, TM, C], F32)
        xnT = sb("xnT", [128, CC, T], BF16)
        ident_b = sb("ident_b", [128, 128], BF16)
        tri_b = sb("tri_b", [128, 128], BF16)
        cvec = sb("cvec", [128, 2], F32)
        ccol = sb("ccol", [128, 5], F32)
        bih = sb("bih", [128, CC], F32)
        brh = sb("brh", [128, CC], F32)
        lamt = sb("lamt", [1, 8], F32)
        stat = sb("stat", [128, 4 * TM], F32)
        qS = sb("qS_", [128, 4, 512], BF16)
        qA = sb("qA_", [128, 4, 512], BF16)
        qB = sb("qB_", [128, 4, 512], BF16)
        qC = sb("qC_", [128, 4, 512], BF16)
        qM = sb("qM_", [128, 4, 32], F32)
        qR = sb("qR_", [128, 4, 32], F32)
        qL = sb("qL_", [128, 4, 32], BF16)
        ps_tr = gstack.enter_context(nc.psum_tensor("ps_tr", [128, 2, 1024], BF16))

        sx = lambda m: stat[:, m:m + 1]
        ssq = lambda m: stat[:, TM + m:TM + m + 1]
        mu = lambda m: stat[:, 2 * TM + m:2 * TM + m + 1]
        rstd = lambda m: stat[:, 3 * TM + m:3 * TM + m + 1]

        def pe_t_bf(src_ap, dst_ap, slot, reads, writes, ev="a"):
            f = src_ap.shape[-1]
            p = src_ap.shape[0]
            pk = ("ps_tr", slot)
            S.op("t", lambda e: e.transpose(ps_tr[0:f, slot, 0:p], src_ap,
                                            ident_b[0:p, 0:p]),
                 reads=list(reads) + ["identb"], writes=[pk])
            if ev == "a":
                S.op("a", lambda e: e.activation(dst_ap, ps_tr[0:f, slot, 0:p],
                                                 AF.Copy),
                     reads=[pk], writes=list(writes))
            else:
                S.op("v", lambda e: e.tensor_copy(dst_ap, ps_tr[0:f, slot, 0:p]),
                     reads=[pk], writes=list(writes))

        def ln_stats(m):
            S.op("a", lambda e: e.activation(
                    qS.rearrange("p a b -> p (a b)")[:, 0:C], x_s[:, m, :],
                    AF.Square, accum_out=ssq(m)),
                 reads=[("x_s", m)],
                 writes=[("QS", 0, "S"), ("QS", 1, "S"), ("ssq", m)])
            S.op("a", lambda e: e.activation(
                    qS.rearrange("p a b -> p (a b)")[:, 0:C], x_s[:, m, :],
                    AF.Copy, accum_out=sx(m)),
                 reads=[("x_s", m)],
                 writes=[("QS", 0, "S"), ("QS", 1, "S"), ("sx", m)])

        def ln_finish():
            sxa = stat[:, 0:TM]
            ssqa = stat[:, TM:2 * TM]
            mua = stat[:, 2 * TM:3 * TM]
            rsa = stat[:, 3 * TM:4 * TM]
            rd = [("sx", m) for m in range(TM)] + [("ssq", m) for m in range(TM)]
            S.op("v", lambda e: e.tensor_scalar(mua, sxa, 1.0 / C, None, OP.mult),
                 reads=rd, writes=["mua"])
            S.op("v", lambda e: e.tensor_mul(rsa, mua, mua),
                 reads=["mua"], writes=["rsa"])
            S.op("v", lambda e: e.scalar_tensor_tensor(rsa, ssqa, 1.0 / C, rsa,
                                                       OP.mult, OP.subtract),
                 reads=rd + ["rsa"], writes=["rsa"])
            S.op("v", lambda e: e.tensor_scalar(rsa, rsa, 1e-5, None, OP.add),
                 reads=["rsa"], writes=["rsa"])
            S.op("a", lambda e: e.activation(rsa, rsa, AF.Sqrt),
                 reads=["rsa"], writes=["rsa"])
            S.op("v", lambda e: e.reciprocal(rsa, rsa),
                 reads=["rsa"], writes=["rsa"])

        def ln_norm(m, out_ap, out_key):
            S.op("v", lambda e: e.tensor_scalar(out_ap, x_s[:, m, :], mu(m),
                                                rstd(m), OP.subtract, OP.mult),
                 reads=[("x_s", m), "mua", "rsa"], writes=[out_key])

        def quant_steps(src_ap, out_ap, W, src_key, out_key, qs):
            B = W // 16
            src3 = src_ap.rearrange("p (b k) -> p b k", k=16)
            M = qM[:, qs, 0:B]
            R = qR[:, qs, 0:B]
            L = qL[:, qs, 0:B]
            Sf = qS[:, qs, 0:W]
            A = qA[:, qs, 0:W]
            Bm = qB[:, qs, 0:W]
            Cm = qC[:, qs, 0:W]
            K = lambda n: ("QS", qs, n)
            R_b = R.unsqueeze(2).broadcast_to([128, B, 16])
            L_b = L.unsqueeze(2).broadcast_to([128, B, 16])
            B3 = Bm.rearrange("p (b k) -> p b k", k=16)
            A3 = A.rearrange("p (b k) -> p b k", k=16)
            S3 = Sf.rearrange("p (b k) -> p b k", k=16)
            st = []
            st.append(("v", lambda e: e.tensor_reduce(M, src3, AX.X, OP.max,
                                                      apply_absolute_value=True),
                       [src_key], [K("M")]))
            st.append(("v", lambda e: e.tensor_scalar(M, M, 6e-12, None, OP.max),
                       [K("M")], [K("M")]))
            st.append(("v", lambda e: e.reciprocal(R, M), [K("M")], [K("R")]))
            st.append(("v", lambda e: e.tensor_scalar(R, R, 24.0, None, OP.mult),
                       [K("R")], [K("R")]))
            st.append(("v", lambda e: e.tensor_scalar(L, M, 1.0 / 12.0, None,
                                                      OP.mult),
                       [K("M")], [K("L")]))
            st.append(("v", lambda e: e.tensor_tensor(B3, src3, R_b, OP.mult),
                       [src_key, K("R")], [K("B")]))
            st.append(("a", lambda e: e.activation(Sf, Bm, AF.Square),
                       [K("B")], [K("S")]))
            st.append(("v", lambda e: e.tensor_scalar(A, Sf, 1.0, None, OP.is_gt),
                       [K("S")], [K("A")]))
            for thr in (9.0, 25.0, 49.0):
                st.append(("v", lambda e, th=thr: e.scalar_tensor_tensor(
                        A, Sf, th, A, OP.is_gt, OP.add), [K("S"), K("A")],
                        [K("A")]))
            st.append(("v", lambda e: e.tensor_scalar(Bm, Sf, 100.0, None,
                                                      OP.is_gt),
                       [K("S"), K("B")], [K("B")]))
            st.append(("v", lambda e: e.scalar_tensor_tensor(Bm, Sf, 196.0, Bm,
                                                             OP.is_gt, OP.add),
                       [K("S"), K("B")], [K("B")]))
            st.append(("a", lambda e: e.activation(Cm, Sf, AF.Sign,
                                                   bias=ccol[:, 4:5]),
                       [K("S"), "ccol"], [K("C")]))
            st.append(("v", lambda e: e.scalar_tensor_tensor(Bm, Cm, 1.0, Bm,
                                                             OP.add, OP.add),
                       [K("B"), K("C")], [K("B")]))
            st.append(("v", lambda e: e.scalar_tensor_tensor(A, Bm, 2.0, A,
                                                             OP.mult, OP.add),
                       [K("A"), K("B")], [K("A")]))
            st.append(("v", lambda e: e.tensor_tensor(S3, A3, L_b, OP.mult),
                       [K("A"), K("L")], [K("S")]))
            st.append(("a", lambda e: e.activation(Cm, src_ap, AF.Sign),
                       [src_key, K("C")], [K("C")]))
            st.append(("v", lambda e: e.tensor_mul(out_ap, Sf, Cm),
                       [K("S"), K("C")], [out_key]))
            return st

        def run_quant(jobs):
            steps = [quant_steps(*j) for j in jobs]
            n = max(len(s) for s in steps)
            for i in range(n):
                for s in steps:
                    if i < len(s):
                        ek, fn, rd, wr = s[i]
                        S.op(ek, fn, reads=rd, writes=wr)

        st1 = contextlib.ExitStack()
        with st1:
            sb1 = lambda n, shp, dt: st1.enter_context(nc.sbuf_tensor(n, shp, dt))
            ps1 = lambda n, shp, dt: st1.enter_context(nc.psum_tensor(n, shp, dt))
            wiT = sb1("wiT", [128, CC, C], BF16)
            wrT = sb1("wrT", [128, CC, C], BF16)
            ident_f = sb1("ident_f", [128, 128], F32)
            ones_f = sb1("ones_f", [128, 128], F32)
            onesrow = sb1("onesrow", [1, 128], F32)
            half_t = sb1("half_t", [1, 1], F32)
            ps_trf = ps1("ps_trf", [128, 2, 512], F32)
            ps_a = ps1("ps_a", [128, 4, 512], F32)
            ps_c = ps_a[:, 0, :]
            PSA0 = ("ps_a", 0)

            st1t = contextlib.ExitStack()
            with st1t:
                sb1t = lambda n, shp, dt: st1t.enter_context(
                    nc.sbuf_tensor(n, shp, dt))
                w_raw = sb1t("w_raw", [128, CC // 2, C], F32)
                wbf = sb1t("wbf", [128, CC, C], BF16)
                birow = sb1t("birow", [1, C], F32)
                brrow = sb1t("brrow", [1, C], F32)
                trif = sb1t("trif", [128, 128], F32)
                xn_t = sb1t("xn_t", [128, 2, C], BF16)

                xr = x_in.rearrange("(m p) c -> p m c", p=128)
                for j in range(4):
                    S.dop("g", x_s[:, 4 * j:4 * (j + 1), :],
                          xr[:, 4 * j:4 * (j + 1), :],
                          writes=[("x_s", m) for m in range(4 * j, 4 * j + 4)])
                S.dop("s", birow[:, :], bi_in, writes=["birow"])
                S.dop("s", brrow[:, :], br_in, writes=["brrow"])
                S.dop("s", lamt[:, 0:1], lam_in, writes=["lam0"])
                S.op("g", lambda e: e.memset(ones_f[:, :], 1.0), writes=["ones_f"])
                S.op("g", lambda e: e.memset(onesrow[:, :], 1.0), writes=["onesrow"])
                S.op("g", lambda e: e.memset(half_t[:, :], 0.5), writes=["half"])
                S.op("g", lambda e: e.memset(ccol[:, 0:1], 0.5), writes=["ccol"])
                S.op("g", lambda e: e.memset(ccol[:, 1:2], -1.0), writes=["ccol"])
                S.op("g", lambda e: e.memset(ccol[:, 2:3], 1.0), writes=["ccol"])
                S.op("g", lambda e: e.memset(ccol[:, 3:4], 0.125), writes=["ccol"])
                S.op("g", lambda e: e.memset(ccol[:, 4:5], -400.0), writes=["ccol"])
                S.op("g", lambda e: e.affine_select(ident_f[:, :], ones_f[:, :],
                                                    [[1, 128]], OP.is_equal, 0.0,
                                                    base=0, channel_multiplier=-1),
                     reads=["ones_f"], writes=["identf"])
                S.op("v", lambda e: e.tensor_copy(ident_b[:, :], ident_f[:, :]),
                     reads=["identf"], writes=["identb"])
                S.op("g", lambda e: e.affine_select(trif[:, :], ones_f[:, :],
                                                    [[1, 128]], OP.is_ge, 0.0,
                                                    base=0, channel_multiplier=-1),
                     reads=["ones_f"], writes=["trif"])
                S.op("v", lambda e: e.tensor_copy(tri_b[:, :], trif[:, :]),
                     reads=["trif"], writes=["tri"])
                S.op("a", lambda e: e.activation(lamt[:, 1:2], lamt[:, 0:1], AF.Exp,
                                                 scale=ccol[0:1, 1:2]),
                     reads=["lam0", "ccol"], writes=["lam1"])
                S.op("a", lambda e: e.activation(lamt[:, 2:3], lamt[:, 1:2], AF.Ln,
                                                 bias=ccol[0:1, 2:3]),
                     reads=["lam1", "ccol"], writes=["lam2"])
                S.op("v", lambda e: e.tensor_add(lamt[:, 3:4], lamt[:, 0:1],
                                                 lamt[:, 2:3]),
                     reads=["lam0", "lam2"], writes=["lam3"])
                S.op("v", lambda e: e.tensor_scalar(lamt[:, 4:5], lamt[:, 3:4],
                                                    -4.0, None, OP.mult),
                     reads=["lam3"], writes=["lam4"])
                S.op("v", lambda e: e.tensor_scalar(lamt[:, 5:6], lamt[:, 3:4],
                                                    -8.0, None, OP.mult),
                     reads=["lam3"], writes=["lam5"])
                S.op("t", lambda e: e.matmul(ps_c[:, 0:2], onesrow[:, :],
                                             lamt[:, 4:6], start=True, stop=True),
                     reads=["onesrow", "lam4", "lam5"], writes=[PSA0])
                S.op("v", lambda e: e.tensor_copy(cvec[:, :], ps_c[:, 0:2]),
                     reads=[PSA0], writes=["cvec"])
                for k in range(CC):
                    S.op("t", lambda e, k=k: e.matmul(
                            ps_c[:, 8 + k:9 + k],
                            birow[0:1, 128 * k:128 * (k + 1)], half_t[:, :],
                            start=True, stop=True),
                         reads=["birow", "half"], writes=[PSA0])
                S.op("v", lambda e: e.tensor_copy(bih[:, :], ps_c[:, 8:8 + CC]),
                     reads=[PSA0], writes=["bih"])
                for k in range(CC):
                    S.op("t", lambda e, k=k: e.matmul(
                            ps_c[:, 8 + k:9 + k],
                            brrow[0:1, 128 * k:128 * (k + 1)], half_t[:, :],
                            start=True, stop=True),
                         reads=["brrow", "half"], writes=[PSA0])
                S.op("v", lambda e: e.tensor_copy(brh[:, :], ps_c[:, 8:8 + CC]),
                     reads=[PSA0], writes=["brh"])

                for m in range(TM):
                    ln_stats(m)
                ln_finish()
                for m in range(TM):
                    sl = m % 2
                    ln_norm(m, xn_t[:, sl, :], ("xn_t", sl))
                    if dump_stage == 10:
                        S.op("v", lambda e, sl=sl: e.tensor_copy(
                                xsq[:, :], xn_t[:, sl, :]),
                             reads=[("xn_t", sl)], writes=["xsq"])
                        S.dop("s", out_hbm.rearrange("(m p) c -> p m c", p=128)
                              [:, m, :], xsq[:, :], reads=["xsq"])
                    for k in range(CC):
                        pe_t_bf(xn_t[:, sl, 128 * k:128 * (k + 1)],
                                xnT[:, k, 128 * m:128 * (m + 1)], k % 2,
                                reads=[("xn_t", sl)], writes=[("xnT", k, m)])
                for (w_src, dstT, dk) in ((wi_in, wiT, "wiT"),
                                          (wr_in, wrT, "wrT")):
                    wsr = w_src.rearrange("(d p) c -> p d c", p=128)
                    for j in range(2):
                        S.dop("s", w_raw[:, :, :], wsr[:, 4 * j:4 * (j + 1), :],
                              writes=["w_raw"])
                        S.op("g", lambda e, j=j: e.tensor_copy(
                                wbf[:, 4 * j:4 * (j + 1), :], w_raw[:, :, :]),
                             reads=["w_raw"], writes=[("wbf", j)])
                    for dcj in range(CC):
                        for k in range(CC):
                            pe_t_bf(wbf[:, dcj, 128 * k:128 * (k + 1)],
                                    dstT[:, k, 128 * dcj:128 * (dcj + 1)], k % 2,
                                    reads=[("wbf", dcj // 4)],
                                    writes=[(dk, k, dcj)])
                S.barrier()

            st1b = contextlib.ExitStack()
            with st1b:
                at_bufs = st1b.enter_context(
                    nc.sbuf_tensor("atb", [128, 4, T], F32))
                wraw = st1b.enter_context(nc.sbuf_tensor("wraw", [128, 4, C], F32))
                wct = [0]
                wqb = st1b.enter_context(nc.sbuf_tensor("wqb", [128, 2, C], BF16))
                stw = st1b.enter_context(
                    nc.sbuf_tensor("stw", [128, 2, 8, 128], BF16))
                AB = lambda i: at_bufs[:, i, :]

                def wq_load_gate(wsrc, i):
                    out = []
                    for ii in (i, i + 1):
                        if ii >= IC:
                            break
                        wsl = wct[0] % 4
                        wct[0] += 1
                        S.dop("s", wraw[:, wsl, :],
                              wsrc[128 * ii:128 * (ii + 1), :],
                              writes=[("wraw", wsl)])
                        out.append(wsl)
                    return out

                def wq_chunk_gate(wsrc, dst, i, wsls):
                    jobs = []
                    for n_, ii in enumerate((i, i + 1)):
                        if ii >= IC:
                            break
                        sl = ii % 2
                        wsl = wsls[n_]
                        jobs.append((wraw[:, wsl, 0:512], wqb[:, sl, 0:512], 512,
                                     ("wraw", wsl), ("wqb", sl, 0), 2 * sl))
                        jobs.append((wraw[:, wsl, 512:1024],
                                     wqb[:, sl, 512:1024], 512,
                                     ("wraw", wsl), ("wqb", sl, 1), 2 * sl + 1))
                    run_quant(jobs)
                    for ii in (i, i + 1):
                        if ii >= IC:
                            break
                        sl = ii % 2
                        for k in range(CC):
                            pe_t_bf(wqb[:, sl, 128 * k:128 * (k + 1)],
                                    stw[:, sl, k, :], k % 2,
                                    reads=[("wqb", sl, k // 4)],
                                    writes=[("stw", sl)])
                        S.dop("s",
                              dst.rearrange("k p i -> p k i")
                              [:, :, 128 * ii:128 * (ii + 1)],
                              stw[:, sl, :, :], reads=[("stw", sl)])

                def wq_load_down(j, pieces_grp):
                    out = []
                    for (p0, pw) in pieces_grp:
                        wsl = wct[0] % 4
                        wct[0] += 1
                        S.dop("s", wraw[:, wsl, 0:pw],
                              wd_in[128 * j:128 * (j + 1), p0:p0 + pw],
                              writes=[("wraw", wsl)])
                        out.append(wsl)
                    return out

                def wq_chunk_down(j, pieces_grp, wsls):
                    jobs = []
                    for n_, (p0, pw) in enumerate(pieces_grp):
                        sl = (p0 // 1024) % 2
                        wsl = wsls[n_]
                        w0 = 512
                        jobs.append((wraw[:, wsl, 0:w0], wqb[:, sl, 0:w0], w0,
                                     ("wraw", wsl), ("wqb", sl, 0), 2 * sl))
                        jobs.append((wraw[:, wsl, w0:pw], wqb[:, sl, w0:pw],
                                     pw - w0, ("wraw", wsl), ("wqb", sl, 1),
                                     2 * sl + 1))
                    run_quant(jobs)
                    for (p0, pw) in pieces_grp:
                        sl = (p0 // 1024) % 2
                        nt = pw // 128
                        for kk in range(nt):
                            pe_t_bf(wqb[:, sl, 128 * kk:128 * (kk + 1)],
                                    stw[:, sl, kk, :], kk % 2,
                                    reads=[("wqb", sl, 0 if 128 * kk < 512
                                            else 1)],
                                    writes=[("stw", sl)])
                        k0 = p0 // 128
                        S.dop("s",
                              wqt_d.rearrange("k p c -> p k c")
                              [:, k0:k0 + nt, 128 * j:128 * (j + 1)],
                              stw[:, sl, 0:nt, :], reads=[("stw", sl)])

                wq_loads = []
                wq_jobs = []
                for (wsrc, dst) in ((wg_in, wqt_g), (wu_in, wqt_u)):
                    for i in range(0, IC, 2):
                        wq_loads.append(lambda wsrc=wsrc, i=i:
                                        wq_load_gate(wsrc, i))
                        wq_jobs.append(lambda w, wsrc=wsrc, dst=dst, i=i:
                                       wq_chunk_gate(wsrc, dst, i, w))
                for j in range(CC):
                    g1 = [(0, 1024), (1024, 1024)]
                    g2 = [(2048, 768)]
                    wq_loads.append(lambda j=j, g=g1: wq_load_down(j, g))
                    wq_jobs.append(lambda w, j=j, g=g1:
                                   wq_chunk_down(j, g, w))
                    wq_loads.append(lambda j=j, g=g2: wq_load_down(j, g))
                    wq_jobs.append(lambda w, j=j, g=g2:
                                   wq_chunk_down(j, g, w))
                njobs = len(wq_jobs)
                ji = [0]
                pending_w = []

                def drain_wq(n):
                    while ji[0] < njobs and n > 0:
                        if len(pending_w) <= ji[0]:
                            pending_w.append(wq_loads[ji[0]]())
                        if pf_w and ji[0] + 1 < njobs and \
                                len(pending_w) <= ji[0] + 1:
                            pending_w.append(wq_loads[ji[0] + 1]())
                        wq_jobs[ji[0]](w=pending_w[ji[0]])
                        ji[0] += 1
                        n -= 1

                for d in range(CC):
                    for (obuf, wT, wkeys, bcol, bkey) in (
                            (0, wiT, "wiT", bih, "bih"),):
                        for t4 in range(4):
                            slot = 2 * (t4 % 2)
                            pk = ("ps_a", slot)
                            for k in range(CC):
                                S.op("t", lambda e, k=k, t4=t4, wT=wT, d=d,
                                     slot=slot: e.matmul(
                                        ps_a[:, slot, :],
                                        wT[:, k, 128 * d:128 * (d + 1)],
                                        xnT[:, k, 512 * t4:512 * (t4 + 1)],
                                        start=(k == 0), stop=(k == CC - 1)),
                                     writes=[pk])
                            S.op("a", lambda e, t4=t4, d=d, bcol=bcol,
                                 slot=slot: e.activation(
                                    AB(0)[:, 512 * t4:512 * (t4 + 1)],
                                    ps_a[:, slot, :], AF.Tanh,
                                    scale=ccol[:, 0:1], bias=bcol[:, d:d + 1]),
                                 reads=[pk, bkey, "ccol"], writes=[("ab", 0)])
                    S.op("v", lambda e, d=d: e.scalar_tensor_tensor(
                            AB(1)[:, :], AB(0)[:, :], 1.0, xnT[:, d, :], OP.add,
                            OP.mult),
                         reads=[("ab", 0)], writes=[("ab", 1)])
                    for (obuf, wT, wkeys, bcol, bkey) in (
                            (0, wrT, "wrT", brh, "brh"),):
                        for t4 in range(4):
                            slot = 2 * (t4 % 2) + 1
                            pk = ("ps_a", slot)
                            for k in range(CC):
                                S.op("t", lambda e, k=k, t4=t4, wT=wT, d=d,
                                     slot=slot: e.matmul(
                                        ps_a[:, slot, :],
                                        wT[:, k, 128 * d:128 * (d + 1)],
                                        xnT[:, k, 512 * t4:512 * (t4 + 1)],
                                        start=(k == 0), stop=(k == CC - 1)),
                                     writes=[pk])
                            S.op("a", lambda e, t4=t4, d=d, bcol=bcol,
                                 slot=slot: e.activation(
                                    AB(0)[:, 512 * t4:512 * (t4 + 1)],
                                    ps_a[:, slot, :], AF.Tanh,
                                    scale=ccol[:, 0:1], bias=bcol[:, d:d + 1]),
                                 reads=[pk, bkey, "ccol", ("ab", 1)],
                                 writes=[("ab", 0)])
                    S.op("a", lambda e: e.activation(AB(2)[:, :], AB(0)[:, :],
                                                     AF.Exp, scale=cvec[:, 0:1],
                                                     bias=cvec[:, 0:1]),
                         reads=[("ab", 0), "cvec"], writes=[("ab", 2)])
                    S.op("a", lambda e: e.activation(AB(3)[:, :], AB(0)[:, :],
                                                     AF.Exp, scale=cvec[:, 1:2],
                                                     bias=cvec[:, 1:2]),
                         reads=[("ab", 0), "cvec"], writes=[("ab", 3)])
                    S.op("a", lambda e: e.activation(AB(0)[:, :], AB(3)[:, :],
                                                     AF.Sqrt, scale=ccol[:, 1:2],
                                                     bias=ccol[:, 2:3]),
                         reads=[("ab", 3), "ccol"], writes=[("ab", 0)])
                    S.op("v", lambda e: e.scalar_tensor_tensor(
                            AB(1)[:, :], AB(1)[:, :], 0.5, AB(0)[:, :], OP.mult,
                            OP.mult),
                         reads=[("ab", 1), ("ab", 0)], writes=[("ab", 1)])
                    S.op("v", lambda e: e.tensor_tensor_scan(
                            AB(3)[:, :], AB(2)[:, :], AB(1)[:, :], 0.0, OP.mult,
                            OP.add),
                         reads=[("ab", 2), ("ab", 1)], writes=[("ab", 3)])
                    if dump_stage in (11, 12, 13, 14, 15):
                        buf = {11: 0, 12: 1, 13: 2, 14: 4, 15: 3}[dump_stage]
                        dv = out_hbm.rearrange("(d p) c -> d p c", p=128)
                        S.dop("s", dv[2 * d, :, :], AB(buf)[:, 0:1024],
                              reads=[("ab", buf)])
                        S.dop("s", dv[2 * d + 1, :, :], AB(buf)[:, 1024:2048],
                              reads=[("ab", buf)])
                    for m in range(TM):
                        slot = m % 2
                        pk = ("ps_trf", slot)
                        S.op("t", lambda e, m=m, slot=slot: e.transpose(
                                ps_trf[:, slot, 0:128],
                                AB(3)[:, 128 * m:128 * (m + 1)], ident_f[:, :]),
                             reads=[("ab", 3), "identf"], writes=[pk])
                        S.op("v", lambda e, m=m, d=d, slot=slot: e.tensor_add(
                                x_s[:, m, 128 * d:128 * (d + 1)],
                                x_s[:, m, 128 * d:128 * (d + 1)],
                                ps_trf[:, slot, 0:128]),
                             reads=[pk, ("x_s", m)], writes=[("x_s", m)])
                    drain_wq(0)
                if dump_stage == 16:
                    for j in range(4):
                        S.dop("s", out_hbm.rearrange("(m p) c -> p m c", p=128)
                              [:, 4 * j:4 * (j + 1), :],
                              x_s[:, 4 * j:4 * (j + 1), :],
                              reads=[("x_s", m2) for m2 in range(4 * j, 4 * j + 4)])
                drain_wq(njobs)
                if dump_stage == 18:
                    wdbg = wqb[:, 0, :]
                    wdbf = wraw[:, 0, :]
                    for k in range(CC):
                        S.dop("s", wdbg, wqt_g[k, :, 0:1024], writes=["wdbg"])
                        S.op("v", lambda e: e.tensor_copy(wdbf, wdbg),
                             reads=["wdbg"], writes=["wdbf"])
                        S.dop("s", out_hbm[128 * k:128 * (k + 1), :],
                              wdbf, reads=["wdbf"])
                if dump_stage in (1, 17):
                    for j in range(4):
                        S.dop("s", out_hbm.rearrange("(m p) c -> p m c", p=128)
                              [:, 4 * j:4 * (j + 1), :],
                              x_s[:, 4 * j:4 * (j + 1), :],
                              reads=[("x_s", m2) for m2 in range(4 * j, 4 * j + 4)])
        S.barrier()

        def ffn_phase(tag):
            st2 = contextlib.ExitStack()
            with st2:
                sb2 = lambda n, shp, dt: st2.enter_context(
                    nc.sbuf_tensor(n + tag, shp, dt))
                ps2 = lambda n, shp, dt: st2.enter_context(
                    nc.psum_tensor(n + tag, shp, dt))
                xn2 = sb2("xn2", [128, 2, C], F32)
                xq_t = sb2("xq_t", [128, 2, C], BF16)
                wqdT = sb2("wqdT", [128, IC, C], BF16)
                wcol = sb2("wcol", [128, 2, CC, 512], BF16)
                sg = sb2("sg", [128, 2, 512], F32)
                hsrc = sb2("hsrc", [128, 2, 512], F32)
                hqq = sb2("hqq", [128, 2, 512], BF16)
                hqT = sb2("hqT", [128, IC, 256], BF16)
                ps_g = ps2("ps_g", [128, 2, 512], F32)
                ps_d = ps2("ps_d", [128, 2, 512], F32)
                wqdr = wqt_d.rearrange("k p c -> p k c")
                for j in range(4):
                    k0, k1 = 6 * j, min(6 * (j + 1), IC)
                    S.dop("s", wqdT[:, k0:k1, :], wqdr[:, k0:k1, :],
                          writes=["wqdT"])
                for m in range(TM):
                    ln_stats(m)
                ln_finish()
                for m0 in range(0, TM, 2):
                    jobs = []
                    for m in (m0, m0 + 1):
                        sl = m % 2
                        ln_norm(m, xn2[:, sl, :], ("xn2", sl))
                        jobs.append((xn2[:, sl, 0:512], xq_t[:, sl, 0:512], 512,
                                     ("xn2", sl), ("xq", sl, 0), 2 * sl))
                        jobs.append((xn2[:, sl, 512:1024],
                                     xq_t[:, sl, 512:1024], 512,
                                     ("xn2", sl), ("xq", sl, 1), 2 * sl + 1))
                    run_quant(jobs)
                    for m in (m0, m0 + 1):
                        sl = m % 2
                        for k in range(CC):
                            pe_t_bf(xq_t[:, sl, 128 * k:128 * (k + 1)],
                                    xnT[:, k, 128 * m:128 * (m + 1)], k % 2,
                                    reads=[("xq", sl, k // 4)],
                                    writes=[("xqT", k, m)])
                icols = [(512 * n, min(512, I - 512 * n))
                         for n in range((I + 511) // 512)]
                wgr = wqt_g.rearrange("k p i -> p k i")
                wur = wqt_u.rearrange("k p i -> p k i")
                def down_mg(mg):
                    for mm in range(2):
                        m = 2 * mg + mm
                        for n2 in range(2):
                            pk = ("ps_d", n2)
                            hts = [("hqT", k2, mm) for k2 in range(IC)]
                            for k in range(IC):
                                S.op("t", lambda e, k=k, mm=mm, n2=n2:
                                     e.matmul(ps_d[:, n2, :],
                                              hqT[:, k, 128 * mm:128 * (mm + 1)],
                                              wqdT[:, k, 512 * n2:512 * (n2 + 1)],
                                              start=(k == 0), stop=(k == IC - 1)),
                                     reads=hts + ["wqdT"], writes=[pk])
                            S.op("v", lambda e, m=m, n2=n2: e.tensor_add(
                                    x_s[:, m, 512 * n2:512 * (n2 + 1)],
                                    x_s[:, m, 512 * n2:512 * (n2 + 1)],
                                    ps_d[:, n2, :]),
                                 reads=[pk, ("x_s", m)], writes=[("x_s", m)])

                sched = []
                for mg in range(8):
                    for (c0, cw) in icols:
                        sched.append(("g", mg, c0, cw))
                        sched.append(("u", mg, c0, cw))

                def load_col(idx, slot):
                    mat, _, c0, cw = sched[idx]
                    srcr = wgr if mat == "g" else wur
                    S.dop("s", wcol[:, slot, :, 0:cw], srcr[:, :, c0:c0 + cw],
                          writes=[("wcol", slot)])

                load_col(0, 0)
                pending = []
                pend_down = []
                for idx, (mat, mg, c0, cw) in enumerate(sched):
                    slot = idx % 2
                    if idx + 1 < len(sched):
                        load_col(idx + 1, 1 - slot)
                    if mat == "g":
                        for mm in range(2):
                            m = 2 * mg + mm
                            for k in range(CC):
                                S.op("t", lambda e, k=k, m=m, mm=mm, cw=cw,
                                     ws=slot: e.matmul(
                                        ps_g[:, mm, 0:cw],
                                        xnT[:, k, 128 * m:128 * (m + 1)],
                                        wcol[:, ws, k, 0:cw],
                                        start=(k == 0), stop=(k == CC - 1)),
                                     reads=[("wcol", slot), ("xqT", k, m)],
                                     writes=[("ps_g", mm)])
                            S.op("a", lambda e, mm=mm, cw=cw: e.activation(
                                    sg[:, mm, 0:cw], ps_g[:, mm, 0:cw], AF.Silu),
                                 reads=[("ps_g", mm)], writes=[("sg", mm)])
                    else:
                        for mm in range(2):
                            m = 2 * mg + mm
                            for k in range(CC):
                                S.op("t", lambda e, k=k, m=m, mm=mm, cw=cw,
                                     ws=slot: e.matmul(
                                        ps_g[:, mm, 0:cw],
                                        xnT[:, k, 128 * m:128 * (m + 1)],
                                        wcol[:, ws, k, 0:cw],
                                        start=(k == 0), stop=(k == CC - 1)),
                                     reads=[("wcol", slot), ("xqT", k, m),
                                            ("sg", mm)],
                                     writes=[("ps_g", mm)])
                        hb = (c0 // 512) % 2
                        for mm in range(2):
                            S.op("v", lambda e, mm=mm, cw=cw:
                                 e.tensor_mul(hsrc[:, mm, 0:cw], sg[:, mm, 0:cw],
                                              ps_g[:, mm, 0:cw]),
                                 reads=[("sg", mm), ("ps_g", mm)],
                                 writes=[("hs", mm)])
                        def qjob(c0=c0, cw=cw, mg=mg, hb=hb):
                            qb0 = hb * 2
                            run_quant([
                                (hsrc[:, 0, 0:cw], hqq[:, 0, 0:cw], cw,
                                 ("hs", 0), ("hqq", 0), qb0),
                                (hsrc[:, 1, 0:cw], hqq[:, 1, 0:cw], cw,
                                 ("hs", 1), ("hqq", 1), qb0 + 1)])
                            for q2 in range(2):
                                for kk in range(cw // 128):
                                    ic = (c0 // 128) + kk
                                    pe_t_bf(hqq[:, q2, 128 * kk:128 * (kk + 1)],
                                            hqT[:, ic, 128 * q2:128 * (q2 + 1)],
                                            kk % 2, reads=[("hqq", q2)],
                                            writes=[("hqT", ic, q2)])
                        pending.append(qjob)
                        if not defer_q:
                            pending.pop(0)()
                    if pend_down and (not defer_down or
                                      (mat == "g" and c0 == 0)):
                        pend_down.pop(0)()
                    if pending and (mat == "g" or (c0, cw) == icols[-1]):
                        pending.pop(0)()
                    if mat == "u" and (c0, cw) == icols[-1]:
                        while pending:
                            pending.pop(0)()
                        if defer_down:
                            pend_down.append(lambda mg=mg: down_mg(mg))
                        else:
                            down_mg(mg)
                while pend_down:
                    pend_down.pop(0)()
            S.barrier()

        ffn_phase("F1")
        if dump_stage == 2:
            for j in range(4):
                S.dop("s", out_hbm.rearrange("(m p) c -> p m c", p=128)
                      [:, 4 * j:4 * (j + 1), :], x_s[:, 4 * j:4 * (j + 1), :])

        st3 = contextlib.ExitStack()
        with st3:
            sb3 = lambda n, shp, dt: st3.enter_context(nc.sbuf_tensor(n, shp, dt))
            ps3 = lambda n, shp, dt: st3.enter_context(nc.psum_tensor(n, shp, dt))
            xn3 = sb3("xn3", [128, 2, C], BF16)
            wqkv_raw = sb3("wqkv_raw", [128, 2, C], F32)
            qkv_bf = sb3("qkv_bf", [128, 2, C], BF16)
            wqkvT = sb3("wqkvT", [128, CC, 192], BF16)
            wout_raw = sb3("wout_raw", [128, CC, 64], F32)
            wout_bf = sb3("wout_bf", [128, CC, 64], BF16)
            woutT = sb3("woutT", [64, C], BF16)
            qkv_sb = sb3("qkv_sb", [128, TM, 192], F32)
            q_bf = sb3("q_bf", [128, 64], BF16)
            qT_sb = sb3("qT_sb", [64, T], BF16)
            vo16 = sb3("vo16", [128, TM, 128], BF16)
            qb = sb3("qb", [128, 4, T], BF16)
            Pb = sb3("Pb", [128, 4, T], BF16)
            nd = sb3("nd", [128, T], F32)
            ohm = qA.rearrange("p a b -> p (a b)")[0:64, 0:T]
            rec0 = qB.rearrange("p a b -> p (a b)")[0:64, 0:T]
            ps_n = ps3("ps_n", [128, T], F32)

            for m in range(TM):
                ln_stats(m)
            ln_finish()
            for m in range(TM):
                sl = m % 2
                ln_norm(m, xn3[:, sl, :], ("xn3", sl))
                for k in range(CC):
                    pe_t_bf(xn3[:, sl, 128 * k:128 * (k + 1)],
                            xnT[:, k, 128 * m:128 * (m + 1)], k % 2,
                            reads=[("xn3", sl)], writes=[("xnT", k, m)])
            S.dop("s", wqkv_raw[:, 0, :], wqkv_in[0:128, :], writes=["qkvr"])
            S.dop("s", wqkv_raw[0:64, 1, :], wqkv_in[128:192, :], writes=["qkvr"])
            S.op("g", lambda e: e.tensor_copy(qkv_bf[:, :, :], wqkv_raw[:, :, :]),
                 reads=["qkvr"], writes=["qkv_bf"])
            for k in range(CC):
                pe_t_bf(qkv_bf[:, 0, 128 * k:128 * (k + 1)], wqkvT[:, k, 0:128],
                        k % 2, reads=["qkv_bf"], writes=[("wqkvT", k)])
                pe_t_bf(qkv_bf[0:64, 1, 128 * k:128 * (k + 1)],
                        wqkvT[:, k, 128:192], k % 2, reads=["qkv_bf"],
                        writes=[("wqkvT", k)])
            S.dop("s", wout_raw[:, :, :],
                  wout_in.rearrange("(cc p) h -> p cc h", p=128),
                  writes=["wout_raw"])
            S.op("g", lambda e: e.tensor_copy(wout_bf[:, :, :], wout_raw[:, :, :]),
                 reads=["wout_raw"], writes=["wout_bf"])
            for k in range(CC):
                pe_t_bf(wout_bf[:, k, 0:64], woutT[0:64, 128 * k:128 * (k + 1)],
                        k % 2, reads=["wout_bf"], writes=["woutT"])
            for m in range(TM):
                pkk = "ps_n"
                for k in range(CC):
                    S.op("t", lambda e, k=k, m=m: e.matmul(
                            ps_n[:, 0:192], xnT[:, k, 128 * m:128 * (m + 1)],
                            wqkvT[:, k, :], start=(k == 0), stop=(k == CC - 1)),
                         reads=[("xnT", k, m)] + [("wqkvT", kk) for kk in
                                                  range(CC)],
                         writes=[pkk])
                S.op("a", lambda e, m=m: e.activation(qkv_sb[:, m, 0:64],
                                                      ps_n[:, 0:64], AF.Copy),
                     reads=[pkk], writes=[("qkv", m)])
                S.op("a", lambda e, m=m: e.activation(qkv_sb[:, m, 64:128],
                                                      ps_n[:, 64:128], AF.Copy,
                                                      scale=ccol[:, 3:4]),
                     reads=[pkk, "ccol"], writes=[("qkv", m)])
                S.op("a", lambda e, m=m: e.activation(qkv_sb[:, m, 128:192],
                                                      ps_n[:, 128:192], AF.Copy),
                     reads=[pkk], writes=[("qkv", m)])
                S.op("v", lambda e, m=m: e.tensor_copy(q_bf[:, :],
                                                       qkv_sb[:, m, 0:64]),
                     reads=[("qkv", m)], writes=["q_bf"])
                pe_t_bf(q_bf[:, :], qT_sb[0:64, 128 * m:128 * (m + 1)], m % 2,
                        reads=["q_bf"], writes=["qT"])
            S.dop("s", qT_dram, qT_sb[0:64, :], reads=["qT"], writes=["qTd"])
            S.op("g", lambda e: e.memset(vo16[:, :, :], 0.0), writes=["vo"])
            def load_qb(h):
                S.dop("s", qb[:, h % 4, :],
                      qT_dram[h:h + 1, :].partition_broadcast(128),
                      reads=["qTd"], writes=[("qb", h % 4)])

            for h in range(3):
                load_qb(h)
            for h in range(NH):
                if h + 3 < NH:
                    load_qb(h + 3)
                if h > 0:
                    S.op("g", lambda e, h=h: e.memset(vo16[:, :, h - 1:h], 0.0),
                         writes=["vo"])
                    S.op("g", lambda e, h=h: e.memset(
                            vo16[:, :, 63 + h:64 + h], 0.0), writes=["vo"])
                S.op("v", lambda e, h=h: e.tensor_copy(
                        vo16[:, :, h:h + 1], qkv_sb[:, :, 128 + h:129 + h]),
                     reads=[("qkv", m2) for m2 in range(TM)], writes=["vo"])
                S.op("g", lambda e, h=h: e.memset(vo16[:, :, 64 + h:65 + h], 1.0),
                     writes=["vo"])
                qsl = h % 4
                for b in range(TM):
                    ext = T - 128 * b
                    psl = (h * TM + b) % 4
                    S.op("a", lambda e, h=h, b=b, psl=psl, qsl=qsl, ext=ext:
                         e.activation(Pb[:, psl, 0:ext], qb[:, qsl, 128 * b:T],
                                      AF.Exp,
                                      scale=qkv_sb[:, b, 64 + h:65 + h]),
                         reads=[("qb", qsl), ("qkv", b)], writes=[("Pb", psl)])
                    S.op("v", lambda e, psl=psl: e.tensor_mul(
                            Pb[:, psl, 0:128], Pb[:, psl, 0:128], tri_b[:, :]),
                         reads=[("Pb", psl), "tri"], writes=[("Pb", psl)])
                    i0 = 128 * b
                    while i0 < T:
                        i1 = min((i0 // 512 + 1) * 512, T)
                        S.op("t", lambda e, b=b, psl=psl, i0=i0, i1=i1, h=h:
                             e.matmul(ps_n[:, i0:i1], vo16[:, b, :],
                                      Pb[:, psl, i0 - 128 * b:i1 - 128 * b],
                                      start=(h == 0 and b == 0),
                                      stop=(h == NH - 1)),
                             reads=["vo", ("Pb", psl)], writes=["ps_n"])
                        i0 = i1
            S.op("v", lambda e: e.tensor_copy(nd[:, :], ps_n[:, :]),
                 reads=["ps_n"], writes=["nd"])
            def _recip(e):
                with nc.allow_low_precision(
                        reason="softmax denom reciprocal stored bf16; "
                               "~0.4% on a small additive term"):
                    return e.reciprocal(rec0, nd[64:128, :])
            S.op("v", _recip, reads=["nd"], writes=["rec0"])
            S.op("v", lambda e: e.tensor_mul(ohm, nd[0:64, :], rec0),
                 reads=["nd", "rec0"], writes=["ohm"])
            for m in range(TM):
                for n2 in range(2):
                    S.op("t", lambda e, m=m, n2=n2: e.matmul(
                            ps_n[:, 512 * n2:512 * (n2 + 1)],
                            ohm[:, 128 * m:128 * (m + 1)],
                            woutT[0:64, 512 * n2:512 * (n2 + 1)],
                            start=True, stop=True),
                         reads=["ohm", "woutT"], writes=["ps_n"])
                    S.op("v", lambda e, m=m, n2=n2: e.tensor_add(
                            x_s[:, m, 512 * n2:512 * (n2 + 1)],
                            x_s[:, m, 512 * n2:512 * (n2 + 1)],
                            ps_n[:, 512 * n2:512 * (n2 + 1)]),
                         reads=["ps_n", ("x_s", m)], writes=[("x_s", m)])
        S.barrier()
        if dump_stage == 3:
            for j in range(4):
                S.dop("s", out_hbm.rearrange("(m p) c -> p m c", p=128)
                      [:, 4 * j:4 * (j + 1), :], x_s[:, 4 * j:4 * (j + 1), :])

        ffn_phase("F2")

        if dump_stage is None:
            outr = out_hbm.rearrange("(m p) c -> p m c", p=128)
            for j in range(4):
                S.dop("s", outr[:, 4 * j:4 * (j + 1), :],
                      x_s[:, 4 * j:4 * (j + 1), :],
                      reads=[("x_s", m) for m in range(4 * j, 4 * j + 4)])
        S.eng["s"].wait_ge(S.sem["ds"], S.cnt["ds"])
        S.eng["g"].wait_ge(S.sem["dg"], S.cnt["dg"])
    return nc


_NC_CACHE = {}


def kernel(L0_attn_a_x, L0_attn_a_Wi, L0_attn_a_bi, L0_attn_a_Wr, L0_attn_a_br,
           L0_attn_a_cap_lambda, L0_attn_b_w_qkv, L0_attn_b_w_out,
           L0_ffn_w_gate, L0_ffn_w_up, L0_ffn_w_down):
    cc = np.ascontiguousarray
    x = cc(np.asarray(L0_attn_a_x, dtype=np.float32))
    wi = cc(np.asarray(L0_attn_a_Wi, dtype=np.float32))
    bi = cc(np.asarray(L0_attn_a_bi, dtype=np.float32)).reshape(1, C)
    wr = cc(np.asarray(L0_attn_a_Wr, dtype=np.float32))
    br = cc(np.asarray(L0_attn_a_br, dtype=np.float32)).reshape(1, C)
    lam = cc(np.asarray(L0_attn_a_cap_lambda, dtype=np.float32)).reshape(1, 1)
    wqkv = cc(np.asarray(L0_attn_b_w_qkv, dtype=np.float32))
    wout = cc(np.asarray(L0_attn_b_w_out, dtype=np.float32))
    wg = cc(np.asarray(L0_ffn_w_gate, dtype=np.float32))
    wu = cc(np.asarray(L0_ffn_w_up, dtype=np.float32))
    wd = cc(np.asarray(L0_ffn_w_down, dtype=np.float32))

    if "nc" not in _NC_CACHE:
        _NC_CACHE["nc"] = build_nc()
    nc = _NC_CACHE["nc"]
    in_maps = []
    for b in range(NCORES):
        in_maps.append({"x": cc(x[b]), "wi": wi, "bi": bi, "wr": wr, "br": br,
                        "lam": lam, "wqkv": wqkv, "wout": wout, "wg": wg,
                        "wu": wu, "wd": wd})
    res = run_bass_kernel_spmd(nc, in_maps, core_ids=list(range(NCORES)))
    return np.stack([r["out"] for r in res.results], axis=0)
